# version 28
# baseline (speedup 1.0000x reference)
import os

import ml_dtypes
import numpy as np
import concourse.bass as bass
import concourse.mybir as mybir
from concourse.bass_utils import run_bass_kernel_spmd

F32 = mybir.dt.float32
BF16 = mybir.dt.bfloat16
AF = mybir.ActivationFunctionType
ALU = mybir.AluOpType
AX = mybir.AxisListType

ENGINES = ("sync", "tensor", "vector", "scalar", "gpsimd")
ENG_SEM = {"tensor": "T", "vector": "V", "scalar": "S", "gpsimd": "G"}

D = 1024
H = 16
HD = 64
FF = 4096
NDC = D // 128
NHC = FF // 128
NHB = FF // 512


class Prog:
    def __init__(self, nc):
        self.nc = nc
        self.ops = {e: [] for e in ENGINES}
        self.sems = {}
        self.counts = {}
        self.qstate = {}
        self.hwm = {e: {} for e in ENGINES}
        self.strict = os.environ.get("BASS_SIM_FP64") == "1"

    def do(self, eng, fn, waits=(), inc=None, n=1, self_wait=False,
           hard=False):
        need = {}
        for w in waits:
            if w is None:
                continue
            s, v = w
            if v > 0 and v > need.get(s, 0):
                need[s] = v
        wl = []
        if self.strict and fn is not None and eng in ENG_SEM:
            s = ENG_SEM[eng]
            prev = self.counts.get(s, 0)
            if inc is None:
                inc = s
                if s not in self.sems:
                    self.sems[s] = self.nc.alloc_semaphore(s)
                    self.counts[s] = 0
            if prev > 0:
                wl.append((s, prev))
                if prev > self.hwm[eng].get(s, 0):
                    self.hwm[eng][s] = prev
        for s, v in need.items():
            if not hard and v <= self.hwm[eng].get(s, 0):
                continue
            if v > self.hwm[eng].get(s, 0):
                self.hwm[eng][s] = v
            wl.append((s, v))
        tok = None
        if inc is not None:
            if inc not in self.sems:
                self.sems[inc] = self.nc.alloc_semaphore(inc)
                self.counts[inc] = 0
            self.counts[inc] += n
            tok = (inc, self.counts[inc])
        self.ops[eng].append((fn, wl, inc, n))
        if self_wait and tok is not None:
            s, v = tok
            if v > self.hwm[eng].get(s, 0):
                self.hwm[eng][s] = v
                self.ops[eng].append((None, [(s, v)], None, 1))
        return tok

    def wait_only(self, eng, waits):
        self.do(eng, None, waits)

    def dma(self, eng, fn, waits=(), queue="dL", depth=4):
        st = self.qstate.setdefault((eng, queue), {"i": 0, "last": {}})
        i = st["i"]
        st["i"] += 1
        sem = "%s%d" % (queue, i % depth)
        w = list(waits)
        prev = st["last"].get(i % depth)
        if prev is not None:
            w.append(prev)
        tok = self.do(eng, fn, w, inc=sem, n=16)
        st["last"][i % depth] = tok
        return tok

    def replay(self, eng_name, e):
        for fn, wl, inc, n in self.ops[eng_name]:
            for s, v in wl:
                e.wait_ge(self.sems[s], v)
            if fn is None:
                continue
            inst = fn(e)
            if inc is not None:
                inst.then_inc(self.sems[inc], n)


def sap(t, p0, nparts, fdims, c0=0):
    w = t.shape[-1]
    return bass.AP(t, p0 * w + c0, [[w, nparts]] + [list(d) for d in fdims])


def dap(t, off, dims):
    return bass.AP(t, off, [list(d) for d in dims])


class Bank:
    __slots__ = ("prod", "cons")

    def __init__(self):
        self.prod = None
        self.cons = []


def build(n_tt=16, debug=False):
    T = n_tt * 128
    NQG = n_tt // 4
    nc = bass.Bass("TRN2", target_bir_lowering=False)

    x_h = nc.dram_tensor("x", [T, D], F32, kind="ExternalInput")
    xtt_h = nc.dram_tensor("xtt", [T, D], F32, kind="ExternalInput")
    wq_h = nc.dram_tensor("wq", [D, D], F32, kind="ExternalInput")
    wk_h = nc.dram_tensor("wk", [D, D], F32, kind="ExternalInput")
    wv_h = nc.dram_tensor("wv", [D, D], F32, kind="ExternalInput")
    wo_h = nc.dram_tensor("wo", [D, D], F32, kind="ExternalInput")
    wob_h = nc.dram_tensor("wob", [D], F32, kind="ExternalInput")
    wfcT_h = nc.dram_tensor("wfcT", [D, FF], F32, kind="ExternalInput")
    lnw_h = nc.dram_tensor("lnw", [FF], F32, kind="ExternalInput")
    lnb_h = nc.dram_tensor("lnb", [FF], F32, kind="ExternalInput")
    wprojT_h = nc.dram_tensor("wprojT", [FF, D], F32, kind="ExternalInput")
    ctab_h = nc.dram_tensor("ctab", [128, n_tt * 512], F32, kind="ExternalInput")
    stab_h = nc.dram_tensor("stab", [128, n_tt * 512], F32, kind="ExternalInput")
    cpkb_h = nc.dram_tensor("cpkb", [128, 384], BF16, kind="ExternalInput")
    cpkf_h = nc.dram_tensor("cpkf", [128, 264], F32, kind="ExternalInput")
    out_h = nc.dram_tensor("out", [T, D], F32, kind="ExternalOutput")
    iokind = "ExternalOutput" if debug else "Internal"
    vhbm = nc.dram_tensor("v_hbm", [T, D], BF16, kind=iokind)
    x1hbm = nc.dram_tensor("x1_hbm", [T, D], BF16, kind=iokind)
    if debug:
        dbg_qt = nc.dram_tensor("dbg_qt", [128, NDC * T], BF16, kind="ExternalOutput")
        dbg_kt = nc.dram_tensor("dbg_kt", [128, NDC * T], BF16, kind="ExternalOutput")
        dbg_att = nc.dram_tensor("dbg_att", [128, NDC * T], BF16, kind="ExternalOutput")
        dbg_thr = nc.dram_tensor("dbg_thr", [128, 4], F32, kind="ExternalOutput")
        dbg_wfc2 = nc.dram_tensor("dbg_wfc2", [128, 8 * FF], BF16, kind="ExternalOutput")
        dbg_wps = nc.dram_tensor("dbg_wps", [128, 32 * D], BF16, kind="ExternalOutput")
        dbg_cbf = nc.dram_tensor("dbg_cbf", [1, 1024], BF16, kind="ExternalOutput")
        dbg_rb = nc.dram_tensor("dbg_rb", [128, 512], F32, kind="ExternalOutput")
        dbg_sm = nc.dram_tensor("dbg_sm", [128, 64], F32, kind="ExternalOutput")
        dbg_r1 = nc.dram_tensor("dbg_r1", [128, 2048], F32, kind="ExternalOutput")

    AT = nc.alloc_sbuf_tensor_at
    ident_bf = AT("ident_bf", [128, 128], BF16, offset=17024)
    ident_f32 = AT("ident_f32", [128, 128], F32, offset=17280)
    mask_bf = AT("mask_bf", [128, 128], BF16, offset=17792)
    ones_bf = AT("ones_bf", [128, 128], BF16, offset=18048)
    ones_f32 = AT("ones_f32", [128, 128], F32, offset=18304)
    consts = AT("consts", [128, 8], F32, offset=18816)
    sm = AT("sm", [128, 64], F32, offset=18848)
    gp = AT("gp", [128, 72], F32, offset=19104)
    thr = AT("thr", [128, 4], F32, offset=19392)
    lnw_raw = AT("lnw_raw", [32, 128], F32, offset=19424)
    lnb_raw = AT("lnb_raw", [32, 128], F32, offset=19936)
    lnw_sb = AT("lnw_sb", [128, 32], F32, offset=20448)
    lnb_half_bf = AT("lnb_half_bf", [128, 32], BF16, offset=20576)
    gs = AT("gs", [128, 16], F32, offset=20640)
    wob_bf = AT("wob_bf", [1, 1024], BF16, offset=20704)
    wob_f = AT("wob_f", [1, 1024], F32, offset=217728)
    c_bf = AT("c_bf", [1, 1024], BF16, offset=22752)
    QT = AT("QT", [128, NDC * T], BF16, offset=25216)
    wfc2_lo = AT("wfc2_lo", [128, 4 * FF], BF16, offset=25216)
    KT = AT("KT", [128, NDC * T], BF16, offset=57984)
    wfc2_hi = AT("wfc2_hi", [128, 4 * FF], BF16, offset=57984)
    attT = AT("attT", [128, NDC * T], BF16, offset=90752)
    wps_lo = AT("wps_lo", [128, 16 * D], BF16, offset=90752)
    wslot = AT("wslot", [128, 3 * NDC * D], BF16, offset=123520)
    wps_hi = AT("wps_hi", [128, 16 * D], BF16, offset=123520)
    r1buf = AT("r1buf", [128, 2 * D], F32, offset=139904)
    tabC2 = AT("tabC2", [128, 512], F32, offset=92800)
    tabS2 = AT("tabS2", [128, 512], F32, offset=94848)
    sg0 = AT("sg0", [128, FF], BF16, offset=156288)
    sg1 = AT("sg1", [128, FF], BF16, offset=164480)
    h2t = AT("h2t", [128, FF], BF16, offset=156288)
    h2t2 = AT("h2t2", [128, FF], BF16, offset=164480)
    stage_x = AT("stage_x", [128, 2 * D], F32, offset=172672)
    r2buf = AT("r2buf", [128, D], F32, offset=172672)
    outbuf = AT("outbuf", [128, D], F32, offset=176768)
    tabC = AT("tabC", [128, 512], F32, offset=180864)
    tabS = AT("tabS", [128, 512], F32, offset=182912)
    x1tile = AT("x1tile", [128, 4 * D], BF16, offset=217728)
    x1T = AT("x1T", [128, 2 * D], BF16, offset=184960)
    tmps = AT("tmps", [128, 4 * D], BF16, offset=189056)
    h2T_t = AT("h2T", [128, FF], BF16, offset=189056)
    PT = AT("PT", [128, 4 * 512], BF16, offset=197248)
    PT2 = AT("PT2", [128, 4 * 512], BF16, offset=180864)
    gbuf = AT("gbuf", [128, FF], BF16, offset=197248)
    vbuf = AT("vbuf", [128, 2 * 2048], BF16, offset=205440)
    rbA = AT("rbA", [128, 512], F32, offset=213632)
    rbB = AT("rbB", [128, 512], F32, offset=215680)
    wstage = AT("wstage", [128, 2 * D], F32, offset=217728)
    rscr = AT("rscr", [128, 512], F32, offset=213632)
    vstage = AT("vstage", [128, D], BF16, offset=225920)
    x1stage0 = AT("x1stage0", [128, D], BF16, offset=225920)
    x1stage1 = AT("x1stage1", [128, D], BF16, offset=215680)
    sqscr = AT("sqscr", [128, FF], BF16, offset=189056)
    gsta = AT("gsta", [128, 1024], F32, offset=184960)
    gstb = AT("gstb", [128, 1024], F32, offset=201344)
    D0 = nc.alloc_psum_tensor("D0", [128, 1024], F32)
    D1 = nc.alloc_psum_tensor("D1", [128, 1024], F32)
    S0 = nc.alloc_psum_tensor("S0", [128, 512], F32)
    S1 = nc.alloc_psum_tensor("S1", [128, 512], F32)
    pb = [nc.alloc_psum_tensor("pb%d" % i, [128, 1024], BF16) for i in range(2)]

    p = Prog(nc)
    bank = {id(t): Bank() for t in [D0, D1, S0, S1] + pb}

    def B(t):
        return bank[id(t)]

    def mm(out_ap, lhsT_ap, rhs_ap, start, stop, waits=(), inc="T"):
        return p.do("tensor",
                    lambda e, o=out_ap, l=lhsT_ap, r=rhs_ap, s=start, st=stop:
                    e.matmul(o, l, r, start=s, stop=st,
                             skip_group_check=True),
                    waits, inc)

    def produce(t, waits):
        return list(waits) + B(t).cons

    def set_prod(t, tok):
        B(t).prod = tok
        B(t).cons = []

    def consume(t, waits=()):
        return list(waits) + ([B(t).prod] if B(t).prod else [])

    def add_cons(t, tok):
        B(t).cons.append(tok)

    tcb1 = p.dma("sync", lambda e: e.dma_start(
        out=ident_bf[:, :], in_=cpkb_h[:, 0:128]), queue="dL")
    tcb2 = p.dma("sync", lambda e: e.dma_start(
        out=mask_bf[:, :], in_=cpkb_h[:, 128:256]), queue="dL")
    tcb3 = p.dma("sync", lambda e: e.dma_start(
        out=ones_bf[:, :], in_=cpkb_h[:, 256:384]), queue="dL")
    tcf1 = p.dma("sync", lambda e: e.dma_start(
        out=ident_f32[:, :], in_=cpkf_h[:, 0:128]), queue="dL")
    tcf2 = p.dma("sync", lambda e: e.dma_start(
        out=ones_f32[:, :], in_=cpkf_h[:, 128:256]), queue="dL")
    tok_consts = p.dma("sync", lambda e: e.dma_start(
        out=consts[:, :], in_=cpkf_h[:, 256:264]), queue="dL")
    CONSTS = [tcb1, tcb2, tcb3, tcf1, tcf2, tok_consts]

    tok = p.dma("sync", lambda e: e.dma_start(
        out=wob_f[0:1, :], in_=dap(wob_h, 0, [[1024, 1], [1, 1024]])),
        queue="dL")
    tok_wob = p.do("vector", lambda e: e.tensor_copy(
        out=wob_bf[0:1, :], in_=wob_f[0:1, :]), [tok], inc="V")
    tok_lnw = p.dma("sync", lambda e: e.dma_start(
        out=lnw_raw[:, :], in_=dap(lnw_h, 0, [[128, 32], [1, 128]])),
        queue="dL")
    tok_lnb = p.dma("sync", lambda e: e.dma_start(
        out=lnb_raw[:, :], in_=dap(lnb_h, 0, [[128, 32], [1, 128]])),
        queue="dL")
    tk = mm(sap(S0, 0, 128, [[1, 32]]), lnw_raw[0:32, 0:128],
            ident_f32[0:32, 0:32], True, True,
            produce(S0, [tok_lnw] + CONSTS))
    set_prod(S0, tk)
    tk2 = p.do("vector", lambda e: e.tensor_scalar(
        out=lnw_sb[:, :], in0=sap(S0, 0, 128, [[1, 32]]), scalar1=1.0,
        scalar2=None, op0=ALU.mult), consume(S0), inc="V")
    add_cons(S0, tk2)
    tk = mm(sap(S1, 0, 128, [[1, 32]]), lnb_raw[0:32, 0:128],
            ident_f32[0:32, 0:32], True, True,
            produce(S1, [tok_lnb] + CONSTS))
    set_prod(S1, tk)
    tk2 = p.do("vector", lambda e: e.tensor_scalar(
        out=lnb_half_bf[:, :], in0=sap(S1, 0, 128, [[1, 32]]), scalar1=1.0,
        scalar2=None, op0=ALU.mult), consume(S1), inc="V")
    add_cons(S1, tk2)

    wst_cons = [tok_wob, None]
    w_cast_tok = {}
    for wi, wh in enumerate((wq_h, wk_h, wv_h)):
        toks = []
        for dc in range(NDC):
            sl = (wi * NDC + dc) % 2
            tok = p.dma("sync", lambda e, wh=wh, dc=dc, sl=sl: e.dma_start(
                out=wstage[:, sl * D:(sl + 1) * D],
                in_=wh[dc * 128:(dc + 1) * 128, :]),
                [wst_cons[sl]], queue="dL")
            ct = p.do("vector", lambda e, wi=wi, dc=dc, sl=sl: e.tensor_copy(
                out=wslot[:, (wi * NDC + dc) * D:(wi * NDC + dc + 1) * D],
                in_=wstage[:, sl * D:(sl + 1) * D]), [tok], inc="V")
            wst_cons[sl] = ct
            toks.append(ct)
        w_cast_tok[wi] = toks

    xstage_cons = [None, None]
    tab_cons = [None, None]
    QT_toks, KT_toks = [], []
    vhbm_toks = []
    pb_rot = [0]
    lastQmm = [None]
    prev_qkv_mm = [None]

    def emit_transpose8(tsrc, col0, waits):
        t = pb[pb_rot[0] % 2]
        pb_rot[0] += 1
        w = produce(t, list(waits) + CONSTS)
        tk = None
        for j in range(8):
            tk = p.do("tensor",
                      lambda e, t=t, tsrc=tsrc, col0=col0, j=j: e.transpose(
                          sap(t, 0, 128, [[1, 128]], j * 128),
                          tsrc[:, col0 + j * 128: col0 + (j + 1) * 128],
                          ident_bf[:, :]),
                      w if j == 0 else (), inc="T")
        set_prod(t, tk)
        return t

    cur_tab = [tabC, tabS]

    def rope6(ps_half, half, dst, dst_tok_list, waits):
        pt_, cb_ = ps_half
        ev = sap(pt_, 0, 128, [[64, 8], [1, 32]], cb_ + 0)
        od = sap(pt_, 0, 128, [[64, 8], [1, 32]], cb_ + 32)
        tC = sap(cur_tab[0], 0, 128, [[32, 8], [1, 32]], half * 256)
        tS = sap(cur_tab[1], 0, 128, [[32, 8], [1, 32]], half * 256)
        sAap = sap(rscr, 0, 128, [[32, 8], [1, 32]], 0)
        sBap = sap(rscr, 0, 128, [[32, 8], [1, 32]], 256)
        dev = sap(tmps, 0, 128, [[64, 8], [1, 32]], dst + half * 512)
        dod = sap(tmps, 0, 128, [[64, 8], [1, 32]], dst + half * 512 + 32)
        p.do("vector", lambda e: e.tensor_tensor(out=sAap, in0=ev, in1=tC,
                                                 op=ALU.mult), waits, inc="V")
        p.do("vector", lambda e: e.tensor_tensor(out=sBap, in0=od, in1=tS,
                                                 op=ALU.mult), (), inc="V")
        p.do("vector", lambda e: e.tensor_tensor(out=dev, in0=sAap, in1=sBap,
                                                 op=ALU.subtract), (), inc="V")
        p.do("vector", lambda e: e.tensor_tensor(out=sAap, in0=ev, in1=tS,
                                                 op=ALU.mult), (), inc="V")
        p.do("vector", lambda e: e.tensor_tensor(out=sBap, in0=od, in1=tC,
                                                 op=ALU.mult), (), inc="V")
        tk = p.do("vector", lambda e: e.tensor_tensor(out=dod, in0=sAap, in1=sBap,
                                                      op=ALU.add), (), inc="V")
        dst_tok_list.append(tk)
        return tk

    XBF, XT, QROT, KROT = 0, D, 2 * D, 3 * D

    NW = float(FF * D)
    gslot = [gsta, gstb]
    gcons = [None, None]
    gq = [0]
    gred_toks = []

    def emit_gamma_piece(load_ap, col):
        i = gq[0] % 2
        gq[0] += 1
        g = gslot[i]
        tok = p.dma("scalar", lambda e, g=g, load_ap=load_ap: e.dma_start(
            out=g[:, :], in_=load_ap), [gcons[i]], queue="dW", depth=2)
        tk = p.do("scalar", lambda e, g=g, col=col: e.activation(
            out=attT[:, 0:D], in_=g[:, :], func=AF.Abs,
            accum_out=gp[:, col:col + 1]), [tok], inc="S")
        gcons[i] = tk
        gred_toks.append(tk)

    gamma_pieces = []
    for dc in range(NDC):
        for q in range(4):
            gamma_pieces.append((dap(wfcT_h, dc * 128 * FF + q * 1024,
                                     [[FF, 128], [1, 1024]]), dc * 4 + q))
    for hcq in range(NHC):
        gamma_pieces.append((wprojT_h[hcq * 128:(hcq + 1) * 128, :], 32 + hcq))
    gper = -(-len(gamma_pieces) // n_tt)

    for tt in range(n_tt):
        sl = tt % 2
        tok_x = p.dma("sync", lambda e, tt=tt, sl=sl: e.dma_start(
            out=stage_x[:, sl * D:(sl + 1) * D],
            in_=xtt_h[tt * 128:(tt + 1) * 128, :]), [xstage_cons[sl]],
            queue="dL")
        tbC = tabC if tt % 2 == 0 else tabC2
        tbS = tabS if tt % 2 == 0 else tabS2
        tok_tc = p.dma("sync", lambda e, tt=tt, tbC=tbC: e.dma_start(
            out=tbC[:, :], in_=ctab_h[:, tt * 512:(tt + 1) * 512]),
            [tab_cons[tt % 2]], queue="dL")
        tok_ts = p.dma("sync", lambda e, tt=tt, tbS=tbS: e.dma_start(
            out=tbS[:, :], in_=stab_h[:, tt * 512:(tt + 1) * 512]),
            [tab_cons[tt % 2]], queue="dL")
        tok_xT = p.do("scalar", lambda e, sl=sl: e.copy(
            out=tmps[:, XT:XT + D], in_=stage_x[:, sl * D:(sl + 1) * D]),
            [tok_x, prev_qkv_mm[0]], inc="S")
        xstage_cons[sl] = tok_xT
        ps_pairs = {0: ((D0, 0), (D0, 512)), 1: ((D1, 0), (D1, 512)),
                    2: ((S0, 0), (S1, 0))}
        for wi in range(3):
            (tA, cA), (tB, cB) = ps_pairs[wi]
            wA = produce(tA, [tok_xT] + w_cast_tok[wi])
            wB = produce(tB, [tok_xT] + w_cast_tok[wi]) if tB is not tA else []
            tkA = tkB = None
            for dc in range(NDC):
                tkA = mm(tA[:, cA:cA + 512],
                         tmps[:, XT + dc * 128:XT + (dc + 1) * 128],
                         wslot[:, (wi * NDC + dc) * D:(wi * NDC + dc) * D + 512],
                         dc == 0, dc == NDC - 1, wA if dc == 0 else ())
                tkB = mm(tB[:, cB:cB + 512],
                         tmps[:, XT + dc * 128:XT + (dc + 1) * 128],
                         wslot[:, (wi * NDC + dc) * D + 512:(wi * NDC + dc + 1) * D],
                         dc == 0, dc == NDC - 1, wB if dc == 0 else ())
            set_prod(tA, tkA)
            set_prod(tB, tkB)
            prev_qkv_mm[0] = tkB
            if wi == 0:
                lastQmm[0] = tkB
        rope_toks = []
        cur_tab[0], cur_tab[1] = tbC, tbS
        for wi, dst in ((0, QROT), (1, KROT)):
            hA, hB = ps_pairs[wi]
            t1 = rope6(hA, 0, dst, rope_toks, consume(hA[0], [tok_tc, tok_ts]))
            add_cons(hA[0], t1)
            t2 = rope6(hB, 1, dst, rope_toks, consume(hB[0]))
            add_cons(hB[0], t2)
        tab_cons[tt % 2] = rope_toks[-1]
        for dst, res, toks in ((QROT, QT, QT_toks), (KROT, KT, KT_toks)):
            tb = emit_transpose8(tmps, dst, [rope_toks[-1]])
            tk = p.do("vector", lambda e, tb=tb, res=res, tt=tt: e.tensor_copy(
                out=sap(res, 0, 128, [[T, 8], [1, 128]], tt * 128),
                in_=sap(tb, 0, 128, [[1, 1024]])), consume(tb), inc="V")
            add_cons(tb, tk)
            toks.append(tk)
        tkv = p.do("vector", lambda e: e.tensor_copy(
            out=vstage[:, 0:512], in_=S0[:, :]),
            consume(S0, [vhbm_toks[-1] if vhbm_toks else None]), inc="V")
        add_cons(S0, tkv)
        tkv2 = p.do("vector", lambda e: e.tensor_copy(
            out=vstage[:, 512:1024], in_=S1[:, :]), consume(S1), inc="V")
        add_cons(S1, tkv2)
        tkst = p.dma("gpsimd", lambda e, tt=tt: e.dma_start(
            out=vhbm[tt * 128:(tt + 1) * 128, :], in_=vstage[:, :]),
            [tkv, tkv2], queue="dG", depth=2)
        vhbm_toks.append(tkst)
        for piece in gamma_pieces[tt * gper:(tt + 1) * gper]:
            emit_gamma_piece(*piece)

    tok_all_vhbm = vhbm_toks[-1]
    t_lastQKV = prev_qkv_mm[0]
    t_lastQmm = lastQmm[0]
    if debug:
        p.dma("gpsimd", lambda e: e.dma_start(out=dbg_qt[:, :], in_=QT[:, :]),
             QT_toks, queue="dG", depth=2)
        p.dma("gpsimd", lambda e: e.dma_start(out=dbg_kt[:, :], in_=KT[:, :]),
             KT_toks, queue="dG", depth=2)

    wo_toks = []
    for dc in range(NDC):
        sl = dc % 2
        tok = p.dma("sync", lambda e, dc=dc, sl=sl: e.dma_start(
            out=wstage[:, sl * D:(sl + 1) * D],
            in_=wo_h[dc * 128:(dc + 1) * 128, :]),
            [wst_cons[sl], t_lastQmm], queue="dL")
        ct = p.do("vector", lambda e, dc=dc, sl=sl: e.tensor_copy(
            out=wslot[:, dc * D:(dc + 1) * D],
            in_=wstage[:, sl * D:(sl + 1) * D]), [tok, t_lastQmm], inc="V")
        wst_cons[sl] = ct
        wo_toks.append(ct)

    t1 = p.do("vector", lambda e: e.reduce_sum(
        gp[:, 64:65], gp[:, 0:32], axis=AX.X), [gred_toks[31]], inc="V")
    t2 = p.do("vector", lambda e: e.reduce_sum(
        gp[:, 65:66], gp[:, 32:64], axis=AX.X), [gred_toks[-1]], inc="V")
    tk = mm(sap(S0, 0, 2, [[1, 128]]), gp[:, 64:66], ident_f32[:, :],
            True, True, produce(S0, [t1, t2]))
    set_prod(S0, tk)
    tg = p.do("vector", lambda e: e.tensor_reduce(
        gp[0:2, 66:67], sap(S0, 0, 2, [[1, 128]]), axis=AX.X, op=ALU.add),
        consume(S0), inc="V")
    add_cons(S0, tg)
    tk2 = mm(sap(S1, 0, 1, [[1, 2]]), gp[0:2, 66:67], ident_f32[0:2, 0:2],
             True, True, produce(S1, [tg]))
    set_prod(S1, tk2)
    tg2 = p.do("vector", lambda e: e.tensor_scalar(
        out=gp[0:1, 68:70], in0=sap(S1, 0, 1, [[1, 2]]), scalar1=0.5 / NW,
        scalar2=5e-7, op0=ALU.mult, op1=ALU.add), consume(S1), inc="V")
    add_cons(S1, tg2)
    tk = None
    for j in range(2):
        tk = mm(sap(S1, 0, 128, [[1, 1]], j), ones_f32[0:1, 0:128],
                gp[0:1, 68 + j:69 + j], j == 0, j == 1,
                produce(S1, [tg2]) if j == 0 else [tg2])
    set_prod(S1, tk)
    tb_ = p.do("vector", lambda e: e.tensor_copy(
        out=thr[:, 0:2], in_=sap(S1, 0, 128, [[1, 2]])),
        consume(S1), inc="V")
    add_cons(S1, tb_)
    tbn = p.do("vector", lambda e: e.tensor_scalar(
        out=thr[:, 2:4], in0=sap(S1, 0, 128, [[1, 2]]), scalar1=-1.0,
        scalar2=None, op0=ALU.mult), consume(S1), inc="V")
    add_cons(S1, tbn)
    TH = [tb_, tbn]

    wfc2_toks = []

    def emit_wfcq(dc, q, gate):
        i = gq[0] % 2
        gq[0] += 1
        g = gslot[i]
        tok = p.dma("scalar", lambda e, g=g, dc=dc, q=q: e.dma_start(
            out=g[:, :], in_=dap(wfcT_h, dc * 128 * FF + q * 1024,
                                 [[FF, 128], [1, 1024]])),
            [gcons[i]], queue="dW", depth=2)
        tm0 = p.do("vector", lambda e, g=g: e.tensor_scalar(
            out=sg0[:, 0:D], in0=g[:, :], scalar1=thr[:, 0:1], scalar2=None,
            op0=ALU.is_gt), [tok, t_lastQKV] + TH, inc="V")
        tm1 = p.do("vector", lambda e, g=g: e.tensor_scalar(
            out=sg1[:, 0:D], in0=g[:, :], scalar1=thr[:, 2:3], scalar2=None,
            op0=ALU.is_lt), (), inc="V")
        dst = wfc2_lo if dc < 4 else wfc2_hi
        dco = dc % 4
        tw = p.do("vector", lambda e, dst=dst, dco=dco, q=q: e.tensor_tensor(
            out=dst[:, dco * FF + q * D:dco * FF + (q + 1) * D],
            in0=sg0[:, 0:D], in1=sg1[:, 0:D], op=ALU.subtract),
            [tm0, tm1, gate], inc="V")
        gcons[i] = tw
        wfc2_toks.append(tw)

    wfcq_sched = {}
    for dc in range(NDC):
        for q in range(4):
            last_col = (dc % 4) * 4096 + q * 1024 + 1023
            gh = min(H - 1, 2 * (last_col // T) + 1)
            wfcq_sched.setdefault(gh, []).append((dc, q))

    vbuf_cons = [[], [], [], []]
    vbuf_tok = [None, None, None, None]
    PT_cons = [None] * 8
    PTS = lambda s: (PT if s < 4 else PT2, (s % 4) * 512)
    attT_toks = []
    rb_cons = [None, None]
    head_st_tok = {}
    lastST = [None]
    for hcp in range(H // 2):
        hA, hB = 2 * hcp, 2 * hcp + 1
        hc = hcp
        grp = max(1, n_tt // 4)
        slA, slB = 2 * (hcp % 2), 2 * (hcp % 2) + 1
        for sl_, hh in ((slA, hA), (slB, hB)):
            tok = p.dma("sync", lambda e, hh=hh, sl_=sl_: e.dma_start(
                out=bass.AP(vbuf, sl_ * 1024,
                            [[4096, 128], [64, n_tt], [1, 64]]),
                in_=bass.AP(vhbm, hh * 64,
                            [[D, 128], [128 * D, n_tt], [1, 64]])),
                [tok_all_vhbm] + vbuf_cons[sl_], queue="dL")
            vbuf_cons[sl_] = []
            vbuf_tok[sl_] = [tok]
        for qg in range(NQG):
            nk = 4 * (qg + 1)
            pend = []
            pv_tok = {hA: None, hB: None}
            first_pv = {(hA, 'main'): True, (hA, 'ones'): True,
                        (hB, 'main'): True, (hB, 'ones'): True}

            def pv_half(head, kt, tkE, pts, c0, kind):
                hp = (head % 2) * 64
                sl_ = slA if head == hA else slB
                ppv = S0 if head == hA else S1
                ptt, ptc = PTS(pts)
                vap = vbuf[:, sl_ * 1024 + kt * 64:sl_ * 1024 + (kt + 1) * 64]
                start = first_pv[(head, kind)]
                first_pv[(head, kind)] = False
                stop = kt == nk - 1
                w = produce(ppv, [tkE] + vbuf_tok[sl_]) if start else [tkE]
                if kind == "ones":
                    tok = p.do("tensor",
                               lambda e, ppv=ppv, hp=hp, ptt=ptt, ptc=ptc,
                               c0=c0, start=start, stop=stop:
                               e.matmul(ppv[64 - hp:128 - hp, c0:512],
                                        ones_bf[:, 0:64],
                                        ptt[:, ptc + c0:ptc + 512],
                                        start=start, stop=stop,
                                        skip_group_check=True,
                                        tile_position=(0, 64 - hp)),
                               w, inc="T")
                else:
                    tok = p.do("tensor",
                               lambda e, ppv=ppv, hp=hp, ptt=ptt, ptc=ptc,
                               c0=c0, vap=vap, start=start, stop=stop:
                               e.matmul(ppv[hp:hp + 64, c0:512], vap,
                                        ptt[:, ptc + c0:ptc + 512],
                                        start=start, stop=stop,
                                        skip_group_check=True,
                                        tile_position=(0, hp)),
                               w, inc="T")
                PT_cons[pts] = tok
                return tok

            def flush_pend():
                pend.sort(key=lambda it: it[1])
                while pend:
                    grp = [pend.pop(0)]
                    while pend and pend[0][1] == grp[0][1]:
                        grp.append(pend.pop(0))
                    for kind in ("main", "ones"):
                        for (h_, kt_, tkE_, pts_, c0_) in grp:
                            pv_tok[h_] = pv_half(h_, kt_, tkE_, pts_, c0_,
                                                 kind)

            for kp in range(0, nk, 2):
                sbase = (kp % 4 // 2) * 4
                stoks = {}
                c0ss = {}
                for kt in (kp, kp + 1):
                    for head, dd in ((hA, D0), (hB, D1)):
                        hp = (head % 2) * 64
                        c0 = max(0, kt * 128 - qg * 512)
                        cb = (kt - kp) * 512
                        ncols = 512 - c0
                        diag = kt >= 4 * qg
                        w = produce(dd, QT_toks + KT_toks + CONSTS) \
                            if kt == kp else ()
                        tkS = mm(dd[:, cb + c0:cb + 512],
                                 KT[hp:hp + 64,
                                    hc * T + kt * 128:hc * T + (kt + 1) * 128],
                                 sap(QT, hp, 64, [[1, ncols]],
                                     hc * T + qg * 512 + c0),
                                 True, not diag, w)
                        if diag:
                            tkS = mm(dd[:, cb + c0:cb + c0 + 128],
                                     ident_bf[:, :], mask_bf[:, :],
                                     False, True, ())
                        stoks[head] = tkS
                        c0ss.setdefault(head, []).append(c0)
                if pend:
                    flush_pend()
                for head, dd, soff in ((hA, D0, 0), (hB, D1, 2)):
                    hp = (head % 2) * 64
                    tkS = stoks[head]
                    c0s = c0ss[head]
                    set_prod(dd, tkS)
                    lastST[0] = tkS
                    head_st_tok[head] = tkS
                    pts = sbase + soff
                    ptt, ptc = PTS(pts)
                    if c0s[0] == 0 and c0s[1] == 0:
                        tkE = p.do("scalar",
                                   lambda e, dd=dd, ptt=ptt, ptc=ptc:
                                   e.activation(
                                       out=ptt[:, ptc:ptc + 1024],
                                       in_=dd[:, :], func=AF.Exp,
                                       bias=consts[:, 0:1], scale=0.125),
                                   consume(dd, [PT_cons[pts],
                                                PT_cons[pts + 1]] + CONSTS),
                                   inc="S")
                    else:
                        for j in range(2):
                            c0 = c0s[j]
                            tkE = p.do("scalar",
                                       lambda e, dd=dd, ptt=ptt, ptc=ptc,
                                       j=j, c0=c0:
                                       e.activation(
                                           out=ptt[:, ptc + j * 512 + c0:
                                                   ptc + (j + 1) * 512],
                                           in_=dd[:, j * 512 + c0:
                                                  (j + 1) * 512],
                                           func=AF.Exp, bias=consts[:, 0:1],
                                           scale=0.125),
                                       consume(dd, [PT_cons[pts + j]]
                                               + CONSTS), inc="S")
                    add_cons(dd, tkE)
                    pend.append((head, kp, tkE, pts, c0s[0]))
                    pend.append((head, kp + 1, tkE, pts + 1, c0s[1]))
            flush_pend()
            for head in (hA, hB):
                hp = (head % 2) * 64
                ppv = S0 if head == hA else S1
                sl_ = slA if head == hA else slB
                set_prod(ppv, pv_tok[head])
                vbuf_cons[sl_].append(pv_tok[head])
                sums_base = 64 - hp
                tkL = p.do("scalar", lambda e, ppv=ppv, sb=sums_base, hp=hp:
                           e.activation(out=rbA[hp:hp + 64, :],
                                        in_=ppv[sb:sb + 64, :], func=AF.Ln),
                           consume(ppv), inc="S")
                tkX = p.do("scalar", lambda e, hp=hp: e.activation(
                    out=rbB[hp:hp + 64, :], in_=rbA[hp:hp + 64, :],
                    func=AF.Exp, scale=-1.0),
                    [tkL, rb_cons[head % 2]], inc="S")
                tkA = p.do("vector", lambda e, ppv=ppv, hp=hp, hc=hc, qg=qg:
                           e.tensor_tensor(
                               out=attT[hp:hp + 64,
                                        hc * T + qg * 512:
                                        hc * T + (qg + 1) * 512],
                               in0=ppv[hp:hp + 64, :], in1=rbB[hp:hp + 64, :],
                               op=ALU.mult), [tkX], inc="V")
                add_cons(ppv, tkA)
                rb_cons[head % 2] = tkA
                attT_toks.append(tkA)
        for hh in (hA, hB):
            for (dcq, qq) in wfcq_sched.get(hh, []):
                emit_wfcq(dcq, qq, head_st_tok[hh])

    t_lastST = lastST[0]
    if debug:
        tda = p.dma("gpsimd", lambda e: e.dma_start(
            out=dbg_att[:, :], in_=attT[:, :]), attT_toks, queue="dG", depth=2)
        p.dma("gpsimd", lambda e: e.dma_start(out=dbg_rb[:, :], in_=rbB[:, :]),
             attT_toks, queue="dG", depth=2)
        attT_toks = attT_toks + [tda]

    xrel_cons = list(xstage_cons)
    x1st_cons = [tok_all_vhbm, tok_all_vhbm]
    r1_cons = [None, None]
    x1_toks = []
    lastWO = [None]
    for ti in range(n_tt):
        sl = ti % 2
        tok_x = p.dma("sync", lambda e, ti=ti, sl=sl: e.dma_start(
            out=stage_x[:, sl * D:(sl + 1) * D],
            in_=x_h[ti * 128:(ti + 1) * 128, :]), [xrel_cons[sl]],
            queue="dL")
        DW = D0 if ti % 2 == 0 else D1
        wA = produce(DW, attT_toks + wo_toks)
        for fc in range(NDC):
            la = attT[:, fc * T + ti * 128:fc * T + (ti + 1) * 128]
            mm(DW[:, 0:512], la, wslot[:, fc * D:fc * D + 512],
               fc == 0, False, wA if fc == 0 else ())
            mm(DW[:, 512:1024], la, wslot[:, fc * D + 512:(fc + 1) * D],
               fc == 0, False, ())
        mm(DW[:, 0:512], ones_bf[0:1, 0:128], wob_bf[0:1, 0:512], False, True,
           [tok_wob])
        tkB = mm(DW[:, 512:1024], ones_bf[0:1, 0:128], wob_bf[0:1, 512:1024],
                 False, True, ())
        set_prod(DW, tkB)
        lastWO[0] = tkB
        tr1a = p.do("vector", lambda e, sl=sl, DW=DW: e.tensor_add(
            out=r1buf[:, sl * D:sl * D + 512], in0=DW[:, 0:512],
            in1=stage_x[:, sl * D:sl * D + 512]),
            consume(DW, [tok_x, r1_cons[sl]]), inc="V")
        tr1b = p.do("vector", lambda e, sl=sl, DW=DW: e.tensor_add(
            out=r1buf[:, sl * D + 512:(sl + 1) * D], in0=DW[:, 512:1024],
            in1=stage_x[:, sl * D + 512:(sl + 1) * D]), (), inc="V")
        add_cons(DW, tr1b)
        xrel_cons[sl] = tr1b
        r1 = r1buf[:, sl * D:(sl + 1) * D]
        pc = 8 * (ti % 2)
        s1, s2 = sm[:, pc:pc + 1], sm[:, pc + 1:pc + 2]
        mu, m2 = sm[:, pc + 2:pc + 3], sm[:, pc + 3:pc + 4]
        var, rstd = sm[:, pc + 4:pc + 5], sm[:, pc + 5:pc + 6]
        nb = sm[:, pc + 6:pc + 7]
        ts1 = p.do("vector", lambda e, r1=r1, s1=s1: e.reduce_sum(
            s1, r1, axis=AX.X), [tr1a, tr1b], inc="V")
        ts2 = p.do("scalar", lambda e, r1=r1, s2=s2: e.activation(
            out=sqscr[:, 0:D], in_=r1, func=AF.Square, accum_out=s2),
            [tr1a, tr1b], inc="S")
        p.do("vector", lambda e, s2=s2, m2=m2: e.tensor_scalar(
            out=m2, in0=s2, scalar1=float(D), scalar2=None, op0=ALU.mult),
            [ts1, ts2], inc="V")
        tv = p.do("vector", lambda e, s1=s1, m2=m2, var=var:
                  e.scalar_tensor_tensor(
                      out=var, in0=s1, scalar=s1, in1=m2,
                      op0=ALU.mult, op1=ALU.subtract), (), inc="V")
        tl = p.do("scalar", lambda e, var=var, mu=mu: e.activation(
            out=mu, in_=var, func=AF.Ln, bias=consts[:, 1:2],
            scale=-1.0 / (D * D)), [tv] + CONSTS, inc="S")
        tl2 = p.do("scalar", lambda e, mu=mu, rstd=rstd: e.activation(
            out=rstd, in_=mu, func=AF.Exp, scale=-0.5), [tl], inc="S")
        tnb = p.do("vector", lambda e, s1=s1, nb=nb, rstd=rstd: e.tensor_scalar(
            out=nb, in0=s1, scalar1=rstd, scalar2=-1.0 / D, op0=ALU.mult,
            op1=ALU.mult), [tl2], inc="V")
        x1st = x1stage0 if sl == 0 else x1stage1
        tap = p.do("scalar", lambda e, r1=r1, rstd=rstd, nb=nb, x1st=x1st:
                   e.activation(
                       out=x1st[:, :], in_=r1,
                       func=AF.Identity, scale=rstd, bias=nb),
                   [tnb, x1st_cons[sl]], inc="S")
        r1_cons[sl] = tap
        tst = p.dma("gpsimd", lambda e, ti=ti, x1st=x1st: e.dma_start(
            out=x1hbm[ti * 128:(ti + 1) * 128, :],
            in_=x1st[:, :]), [tap], queue="dG", depth=2)
        x1st_cons[sl] = tst
        x1_toks.append(tst)
    t_lastWO = lastWO[0]
    t_x1_done = x1_toks[-1]
    if debug:
        p.dma("gpsimd", lambda e: e.dma_start(out=dbg_sm[:, :], in_=sm[:, :]),
             [t_x1_done], queue="dG", depth=2)
        p.dma("gpsimd", lambda e: e.dma_start(out=dbg_r1[:, :], in_=r1buf[:, :]),
             [t_x1_done], queue="dG", depth=2)

    wps_toks = []
    ctk = None
    tkc = None
    for hc in range(NHC):
        i = gq[0] % 2
        gq[0] += 1
        g = gslot[i]
        tok = p.dma("scalar", lambda e, g=g, hc=hc: e.dma_start(
            out=g[:, :], in_=wprojT_h[hc * 128:(hc + 1) * 128, :]),
            [gcons[i]], queue="dW", depth=2)
        tm0 = p.do("vector", lambda e, g=g: e.tensor_scalar(
            out=sg0[:, 0:D], in0=g[:, :], scalar1=thr[:, 1:2], scalar2=None,
            op0=ALU.is_gt), [tok] + TH, inc="V")
        tm1 = p.do("vector", lambda e, g=g: e.tensor_scalar(
            out=sg1[:, 0:D], in0=g[:, :], scalar1=thr[:, 3:4], scalar2=None,
            op0=ALU.is_lt), (), inc="V")
        dst = wps_lo if hc < 16 else wps_hi
        hco = hc % 16
        gate = t_lastWO if hc < 16 else t_x1_done
        tw = p.do("vector", lambda e, dst=dst, hco=hco: e.tensor_tensor(
            out=dst[:, hco * D:(hco + 1) * D], in0=sg0[:, 0:D],
            in1=sg1[:, 0:D], op=ALU.subtract), [tm0, tm1, gate], inc="V")
        gcons[i] = tw
        for half, ct_ps in ((0, S0), (1, S1)):
            w = produce(ct_ps, [tw]) if hc == 0 else [tw]
            ctk = mm(sap(ct_ps, 0, 1, [[1, 512]]),
                     lnb_half_bf[:, hc:hc + 1],
                     dst[:, hco * D + half * 512:hco * D + (half + 1) * 512],
                     hc == 0, hc == NHC - 1, w)
        tsc = p.do("scalar", lambda e, dst=dst, hco=hco, hc=hc: e.activation(
            out=dst[:, hco * D:(hco + 1) * D],
            in_=dst[:, hco * D:(hco + 1) * D],
            func=AF.Identity, scale=lnw_sb[:, hc:hc + 1]),
            [ctk], inc="S")
        wps_toks.append(tsc)
    set_prod(S0, ctk)
    set_prod(S1, ctk)
    tkc = None
    for half, ct_ps in ((0, S0), (1, S1)):
        tkc = p.do("scalar", lambda e, half=half, ct_ps=ct_ps: e.activation(
            out=c_bf[0:1, half * 512:(half + 1) * 512],
            in_=sap(ct_ps, 0, 1, [[1, 512]]), func=AF.Identity),
            consume(ct_ps), inc="S")
        add_cons(ct_ps, tkc)
    t_quant = [wfc2_toks[-1], wps_toks[-1], tkc]
    if debug:
        p.dma("gpsimd", lambda e: e.dma_start(out=dbg_thr[:, :], in_=thr[:, :]),
             TH, queue="dG", depth=2)
        p.dma("gpsimd", lambda e: e.dma_start(
            out=dap(dbg_wfc2, 0, [[8 * FF, 128], [1, 4 * FF]]),
            in_=wfc2_lo[:, :]), wfc2_toks, queue="dG", depth=2)
        p.dma("gpsimd", lambda e: e.dma_start(
            out=dap(dbg_wfc2, 4 * FF, [[8 * FF, 128], [1, 4 * FF]]),
            in_=wfc2_hi[:, :]), wfc2_toks, queue="dG", depth=2)
        p.dma("gpsimd", lambda e: e.dma_start(
            out=dap(dbg_wps, 0, [[32 * D, 128], [1, 16 * D]]),
            in_=wps_lo[:, :]), wps_toks, queue="dG", depth=2)
        p.dma("gpsimd", lambda e: e.dma_start(
            out=dap(dbg_wps, 16 * D, [[32 * D, 128], [1, 16 * D]]),
            in_=wps_hi[:, :]), wps_toks, queue="dG", depth=2)
        tcb = p.dma("gpsimd", lambda e: e.dma_start(
            out=dbg_cbf[0:1, :], in_=c_bf[0:1, :]), [tkc], queue="dG", depth=2)
        t_quant = t_quant + [tcb]

    x1t_cons = [None, None, None, None]
    x1T_mm_cons = [None, None]
    prev_proj_mm = [None]
    prev_h2_transp = [None, None]
    out_cons = [None, None]
    out_toks = []
    state = {}

    def p7_stage_a1(tt):
        sl = tt % 2
        sl4 = tt % 4
        tok_x1 = p.dma("sync", lambda e, tt=tt, sl4=sl4: e.dma_start(
            out=x1tile[:, sl4 * D:(sl4 + 1) * D],
            in_=x1hbm[tt * 128:(tt + 1) * 128, :]),
            [x1_toks[tt], x1t_cons[sl4]], queue="dL")
        tb = emit_transpose8(x1tile, sl4 * D, [tok_x1])
        tok_x1T = p.do("vector", lambda e, tb=tb, sl=sl: e.tensor_copy(
            out=x1T[:, sl * D:(sl + 1) * D], in_=sap(tb, 0, 128, [[1, 1024]])),
            consume(tb, [x1T_mm_cons[sl]]), inc="V")
        add_cons(tb, tok_x1T)
        return tok_x1T

    def p7_stage_a(tt, tok_x1T):
        sl = tt % 2
        st = {}
        gelu_toks = []
        pc = 8 * (tt % 2)
        for hb in range(0, NHB, 2):
            dd = [D0, D1][(hb // 2) % 2]
            w = produce(dd, [tok_x1T] + wfc2_toks)
            tkF = None
            for dc in range(NDC):
                wsrc = wfc2_lo if dc < 4 else wfc2_hi
                dco = dc % 4
                tkF = mm(dd[:, 0:512],
                         x1T[:, sl * D + dc * 128:sl * D + (dc + 1) * 128],
                         wsrc[:, dco * FF + hb * 512:dco * FF + (hb + 1) * 512],
                         dc == 0, False, w if dc == 0 else ())
                tkF = mm(dd[:, 512:1024],
                         x1T[:, sl * D + dc * 128:sl * D + (dc + 1) * 128],
                         wsrc[:, dco * FF + (hb + 1) * 512:
                              dco * FF + (hb + 2) * 512],
                         dc == 0, dc == NDC - 1, ())
            set_prod(dd, tkF)
            x1T_mm_cons[sl] = tkF
            tkG = p.do("scalar", lambda e, dd=dd, hb=hb, pc=pc: e.activation(
                out=gbuf[:, hb * 512:(hb + 2) * 512], in_=dd[:, :],
                func=AF.Gelu, scale=1.0,
                accum_out=gs[:, pc + hb // 2:pc + hb // 2 + 1]),
                consume(dd), inc="S")
            add_cons(dd, tkG)
            tkQ = p.do("scalar", lambda e, hb=hb, pc=pc: e.activation(
                out=sqscr[:, hb * 512:(hb + 2) * 512],
                in_=gbuf[:, hb * 512:(hb + 2) * 512], func=AF.Square,
                accum_out=gs[:, pc + 4 + hb // 2:pc + 5 + hb // 2]),
                [tkG], inc="S")
            gelu_toks.append(tkQ)
        if tt + 1 < n_tt:
            a1_toks[tt + 1] = p7_stage_a1(tt + 1)
        pcs = 16 + 8 * (tt % 2)
        s1, s2 = sm[:, pcs:pcs + 1], sm[:, pcs + 1:pcs + 2]
        m2 = sm[:, pcs + 3:pcs + 4]
        var, rstd = sm[:, pcs + 4:pcs + 5], sm[:, pcs + 5:pcs + 6]
        mu, nb = sm[:, pcs + 2:pcs + 3], sm[:, pcs + 6:pcs + 7]
        ts1 = p.do("vector", lambda e, pc=pc, s1=s1: e.reduce_sum(
            s1, gs[:, pc:pc + 4], axis=AX.X), gelu_toks, inc="V")
        ts2 = p.do("vector", lambda e, pc=pc, s2=s2: e.reduce_sum(
            s2, gs[:, pc + 4:pc + 8], axis=AX.X), gelu_toks, inc="V")
        p.do("vector", lambda e, s2=s2, m2=m2: e.tensor_scalar(
            out=m2, in0=s2, scalar1=float(FF), scalar2=None, op0=ALU.mult),
            [ts1, ts2], inc="V")
        tv = p.do("vector", lambda e, s1=s1, m2=m2, var=var:
                  e.scalar_tensor_tensor(
                      out=var, in0=s1, scalar=s1, in1=m2,
                      op0=ALU.mult, op1=ALU.subtract), (), inc="V")
        tl = p.do("scalar", lambda e, var=var, mu=mu: e.activation(
            out=mu, in_=var, func=AF.Ln, bias=consts[:, 1:2],
            scale=-1.0 / (float(FF) * FF)), [tv], inc="S")
        tl2 = p.do("scalar", lambda e, mu=mu, rstd=rstd: e.activation(
            out=rstd, in_=mu, func=AF.Exp, scale=-0.5), [tl], inc="S")
        tnb = p.do("vector", lambda e, s1=s1, nb=nb, rstd=rstd: e.tensor_scalar(
            out=nb, in0=s1, scalar1=rstd, scalar2=-1.0 / FF, op0=ALU.mult,
            op1=ALU.mult), [tl2], inc="V")
        h2dst = h2t if sl == 0 else h2t2
        th2a = p.do("scalar", lambda e, rstd=rstd, nb=nb, h2dst=h2dst:
                    e.activation(
                        out=h2dst[:, 0:2048], in_=gbuf[:, 0:2048],
                        func=AF.Identity, scale=rstd, bias=nb),
                    [tnb, prev_h2_transp[sl]] + t_quant, inc="S")
        th2b = p.do("scalar", lambda e, rstd=rstd, nb=nb, h2dst=h2dst:
                    e.activation(
                        out=h2dst[:, 2048:4096], in_=gbuf[:, 2048:4096],
                        func=AF.Identity, scale=rstd, bias=nb),
                    (), inc="S")
        st["th2"] = (th2a, th2b)
        st["sl"] = sl
        return st

    def p7_stage_b(tt, st):
        sl = st["sl"]
        h2src = h2t if sl == 0 else h2t2
        h2T_toks = []
        for g4 in range(4):
            tbk = pb[pb_rot[0] % 2]
            pb_rot[0] += 1
            w = produce(tbk, [st["th2"][0 if g4 < 2 else 1]])
            tkT = None
            for j in range(8):
                cj = g4 * 8 + j
                tkT = p.do("tensor",
                           lambda e, tbk=tbk, cj=cj, j=j, h2src=h2src:
                           e.transpose(
                               sap(tbk, 0, 128, [[1, 128]], j * 128),
                               h2src[:, cj * 128:(cj + 1) * 128],
                               ident_bf[:, :]),
                           w if j == 0 else (), inc="T")
            set_prod(tbk, tkT)
            prev_h2_transp[sl] = tkT
            tke = p.do("vector", lambda e, tbk=tbk, g4=g4: e.tensor_copy(
                out=h2T_t[:, g4 * 1024:(g4 + 1) * 1024],
                in_=sap(tbk, 0, 128, [[1, 1024]])),
                consume(tbk, [prev_proj_mm[0]]), inc="V")
            add_cons(tbk, tke)
            h2T_toks.append(tke)
        wA = produce(S0, h2T_toks + wps_toks)
        wB = produce(S1, h2T_toks + wps_toks)
        for hc in range(NHC):
            wsrc = wps_lo if hc < 16 else wps_hi
            hco = hc % 16
            la = h2T_t[:, hc * 128:(hc + 1) * 128]
            mm(S0[:, :], la, wsrc[:, hco * D:hco * D + 512],
               hc == 0, False, wA if hc == 0 else ())
            mm(S1[:, :], la, wsrc[:, hco * D + 512:(hco + 1) * D],
               hc == 0, False, wB if hc == 0 else ())
        tkA = mm(S0[:, :], ones_bf[0:1, 0:128], c_bf[0:1, 0:512], False, True,
                 [tkc])
        tkB = mm(S1[:, :], ones_bf[0:1, 0:128], c_bf[0:1, 512:1024],
                 False, True, ())
        set_prod(S0, tkA)
        set_prod(S1, tkB)
        prev_proj_mm[0] = tkB
        sl4 = tt % 4
        tr2a = p.do("vector", lambda e, sl4=sl4: e.tensor_add(
            out=r2buf[:, 0:512], in0=S0[:, :],
            in1=x1tile[:, sl4 * D:sl4 * D + 512]),
            consume(S0, [out_cons[0]]), inc="V")
        add_cons(S0, tr2a)
        tr2b = p.do("vector", lambda e, sl4=sl4: e.tensor_add(
            out=r2buf[:, 512:1024], in0=S1[:, :],
            in1=x1tile[:, sl4 * D + 512:(sl4 + 1) * D]),
            consume(S1, [out_cons[0]]), inc="V")
        add_cons(S1, tr2b)
        x1t_cons[sl4] = tr2b
        pcs = 32 + 8 * (tt % 2)
        s1, s2 = sm[:, pcs:pcs + 1], sm[:, pcs + 1:pcs + 2]
        m2 = sm[:, pcs + 3:pcs + 4]
        var, rstd = sm[:, pcs + 4:pcs + 5], sm[:, pcs + 5:pcs + 6]
        mu, nb = sm[:, pcs + 2:pcs + 3], sm[:, pcs + 6:pcs + 7]
        ts1 = p.do("vector", lambda e, s1=s1: e.reduce_sum(
            s1, r2buf[:, :], axis=AX.X), [tr2a, tr2b], inc="V")
        ts2 = p.do("scalar", lambda e, s2=s2: e.activation(
            out=sqscr[:, 0:D], in_=r2buf[:, :], func=AF.Square, accum_out=s2),
            [tr2a, tr2b], inc="S")
        p.do("vector", lambda e, s2=s2, m2=m2: e.tensor_scalar(
            out=m2, in0=s2, scalar1=float(D), scalar2=None, op0=ALU.mult),
            [ts1, ts2], inc="V")
        tv = p.do("vector", lambda e, s1=s1, m2=m2, var=var:
                  e.scalar_tensor_tensor(
                      out=var, in0=s1, scalar=s1, in1=m2,
                      op0=ALU.mult, op1=ALU.subtract), (), inc="V")
        tl = p.do("scalar", lambda e, var=var, mu=mu: e.activation(
            out=mu, in_=var, func=AF.Ln, bias=consts[:, 1:2],
            scale=-1.0 / (D * D)), [tv], inc="S")
        tl2 = p.do("scalar", lambda e, mu=mu, rstd=rstd: e.activation(
            out=rstd, in_=mu, func=AF.Exp, scale=-0.5), [tl], inc="S")
        tnb = p.do("vector", lambda e, s1=s1, nb=nb, rstd=rstd: e.tensor_scalar(
            out=nb, in0=s1, scalar1=rstd, scalar2=-1.0 / D, op0=ALU.mult,
            op1=ALU.mult), [tl2], inc="V")
        tout = p.do("scalar", lambda e, rstd=rstd, nb=nb: e.activation(
            out=outbuf[:, :], in_=r2buf[:, :], func=AF.Identity, scale=rstd,
            bias=nb), [tnb, out_cons[1]], inc="S")
        tst = p.dma("gpsimd", lambda e, tt=tt: e.dma_start(
            out=out_h[tt * 128:(tt + 1) * 128, :], in_=outbuf[:, :]),
            [tout], queue="dO", depth=2)
        out_cons[0] = tst
        out_cons[1] = tst
        out_toks.append(tst)

    prev = None
    a1_toks = {0: None}
    a1_toks[0] = p7_stage_a1(0)
    for tt in range(n_tt):
        cur = p7_stage_a(tt, a1_toks.pop(tt))
        if prev is not None:
            p7_stage_b(tt - 1, prev)
        prev = cur
    p7_stage_b(n_tt - 1, prev)

    p.wait_only("sync", out_toks[-2:])

    with nc.Block() as block:
        @block.sync
        def _(e):
            p.replay("sync", e)

        @block.tensor
        def _(e):
            p.replay("tensor", e)

        @block.vector
        def _(e):
            p.replay("vector", e)

        @block.scalar
        def _(e):
            p.replay("scalar", e)

        @block.gpsimd
        def _(e):
            p.replay("gpsimd", e)

    return nc


def make_xtt(x2d, n_tt=16):
    T = n_tt * 128
    return np.ascontiguousarray(
        x2d.reshape(n_tt, 128, NDC, 128).transpose(0, 3, 2, 1).reshape(T, D))


def host_prep(wq, wk, wv, wo, wob, wfc, lnw, lnb, wproj, n_tt=16):
    T = n_tt * 128
    perm = np.concatenate([np.concatenate([64 * h + np.arange(0, 64, 2),
                                           64 * h + np.arange(1, 64, 2)])
                           for h in range(H)])
    d = dict(
        wq=np.ascontiguousarray(np.asarray(wq, np.float32)[:, perm]),
        wk=np.ascontiguousarray(np.asarray(wk, np.float32)[:, perm]),
        wv=np.ascontiguousarray(np.asarray(wv, np.float32)),
        wo=np.ascontiguousarray(np.asarray(wo, np.float32)),
        wob=np.ascontiguousarray(np.asarray(wob, np.float32)),
        wfcT=np.ascontiguousarray(np.asarray(wfc, np.float32).T),
        lnw=np.ascontiguousarray(np.asarray(lnw, np.float32)),
        lnb=np.ascontiguousarray(np.asarray(lnb, np.float32)),
        wprojT=np.ascontiguousarray(np.asarray(wproj, np.float32).T),
    )
    import jax
    import jax.numpy as jnp
    with jax.default_device(jax.devices("cpu")[0]):
        invj = 1.0 / 10000.0 ** (jnp.arange(0, HD, 2)[:HD // 2].astype(
            jnp.float32) / HD)
        angj = jnp.outer(jnp.arange(T, dtype=jnp.float32), invj)
        cosv = np.asarray(jnp.cos(angj)).reshape(n_tt, 128, 32)
        sinv = np.asarray(jnp.sin(angj)).reshape(n_tt, 128, 32)
    for nm, v in (("ctab", cosv), ("stab", sinv)):
        tab = np.zeros((128, n_tt * 512), np.float32)
        for tt in range(n_tt):
            tab[:, tt * 512:(tt + 1) * 512] = np.tile(v[tt], (1, H))
        d[nm] = tab
    eye = np.eye(128, dtype=np.float32)
    maskv = np.tril(np.full((128, 128), -30000.0, np.float32), -1)
    onesv = np.ones((128, 128), np.float32)
    d["cpkb"] = np.concatenate([eye, maskv, onesv], axis=1).astype(
        ml_dtypes.bfloat16)
    cvec = np.zeros((128, 8), np.float32)
    cvec[:, 0] = -5.0
    cvec[:, 1] = 1e-5
    d["cpkf"] = np.concatenate([eye, onesv, cvec], axis=1)
    return d


def kernel(L0_attn_x, L0_attn_wq, L0_attn_wk, L0_attn_wv, L0_attn_wo_w,
           L0_attn_wo_b, L0_ffn_W_fc, L0_ffn_ln_w, L0_ffn_ln_b, L0_ffn_W_proj):
    x = np.ascontiguousarray(np.asarray(L0_attn_x, np.float32))
    n_cores = x.shape[0]
    shared = host_prep(L0_attn_wq, L0_attn_wk, L0_attn_wv, L0_attn_wo_w,
                       L0_attn_wo_b, L0_ffn_W_fc, L0_ffn_ln_w, L0_ffn_ln_b,
                       L0_ffn_W_proj, n_tt=16)
    nc = build(n_tt=16)
    in_maps = [dict(shared, x=np.ascontiguousarray(x[b]),
                    xtt=make_xtt(x[b]))
               for b in range(n_cores)]
    res = run_bass_kernel_spmd(nc, in_maps, core_ids=list(range(n_cores)))
    return np.stack([r["out"] for r in res.results], axis=0)


# version 29
# speedup vs baseline: 1.0631x; 1.0631x over previous
import os

import ml_dtypes
import numpy as np
import concourse.bass as bass
import concourse.mybir as mybir
from concourse.bass_utils import run_bass_kernel_spmd

F32 = mybir.dt.float32
BF16 = mybir.dt.bfloat16
AF = mybir.ActivationFunctionType
ALU = mybir.AluOpType
AX = mybir.AxisListType

ENGINES = ("sync", "tensor", "vector", "scalar", "gpsimd")
ENG_SEM = {"tensor": "T", "vector": "V", "scalar": "S", "gpsimd": "G"}

D = 1024
H = 16
HD = 64
FF = 4096
NDC = D // 128
NHC = FF // 128
NHB = FF // 512


class Prog:
    def __init__(self, nc):
        self.nc = nc
        self.ops = {e: [] for e in ENGINES}
        self.sems = {}
        self.counts = {}
        self.qstate = {}
        self.hwm = {e: {} for e in ENGINES}
        self.strict = os.environ.get("BASS_SIM_FP64") == "1"

    def do(self, eng, fn, waits=(), inc=None, n=1, self_wait=False,
           hard=False):
        need = {}
        for w in waits:
            if w is None:
                continue
            s, v = w
            if v > 0 and v > need.get(s, 0):
                need[s] = v
        wl = []
        if self.strict and fn is not None and eng in ENG_SEM:
            s = ENG_SEM[eng]
            prev = self.counts.get(s, 0)
            if inc is None:
                inc = s
                if s not in self.sems:
                    self.sems[s] = self.nc.alloc_semaphore(s)
                    self.counts[s] = 0
            if prev > 0:
                wl.append((s, prev))
                if prev > self.hwm[eng].get(s, 0):
                    self.hwm[eng][s] = prev
        for s, v in need.items():
            if not hard and v <= self.hwm[eng].get(s, 0):
                continue
            if v > self.hwm[eng].get(s, 0):
                self.hwm[eng][s] = v
            wl.append((s, v))
        tok = None
        if inc is not None:
            if inc not in self.sems:
                self.sems[inc] = self.nc.alloc_semaphore(inc)
                self.counts[inc] = 0
            self.counts[inc] += n
            tok = (inc, self.counts[inc])
        self.ops[eng].append((fn, wl, inc, n))
        if self_wait and tok is not None:
            s, v = tok
            if v > self.hwm[eng].get(s, 0):
                self.hwm[eng][s] = v
                self.ops[eng].append((None, [(s, v)], None, 1))
        return tok

    def wait_only(self, eng, waits):
        self.do(eng, None, waits)

    def dma(self, eng, fn, waits=(), queue="dL", depth=4):
        st = self.qstate.setdefault((eng, queue), {"i": 0, "last": {}})
        i = st["i"]
        st["i"] += 1
        sem = "%s%d" % (queue, i % depth)
        w = list(waits)
        prev = st["last"].get(i % depth)
        if prev is not None:
            w.append(prev)
        tok = self.do(eng, fn, w, inc=sem, n=16)
        st["last"][i % depth] = tok
        return tok

    def replay(self, eng_name, e):
        for fn, wl, inc, n in self.ops[eng_name]:
            for s, v in wl:
                e.wait_ge(self.sems[s], v)
            if fn is None:
                continue
            inst = fn(e)
            if inc is not None:
                inst.then_inc(self.sems[inc], n)


def sap(t, p0, nparts, fdims, c0=0):
    w = t.shape[-1]
    return bass.AP(t, p0 * w + c0, [[w, nparts]] + [list(d) for d in fdims])


def dap(t, off, dims):
    return bass.AP(t, off, [list(d) for d in dims])


class Bank:
    __slots__ = ("prod", "cons")

    def __init__(self):
        self.prod = None
        self.cons = []


def build(n_tt=16, debug=False):
    T = n_tt * 128
    NQG = n_tt // 4
    nc = bass.Bass("TRN2", target_bir_lowering=False)

    x_h = nc.dram_tensor("x", [T, D], F32, kind="ExternalInput")
    xtt_h = nc.dram_tensor("xtt", [T, D], F32, kind="ExternalInput")
    wq_h = nc.dram_tensor("wq", [D, D], F32, kind="ExternalInput")
    wk_h = nc.dram_tensor("wk", [D, D], F32, kind="ExternalInput")
    wv_h = nc.dram_tensor("wv", [D, D], F32, kind="ExternalInput")
    wo_h = nc.dram_tensor("wo", [D, D], F32, kind="ExternalInput")
    wob_h = nc.dram_tensor("wob", [D], F32, kind="ExternalInput")
    wfcT_h = nc.dram_tensor("wfcT", [D, FF], F32, kind="ExternalInput")
    lnw_h = nc.dram_tensor("lnw", [FF], F32, kind="ExternalInput")
    lnb_h = nc.dram_tensor("lnb", [FF], F32, kind="ExternalInput")
    wprojT_h = nc.dram_tensor("wprojT", [FF, D], F32, kind="ExternalInput")
    ctab_h = nc.dram_tensor("ctab", [128, n_tt * 512], F32, kind="ExternalInput")
    stab_h = nc.dram_tensor("stab", [128, n_tt * 512], F32, kind="ExternalInput")
    cpkb_h = nc.dram_tensor("cpkb", [128, 384], BF16, kind="ExternalInput")
    cpkf_h = nc.dram_tensor("cpkf", [128, 264], F32, kind="ExternalInput")
    out_h = nc.dram_tensor("out", [T, D], F32, kind="ExternalOutput")
    iokind = "ExternalOutput" if debug else "Internal"
    vhbm = nc.dram_tensor("v_hbm", [T, D], BF16, kind=iokind)
    x1hbm = nc.dram_tensor("x1_hbm", [T, D], BF16, kind=iokind)
    if debug:
        dbg_qt = nc.dram_tensor("dbg_qt", [128, NDC * T], BF16, kind="ExternalOutput")
        dbg_kt = nc.dram_tensor("dbg_kt", [128, NDC * T], BF16, kind="ExternalOutput")
        dbg_att = nc.dram_tensor("dbg_att", [128, NDC * T], BF16, kind="ExternalOutput")
        dbg_thr = nc.dram_tensor("dbg_thr", [128, 4], F32, kind="ExternalOutput")
        dbg_wfc2 = nc.dram_tensor("dbg_wfc2", [128, 8 * FF], BF16, kind="ExternalOutput")
        dbg_wps = nc.dram_tensor("dbg_wps", [128, 32 * D], BF16, kind="ExternalOutput")
        dbg_cbf = nc.dram_tensor("dbg_cbf", [1, 1024], BF16, kind="ExternalOutput")
        dbg_rb = nc.dram_tensor("dbg_rb", [128, 512], F32, kind="ExternalOutput")
        dbg_sm = nc.dram_tensor("dbg_sm", [128, 64], F32, kind="ExternalOutput")
        dbg_r1 = nc.dram_tensor("dbg_r1", [128, 2048], F32, kind="ExternalOutput")

    AT = nc.alloc_sbuf_tensor_at
    ident_bf = AT("ident_bf", [128, 128], BF16, offset=17024)
    ident_f32 = AT("ident_f32", [128, 128], F32, offset=17280)
    mask_bf = AT("mask_bf", [128, 128], BF16, offset=17792)
    ones_bf = AT("ones_bf", [128, 128], BF16, offset=18048)
    ones_f32 = AT("ones_f32", [128, 128], F32, offset=18304)
    consts = AT("consts", [128, 8], F32, offset=18816)
    sm = AT("sm", [128, 64], F32, offset=18848)
    gp = AT("gp", [128, 72], F32, offset=19104)
    thr = AT("thr", [128, 4], F32, offset=19392)
    lnw_raw = AT("lnw_raw", [32, 128], F32, offset=19424)
    lnb_raw = AT("lnb_raw", [32, 128], F32, offset=19936)
    lnw_sb = AT("lnw_sb", [128, 32], F32, offset=20448)
    lnb_half_bf = AT("lnb_half_bf", [128, 32], BF16, offset=20576)
    gs = AT("gs", [128, 16], F32, offset=20640)
    wob_bf = AT("wob_bf", [1, 1024], BF16, offset=20704)
    wob_f = AT("wob_f", [1, 1024], F32, offset=217728)
    c_bf = AT("c_bf", [1, 1024], BF16, offset=22752)
    QT = AT("QT", [128, NDC * T], BF16, offset=25216)
    wfc2_lo = AT("wfc2_lo", [128, 4 * FF], BF16, offset=25216)
    KT = AT("KT", [128, NDC * T], BF16, offset=57984)
    wfc2_hi = AT("wfc2_hi", [128, 4 * FF], BF16, offset=57984)
    attT = AT("attT", [128, NDC * T], BF16, offset=90752)
    wps_lo = AT("wps_lo", [128, 16 * D], BF16, offset=90752)
    wslot = AT("wslot", [128, 3 * NDC * D], BF16, offset=123520)
    wps_hi = AT("wps_hi", [128, 16 * D], BF16, offset=123520)
    r1buf = AT("r1buf", [128, 2 * D], F32, offset=139904)
    tabC2 = AT("tabC2", [128, 512], F32, offset=92800)
    tabS2 = AT("tabS2", [128, 512], F32, offset=94848)
    sg0 = AT("sg0", [128, FF], BF16, offset=156288)
    sg1 = AT("sg1", [128, FF], BF16, offset=164480)
    h2t = AT("h2t", [128, FF], BF16, offset=156288)
    h2t2 = AT("h2t2", [128, FF], BF16, offset=164480)
    stage_x = AT("stage_x", [128, 2 * D], F32, offset=172672)
    r2buf = AT("r2buf", [128, D], F32, offset=172672)
    outbuf = AT("outbuf", [128, D], F32, offset=176768)
    tabC = AT("tabC", [128, 512], F32, offset=180864)
    tabS = AT("tabS", [128, 512], F32, offset=182912)
    x1tile = AT("x1tile", [128, 4 * D], BF16, offset=217728)
    x1T = AT("x1T", [128, 2 * D], BF16, offset=184960)
    tmps = AT("tmps", [128, 4 * D], BF16, offset=189056)
    h2T_t = AT("h2T", [128, FF], BF16, offset=189056)
    PT = AT("PT", [128, 4 * 512], BF16, offset=197248)
    PT2 = AT("PT2", [128, 4 * 512], BF16, offset=180864)
    gbuf = AT("gbuf", [128, FF], BF16, offset=197248)
    vbuf = AT("vbuf", [128, 2 * 2048], BF16, offset=205440)
    rbA = AT("rbA", [128, 512], F32, offset=213632)
    rbB = AT("rbB", [128, 512], F32, offset=215680)
    wstage = AT("wstage", [128, 2 * D], F32, offset=217728)
    rscr = AT("rscr", [128, 512], F32, offset=213632)
    vstage = AT("vstage", [128, D], BF16, offset=225920)
    x1stage0 = AT("x1stage0", [128, D], BF16, offset=225920)
    x1stage1 = AT("x1stage1", [128, D], BF16, offset=215680)
    sqscr = AT("sqscr", [128, FF], BF16, offset=189056)
    gsta = AT("gsta", [128, 1024], F32, offset=184960)
    gstb = AT("gstb", [128, 1024], F32, offset=201344)
    D0 = nc.alloc_psum_tensor("D0", [128, 1024], F32)
    D1 = nc.alloc_psum_tensor("D1", [128, 1024], F32)
    S0 = nc.alloc_psum_tensor("S0", [128, 512], F32)
    S1 = nc.alloc_psum_tensor("S1", [128, 512], F32)
    pb = [nc.alloc_psum_tensor("pb%d" % i, [128, 1024], BF16) for i in range(2)]

    p = Prog(nc)
    bank = {id(t): Bank() for t in [D0, D1, S0, S1] + pb}

    def B(t):
        return bank[id(t)]

    def mm(out_ap, lhsT_ap, rhs_ap, start, stop, waits=(), inc="T"):
        return p.do("tensor",
                    lambda e, o=out_ap, l=lhsT_ap, r=rhs_ap, s=start, st=stop:
                    e.matmul(o, l, r, start=s, stop=st,
                             skip_group_check=True),
                    waits, inc)

    def produce(t, waits):
        return list(waits) + B(t).cons

    def set_prod(t, tok):
        B(t).prod = tok
        B(t).cons = []

    def consume(t, waits=()):
        return list(waits) + ([B(t).prod] if B(t).prod else [])

    def add_cons(t, tok):
        B(t).cons.append(tok)

    tcb1 = p.dma("sync", lambda e: e.dma_start(
        out=ident_bf[:, :], in_=cpkb_h[:, 0:128]), queue="dL")
    tcb2 = p.dma("sync", lambda e: e.dma_start(
        out=mask_bf[:, :], in_=cpkb_h[:, 128:256]), queue="dL")
    tcb3 = p.dma("sync", lambda e: e.dma_start(
        out=ones_bf[:, :], in_=cpkb_h[:, 256:384]), queue="dL")
    tcf1 = p.dma("sync", lambda e: e.dma_start(
        out=ident_f32[:, :], in_=cpkf_h[:, 0:128]), queue="dL")
    tcf2 = p.dma("sync", lambda e: e.dma_start(
        out=ones_f32[:, :], in_=cpkf_h[:, 128:256]), queue="dL")
    tok_consts = p.dma("sync", lambda e: e.dma_start(
        out=consts[:, :], in_=cpkf_h[:, 256:264]), queue="dL")
    CONSTS = [tcb1, tcb2, tcb3, tcf1, tcf2, tok_consts]

    tok = p.dma("sync", lambda e: e.dma_start(
        out=wob_f[0:1, :], in_=dap(wob_h, 0, [[1024, 1], [1, 1024]])),
        queue="dL")
    tok_wob = p.do("vector", lambda e: e.tensor_copy(
        out=wob_bf[0:1, :], in_=wob_f[0:1, :]), [tok], inc="V")
    tok_lnw = p.dma("sync", lambda e: e.dma_start(
        out=lnw_raw[:, :], in_=dap(lnw_h, 0, [[128, 32], [1, 128]])),
        queue="dL")
    tok_lnb = p.dma("sync", lambda e: e.dma_start(
        out=lnb_raw[:, :], in_=dap(lnb_h, 0, [[128, 32], [1, 128]])),
        queue="dL")
    tk = mm(sap(S0, 0, 128, [[1, 32]]), lnw_raw[0:32, 0:128],
            ident_f32[0:32, 0:32], True, True,
            produce(S0, [tok_lnw] + CONSTS))
    set_prod(S0, tk)
    tk2 = p.do("vector", lambda e: e.tensor_scalar(
        out=lnw_sb[:, :], in0=sap(S0, 0, 128, [[1, 32]]), scalar1=1.0,
        scalar2=None, op0=ALU.mult), consume(S0), inc="V")
    add_cons(S0, tk2)
    tk = mm(sap(S1, 0, 128, [[1, 32]]), lnb_raw[0:32, 0:128],
            ident_f32[0:32, 0:32], True, True,
            produce(S1, [tok_lnb] + CONSTS))
    set_prod(S1, tk)
    tk2 = p.do("vector", lambda e: e.tensor_scalar(
        out=lnb_half_bf[:, :], in0=sap(S1, 0, 128, [[1, 32]]), scalar1=1.0,
        scalar2=None, op0=ALU.mult), consume(S1), inc="V")
    add_cons(S1, tk2)

    wst_cons = [tok_wob, None]
    w_cast_tok = {}
    for wi, wh in enumerate((wq_h, wk_h, wv_h)):
        toks = []
        for dc in range(NDC):
            sl = (wi * NDC + dc) % 2
            tok = p.dma("sync", lambda e, wh=wh, dc=dc, sl=sl: e.dma_start(
                out=wstage[:, sl * D:(sl + 1) * D],
                in_=wh[dc * 128:(dc + 1) * 128, :]),
                [wst_cons[sl]], queue="dL")
            ct = p.do("vector", lambda e, wi=wi, dc=dc, sl=sl: e.tensor_copy(
                out=wslot[:, (wi * NDC + dc) * D:(wi * NDC + dc + 1) * D],
                in_=wstage[:, sl * D:(sl + 1) * D]), [tok], inc="V")
            wst_cons[sl] = ct
            toks.append(ct)
        w_cast_tok[wi] = toks

    xstage_cons = [None, None]
    tab_cons = [None, None]
    QT_toks, KT_toks = [], []
    vhbm_toks = []
    pb_rot = [0]
    lastQmm = [None]
    prev_qkv_mm = [None]

    def emit_transpose8(tsrc, col0, waits):
        t = pb[pb_rot[0] % 2]
        pb_rot[0] += 1
        w = produce(t, list(waits) + CONSTS)
        tk = None
        for j in range(8):
            tk = p.do("tensor",
                      lambda e, t=t, tsrc=tsrc, col0=col0, j=j: e.transpose(
                          sap(t, 0, 128, [[1, 128]], j * 128),
                          tsrc[:, col0 + j * 128: col0 + (j + 1) * 128],
                          ident_bf[:, :]),
                      w if j == 0 else (), inc="T")
        set_prod(t, tk)
        return t

    cur_tab = [tabC, tabS]

    def rope6(ps_half, half, dst, dst_tok_list, waits):
        pt_, cb_ = ps_half
        ev = sap(pt_, 0, 128, [[64, 8], [1, 32]], cb_ + 0)
        od = sap(pt_, 0, 128, [[64, 8], [1, 32]], cb_ + 32)
        tC = sap(cur_tab[0], 0, 128, [[32, 8], [1, 32]], half * 256)
        tS = sap(cur_tab[1], 0, 128, [[32, 8], [1, 32]], half * 256)
        sAap = sap(rscr, 0, 128, [[32, 8], [1, 32]], 0)
        sBap = sap(rscr, 0, 128, [[32, 8], [1, 32]], 256)
        dev = sap(tmps, 0, 128, [[64, 8], [1, 32]], dst + half * 512)
        dod = sap(tmps, 0, 128, [[64, 8], [1, 32]], dst + half * 512 + 32)
        p.do("vector", lambda e: e.tensor_tensor(out=sAap, in0=ev, in1=tC,
                                                 op=ALU.mult), waits, inc="V")
        p.do("vector", lambda e: e.tensor_tensor(out=sBap, in0=od, in1=tS,
                                                 op=ALU.mult), (), inc="V")
        p.do("vector", lambda e: e.tensor_tensor(out=dev, in0=sAap, in1=sBap,
                                                 op=ALU.subtract), (), inc="V")
        p.do("vector", lambda e: e.tensor_tensor(out=sAap, in0=ev, in1=tS,
                                                 op=ALU.mult), (), inc="V")
        p.do("vector", lambda e: e.tensor_tensor(out=sBap, in0=od, in1=tC,
                                                 op=ALU.mult), (), inc="V")
        tk = p.do("vector", lambda e: e.tensor_tensor(out=dod, in0=sAap, in1=sBap,
                                                      op=ALU.add), (), inc="V")
        dst_tok_list.append(tk)
        return tk

    XBF, XT, QROT, KROT = 0, D, 2 * D, 3 * D

    NW = float(FF * D)
    gslot = [gsta, gstb]
    gcons = [None, None]
    gq = [0]
    gred_toks = []

    def emit_gamma_piece(load_ap, col):
        i = gq[0] % 2
        gq[0] += 1
        g = gslot[i]
        tok = p.dma("scalar", lambda e, g=g, load_ap=load_ap: e.dma_start(
            out=g[:, :], in_=load_ap), [gcons[i]], queue="dW", depth=2)
        tk = p.do("scalar", lambda e, g=g, col=col: e.activation(
            out=attT[:, 0:D], in_=g[:, :], func=AF.Abs,
            accum_out=gp[:, col:col + 1]), [tok], inc="S")
        gcons[i] = tk
        gred_toks.append(tk)

    gamma_pieces = []
    for dc in range(NDC):
        for q in range(4):
            gamma_pieces.append((dap(wfcT_h, dc * 128 * FF + q * 1024,
                                     [[FF, 128], [1, 1024]]), dc * 4 + q))
    for hcq in range(NHC):
        gamma_pieces.append((wprojT_h[hcq * 128:(hcq + 1) * 128, :], 32 + hcq))
    gper = -(-len(gamma_pieces) // n_tt)

    for tt in range(n_tt):
        sl = tt % 2
        tok_x = p.dma("sync", lambda e, tt=tt, sl=sl: e.dma_start(
            out=stage_x[:, sl * D:(sl + 1) * D],
            in_=xtt_h[tt * 128:(tt + 1) * 128, :]), [xstage_cons[sl]],
            queue="dL")
        tbC = tabC if tt % 2 == 0 else tabC2
        tbS = tabS if tt % 2 == 0 else tabS2
        tok_tc = p.dma("sync", lambda e, tt=tt, tbC=tbC: e.dma_start(
            out=tbC[:, :], in_=ctab_h[:, tt * 512:(tt + 1) * 512]),
            [tab_cons[tt % 2]], queue="dL")
        tok_ts = p.dma("sync", lambda e, tt=tt, tbS=tbS: e.dma_start(
            out=tbS[:, :], in_=stab_h[:, tt * 512:(tt + 1) * 512]),
            [tab_cons[tt % 2]], queue="dL")
        tok_xT = p.do("scalar", lambda e, sl=sl: e.copy(
            out=tmps[:, XT:XT + D], in_=stage_x[:, sl * D:(sl + 1) * D]),
            [tok_x, prev_qkv_mm[0]], inc="S")
        xstage_cons[sl] = tok_xT
        ps_pairs = {0: ((D0, 0), (D0, 512)), 1: ((D1, 0), (D1, 512)),
                    2: ((S0, 0), (S1, 0))}
        for wi in range(3):
            (tA, cA), (tB, cB) = ps_pairs[wi]
            wA = produce(tA, [tok_xT] + w_cast_tok[wi])
            wB = produce(tB, [tok_xT] + w_cast_tok[wi]) if tB is not tA else []
            tkA = tkB = None
            for dc in range(NDC):
                tkA = mm(tA[:, cA:cA + 512],
                         tmps[:, XT + dc * 128:XT + (dc + 1) * 128],
                         wslot[:, (wi * NDC + dc) * D:(wi * NDC + dc) * D + 512],
                         dc == 0, dc == NDC - 1, wA if dc == 0 else ())
                tkB = mm(tB[:, cB:cB + 512],
                         tmps[:, XT + dc * 128:XT + (dc + 1) * 128],
                         wslot[:, (wi * NDC + dc) * D + 512:(wi * NDC + dc + 1) * D],
                         dc == 0, dc == NDC - 1, wB if dc == 0 else ())
            set_prod(tA, tkA)
            set_prod(tB, tkB)
            prev_qkv_mm[0] = tkB
            if wi == 0:
                lastQmm[0] = tkB
        rope_toks = []
        cur_tab[0], cur_tab[1] = tbC, tbS
        for wi, dst in ((0, QROT), (1, KROT)):
            hA, hB = ps_pairs[wi]
            t1 = rope6(hA, 0, dst, rope_toks, consume(hA[0], [tok_tc, tok_ts]))
            add_cons(hA[0], t1)
            t2 = rope6(hB, 1, dst, rope_toks, consume(hB[0]))
            add_cons(hB[0], t2)
        tab_cons[tt % 2] = rope_toks[-1]
        for dst, res, toks in ((QROT, QT, QT_toks), (KROT, KT, KT_toks)):
            tb = emit_transpose8(tmps, dst, [rope_toks[-1]])
            tk = p.do("vector", lambda e, tb=tb, res=res, tt=tt: e.tensor_copy(
                out=sap(res, 0, 128, [[T, 8], [1, 128]], tt * 128),
                in_=sap(tb, 0, 128, [[1, 1024]])), consume(tb), inc="V")
            add_cons(tb, tk)
            toks.append(tk)
        tkv = p.do("vector", lambda e: e.tensor_copy(
            out=vstage[:, 0:512], in_=S0[:, :]),
            consume(S0, [vhbm_toks[-1] if vhbm_toks else None]), inc="V")
        add_cons(S0, tkv)
        tkv2 = p.do("vector", lambda e: e.tensor_copy(
            out=vstage[:, 512:1024], in_=S1[:, :]), consume(S1), inc="V")
        add_cons(S1, tkv2)
        tkst = p.dma("gpsimd", lambda e, tt=tt: e.dma_start(
            out=vhbm[tt * 128:(tt + 1) * 128, :], in_=vstage[:, :]),
            [tkv, tkv2], queue="dG", depth=2)
        vhbm_toks.append(tkst)
        for piece in gamma_pieces[tt * gper:(tt + 1) * gper]:
            emit_gamma_piece(*piece)

    tok_all_vhbm = vhbm_toks[-1]
    t_lastQKV = prev_qkv_mm[0]
    t_lastQmm = lastQmm[0]
    if debug:
        p.dma("gpsimd", lambda e: e.dma_start(out=dbg_qt[:, :], in_=QT[:, :]),
             QT_toks, queue="dG", depth=2)
        p.dma("gpsimd", lambda e: e.dma_start(out=dbg_kt[:, :], in_=KT[:, :]),
             KT_toks, queue="dG", depth=2)

    wo_toks = []
    for dc in range(NDC):
        sl = dc % 2
        tok = p.dma("sync", lambda e, dc=dc, sl=sl: e.dma_start(
            out=wstage[:, sl * D:(sl + 1) * D],
            in_=wo_h[dc * 128:(dc + 1) * 128, :]),
            [wst_cons[sl], t_lastQmm], queue="dL")
        ct = p.do("vector", lambda e, dc=dc, sl=sl: e.tensor_copy(
            out=wslot[:, dc * D:(dc + 1) * D],
            in_=wstage[:, sl * D:(sl + 1) * D]), [tok, t_lastQmm], inc="V")
        wst_cons[sl] = ct
        wo_toks.append(ct)

    t1 = p.do("vector", lambda e: e.reduce_sum(
        gp[:, 64:65], gp[:, 0:32], axis=AX.X), [gred_toks[31]], inc="V")
    t2 = p.do("vector", lambda e: e.reduce_sum(
        gp[:, 65:66], gp[:, 32:64], axis=AX.X), [gred_toks[-1]], inc="V")
    tk = mm(sap(S0, 0, 2, [[1, 128]]), gp[:, 64:66], ident_f32[:, :],
            True, True, produce(S0, [t1, t2]))
    set_prod(S0, tk)
    tg = p.do("vector", lambda e: e.tensor_reduce(
        gp[0:2, 66:67], sap(S0, 0, 2, [[1, 128]]), axis=AX.X, op=ALU.add),
        consume(S0), inc="V")
    add_cons(S0, tg)
    tk2 = mm(sap(S1, 0, 1, [[1, 2]]), gp[0:2, 66:67], ident_f32[0:2, 0:2],
             True, True, produce(S1, [tg]))
    set_prod(S1, tk2)
    tg2 = p.do("vector", lambda e: e.tensor_scalar(
        out=gp[0:1, 68:70], in0=sap(S1, 0, 1, [[1, 2]]), scalar1=0.5 / NW,
        scalar2=5e-7, op0=ALU.mult, op1=ALU.add), consume(S1), inc="V")
    add_cons(S1, tg2)
    tk = None
    for j in range(2):
        tk = mm(sap(S1, 0, 128, [[1, 1]], j), ones_f32[0:1, 0:128],
                gp[0:1, 68 + j:69 + j], j == 0, j == 1,
                produce(S1, [tg2]) if j == 0 else [tg2])
    set_prod(S1, tk)
    tb_ = p.do("vector", lambda e: e.tensor_copy(
        out=thr[:, 0:2], in_=sap(S1, 0, 128, [[1, 2]])),
        consume(S1), inc="V")
    add_cons(S1, tb_)
    tbn = p.do("vector", lambda e: e.tensor_scalar(
        out=thr[:, 2:4], in0=sap(S1, 0, 128, [[1, 2]]), scalar1=-1.0,
        scalar2=None, op0=ALU.mult), consume(S1), inc="V")
    add_cons(S1, tbn)
    TH = [tb_, tbn]

    wfc2_toks = []

    def emit_wfcq(dc, q, gate):
        i = gq[0] % 2
        gq[0] += 1
        g = gslot[i]
        tok = p.dma("scalar", lambda e, g=g, dc=dc, q=q: e.dma_start(
            out=g[:, :], in_=dap(wfcT_h, dc * 128 * FF + q * 1024,
                                 [[FF, 128], [1, 1024]])),
            [gcons[i]], queue="dW", depth=2)
        tm0 = p.do("vector", lambda e, g=g: e.tensor_scalar(
            out=sg0[:, 0:D], in0=g[:, :], scalar1=thr[:, 0:1], scalar2=None,
            op0=ALU.is_gt), [tok, t_lastQKV] + TH, inc="V")
        tm1 = p.do("vector", lambda e, g=g: e.tensor_scalar(
            out=sg1[:, 0:D], in0=g[:, :], scalar1=thr[:, 2:3], scalar2=None,
            op0=ALU.is_lt), (), inc="V")
        dst = wfc2_lo if dc < 4 else wfc2_hi
        dco = dc % 4
        tw = p.do("vector", lambda e, dst=dst, dco=dco, q=q: e.tensor_tensor(
            out=dst[:, dco * FF + q * D:dco * FF + (q + 1) * D],
            in0=sg0[:, 0:D], in1=sg1[:, 0:D], op=ALU.subtract),
            [tm0, tm1, gate], inc="V")
        gcons[i] = tw
        wfc2_toks.append(tw)

    wfcq_sched = {}
    for dc in range(NDC):
        for q in range(4):
            last_col = (dc % 4) * 4096 + q * 1024 + 1023
            gh = min(H - 1, 2 * (last_col // T) + 1)
            wfcq_sched.setdefault(gh, []).append((dc, q))

    vbuf_cons = [[], [], [], []]
    vbuf_tok = [None, None, None, None]
    PT_cons = [None] * 8
    PTS = lambda s: (PT if s < 4 else PT2, (s % 4) * 512)
    attT_toks = []
    rb_cons = [None, None]
    head_st_tok = {}
    lastST = [None]
    for hcp in range(H // 2):
        hA, hB = 2 * hcp, 2 * hcp + 1
        hc = hcp
        grp = max(1, n_tt // 4)
        slA, slB = 2 * (hcp % 2), 2 * (hcp % 2) + 1
        for sl_, hh in ((slA, hA), (slB, hB)):
            tok = p.dma("sync", lambda e, hh=hh, sl_=sl_: e.dma_start(
                out=bass.AP(vbuf, sl_ * 1024,
                            [[4096, 128], [64, n_tt], [1, 64]]),
                in_=bass.AP(vhbm, hh * 64,
                            [[D, 128], [128 * D, n_tt], [1, 64]])),
                [tok_all_vhbm] + vbuf_cons[sl_], queue="dL")
            vbuf_cons[sl_] = []
            vbuf_tok[sl_] = [tok]
        for qg in range(NQG):
            nk = 4 * (qg + 1)
            pend = []
            pv_tok = {hA: None, hB: None}
            first_pv = {(hA, 'main'): True, (hA, 'ones'): True,
                        (hB, 'main'): True, (hB, 'ones'): True}

            def pv_half(head, kt, tkE, pts, c0, kind):
                hp = (head % 2) * 64
                sl_ = slA if head == hA else slB
                ppv = S0 if head == hA else S1
                ptt, ptc = PTS(pts)
                vap = vbuf[:, sl_ * 1024 + kt * 64:sl_ * 1024 + (kt + 1) * 64]
                start = first_pv[(head, kind)]
                first_pv[(head, kind)] = False
                stop = kt == nk - 1
                w = produce(ppv, [tkE] + vbuf_tok[sl_]) if start else [tkE]
                if kind == "ones":
                    tok = p.do("tensor",
                               lambda e, ppv=ppv, hp=hp, ptt=ptt, ptc=ptc,
                               c0=c0, start=start, stop=stop:
                               e.matmul(ppv[64 - hp:128 - hp, c0:512],
                                        ones_bf[:, 0:64],
                                        ptt[:, ptc + c0:ptc + 512],
                                        start=start, stop=stop,
                                        skip_group_check=True,
                                        tile_position=(0, 64 - hp)),
                               w, inc="T")
                else:
                    tok = p.do("tensor",
                               lambda e, ppv=ppv, hp=hp, ptt=ptt, ptc=ptc,
                               c0=c0, vap=vap, start=start, stop=stop:
                               e.matmul(ppv[hp:hp + 64, c0:512], vap,
                                        ptt[:, ptc + c0:ptc + 512],
                                        start=start, stop=stop,
                                        skip_group_check=True,
                                        tile_position=(0, hp)),
                               w, inc="T")
                PT_cons[pts] = tok
                return tok

            def flush_pend():
                pend.sort(key=lambda it: it[1])
                while pend:
                    grp = [pend.pop(0)]
                    while pend and pend[0][1] == grp[0][1]:
                        grp.append(pend.pop(0))
                    for kind in ("main", "ones"):
                        for (h_, kt_, tkE_, pts_, c0_) in grp:
                            pv_tok[h_] = pv_half(h_, kt_, tkE_, pts_, c0_,
                                                 kind)

            for kp in range(0, nk, 2):
                sbase = (kp % 4 // 2) * 4
                stoks = {}
                c0ss = {}
                for kt in (kp, kp + 1):
                    for head, dd in ((hA, D0), (hB, D1)):
                        hp = (head % 2) * 64
                        c0 = max(0, kt * 128 - qg * 512)
                        cb = (kt - kp) * 512
                        ncols = 512 - c0
                        diag = kt >= 4 * qg
                        w = produce(dd, QT_toks + KT_toks + CONSTS) \
                            if kt == kp else ()
                        tkS = mm(dd[:, cb + c0:cb + 512],
                                 KT[hp:hp + 64,
                                    hc * T + kt * 128:hc * T + (kt + 1) * 128],
                                 sap(QT, hp, 64, [[1, ncols]],
                                     hc * T + qg * 512 + c0),
                                 True, not diag, w)
                        if diag:
                            tkS = mm(dd[:, cb + c0:cb + c0 + 128],
                                     ident_bf[:, :], mask_bf[:, :],
                                     False, True, ())
                        stoks[head] = tkS
                        c0ss.setdefault(head, []).append(c0)
                if pend:
                    flush_pend()
                for head, dd, soff in ((hA, D0, 0), (hB, D1, 2)):
                    hp = (head % 2) * 64
                    tkS = stoks[head]
                    c0s = c0ss[head]
                    set_prod(dd, tkS)
                    lastST[0] = tkS
                    head_st_tok[head] = tkS
                    pts = sbase + soff
                    ptt, ptc = PTS(pts)
                    if c0s[0] == 0 and c0s[1] == 0:
                        tkE = p.do("scalar",
                                   lambda e, dd=dd, ptt=ptt, ptc=ptc:
                                   e.activation(
                                       out=ptt[:, ptc:ptc + 1024],
                                       in_=dd[:, :], func=AF.Exp,
                                       bias=consts[:, 0:1], scale=0.125),
                                   consume(dd, [PT_cons[pts],
                                                PT_cons[pts + 1]] + CONSTS),
                                   inc="S")
                    else:
                        for j in range(2):
                            c0 = c0s[j]
                            tkE = p.do("scalar",
                                       lambda e, dd=dd, ptt=ptt, ptc=ptc,
                                       j=j, c0=c0:
                                       e.activation(
                                           out=ptt[:, ptc + j * 512 + c0:
                                                   ptc + (j + 1) * 512],
                                           in_=dd[:, j * 512 + c0:
                                                  (j + 1) * 512],
                                           func=AF.Exp, bias=consts[:, 0:1],
                                           scale=0.125),
                                       consume(dd, [PT_cons[pts + j]]
                                               + CONSTS), inc="S")
                    add_cons(dd, tkE)
                    pend.append((head, kp, tkE, pts, c0s[0]))
                    pend.append((head, kp + 1, tkE, pts + 1, c0s[1]))
            flush_pend()
            for head in (hA, hB):
                hp = (head % 2) * 64
                ppv = S0 if head == hA else S1
                sl_ = slA if head == hA else slB
                set_prod(ppv, pv_tok[head])
                vbuf_cons[sl_].append(pv_tok[head])
                sums_base = 64 - hp
                tkL = p.do("scalar", lambda e, ppv=ppv, sb=sums_base, hp=hp:
                           e.activation(out=rbA[hp:hp + 64, :],
                                        in_=ppv[sb:sb + 64, :], func=AF.Ln),
                           consume(ppv), inc="S")
                tkX = p.do("scalar", lambda e, hp=hp: e.activation(
                    out=rbB[hp:hp + 64, :], in_=rbA[hp:hp + 64, :],
                    func=AF.Exp, scale=-1.0),
                    [tkL, rb_cons[head % 2]], inc="S")
                tkA = p.do("vector", lambda e, ppv=ppv, hp=hp, hc=hc, qg=qg:
                           e.tensor_tensor(
                               out=attT[hp:hp + 64,
                                        hc * T + qg * 512:
                                        hc * T + (qg + 1) * 512],
                               in0=ppv[hp:hp + 64, :], in1=rbB[hp:hp + 64, :],
                               op=ALU.mult), [tkX], inc="V")
                add_cons(ppv, tkA)
                rb_cons[head % 2] = tkA
                attT_toks.append(tkA)
        for hh in (hA, hB):
            for (dcq, qq) in wfcq_sched.get(hh, []):
                emit_wfcq(dcq, qq, head_st_tok[hh])

    t_lastST = lastST[0]
    if debug:
        tda = p.dma("gpsimd", lambda e: e.dma_start(
            out=dbg_att[:, :], in_=attT[:, :]), attT_toks, queue="dG", depth=2)
        p.dma("gpsimd", lambda e: e.dma_start(out=dbg_rb[:, :], in_=rbB[:, :]),
             attT_toks, queue="dG", depth=2)
        attT_toks = attT_toks + [tda]

    xrel_cons = list(xstage_cons)
    x1st_cons = [tok_all_vhbm, tok_all_vhbm]
    r1_cons = [None, None]
    x1_toks = []
    lastWO = [None]
    for ti in range(n_tt):
        sl = ti % 2
        tok_x = p.dma("sync", lambda e, ti=ti, sl=sl: e.dma_start(
            out=stage_x[:, sl * D:(sl + 1) * D],
            in_=x_h[ti * 128:(ti + 1) * 128, :]), [xrel_cons[sl]],
            queue="dL")
        DW = D0 if ti % 2 == 0 else D1
        wA = produce(DW, attT_toks + wo_toks)
        for fc in range(NDC):
            la = attT[:, fc * T + ti * 128:fc * T + (ti + 1) * 128]
            mm(DW[:, 0:512], la, wslot[:, fc * D:fc * D + 512],
               fc == 0, False, wA if fc == 0 else ())
            mm(DW[:, 512:1024], la, wslot[:, fc * D + 512:(fc + 1) * D],
               fc == 0, False, ())
        mm(DW[:, 0:512], ones_bf[0:1, 0:128], wob_bf[0:1, 0:512], False, True,
           [tok_wob])
        tkB = mm(DW[:, 512:1024], ones_bf[0:1, 0:128], wob_bf[0:1, 512:1024],
                 False, True, ())
        set_prod(DW, tkB)
        lastWO[0] = tkB
        tr1a = p.do("vector", lambda e, sl=sl, DW=DW: e.tensor_add(
            out=r1buf[:, sl * D:sl * D + 512], in0=DW[:, 0:512],
            in1=stage_x[:, sl * D:sl * D + 512]),
            consume(DW, [tok_x, r1_cons[sl]]), inc="V")
        tr1b = p.do("vector", lambda e, sl=sl, DW=DW: e.tensor_add(
            out=r1buf[:, sl * D + 512:(sl + 1) * D], in0=DW[:, 512:1024],
            in1=stage_x[:, sl * D + 512:(sl + 1) * D]), (), inc="V")
        add_cons(DW, tr1b)
        xrel_cons[sl] = tr1b
        r1 = r1buf[:, sl * D:(sl + 1) * D]
        pc = 8 * (ti % 2)
        s1, s2 = sm[:, pc:pc + 1], sm[:, pc + 1:pc + 2]
        mu, m2 = sm[:, pc + 2:pc + 3], sm[:, pc + 3:pc + 4]
        var, rstd = sm[:, pc + 4:pc + 5], sm[:, pc + 5:pc + 6]
        nb = sm[:, pc + 6:pc + 7]
        ts1 = p.do("vector", lambda e, r1=r1, s1=s1: e.reduce_sum(
            s1, r1, axis=AX.X), [tr1a, tr1b], inc="V")
        ts2 = p.do("scalar", lambda e, r1=r1, s2=s2: e.activation(
            out=sqscr[:, 0:D], in_=r1, func=AF.Square, accum_out=s2),
            [tr1a, tr1b], inc="S")
        p.do("vector", lambda e, s2=s2, m2=m2: e.tensor_scalar(
            out=m2, in0=s2, scalar1=float(D), scalar2=None, op0=ALU.mult),
            [ts1, ts2], inc="V")
        tv = p.do("vector", lambda e, s1=s1, m2=m2, var=var:
                  e.scalar_tensor_tensor(
                      out=var, in0=s1, scalar=s1, in1=m2,
                      op0=ALU.mult, op1=ALU.subtract), (), inc="V")
        tl = p.do("scalar", lambda e, var=var, mu=mu: e.activation(
            out=mu, in_=var, func=AF.Ln, bias=consts[:, 1:2],
            scale=-1.0 / (D * D)), [tv] + CONSTS, inc="S")
        tl2 = p.do("scalar", lambda e, mu=mu, rstd=rstd: e.activation(
            out=rstd, in_=mu, func=AF.Exp, scale=-0.5), [tl], inc="S")
        tnb = p.do("vector", lambda e, s1=s1, nb=nb, rstd=rstd: e.tensor_scalar(
            out=nb, in0=s1, scalar1=rstd, scalar2=-1.0 / D, op0=ALU.mult,
            op1=ALU.mult), [tl2], inc="V")
        x1st = x1stage0 if sl == 0 else x1stage1
        tap = p.do("scalar", lambda e, r1=r1, rstd=rstd, nb=nb, x1st=x1st:
                   e.activation(
                       out=x1st[:, :], in_=r1,
                       func=AF.Identity, scale=rstd, bias=nb),
                   [tnb, x1st_cons[sl]], inc="S")
        r1_cons[sl] = tap
        tst = p.dma("gpsimd", lambda e, ti=ti, x1st=x1st: e.dma_start(
            out=x1hbm[ti * 128:(ti + 1) * 128, :],
            in_=x1st[:, :]), [tap], queue="dG", depth=2)
        x1st_cons[sl] = tst
        x1_toks.append(tst)
    t_lastWO = lastWO[0]
    t_x1_done = x1_toks[-1]
    if debug:
        p.dma("gpsimd", lambda e: e.dma_start(out=dbg_sm[:, :], in_=sm[:, :]),
             [t_x1_done], queue="dG", depth=2)
        p.dma("gpsimd", lambda e: e.dma_start(out=dbg_r1[:, :], in_=r1buf[:, :]),
             [t_x1_done], queue="dG", depth=2)

    wps_toks = []
    ctk = None
    tkc = None
    for hc in range(NHC):
        i = gq[0] % 2
        gq[0] += 1
        g = gslot[i]
        tok = p.dma("scalar", lambda e, g=g, hc=hc: e.dma_start(
            out=g[:, :], in_=wprojT_h[hc * 128:(hc + 1) * 128, :]),
            [gcons[i]], queue="dW", depth=2)
        tm0 = p.do("vector", lambda e, g=g: e.tensor_scalar(
            out=sg0[:, 0:D], in0=g[:, :], scalar1=thr[:, 1:2], scalar2=None,
            op0=ALU.is_gt), [tok] + TH, inc="V")
        tm1 = p.do("vector", lambda e, g=g: e.tensor_scalar(
            out=sg1[:, 0:D], in0=g[:, :], scalar1=thr[:, 3:4], scalar2=None,
            op0=ALU.is_lt), (), inc="V")
        dst = wps_lo if hc < 16 else wps_hi
        hco = hc % 16
        gate = t_lastWO if hc < 16 else t_x1_done
        tw = p.do("vector", lambda e, dst=dst, hco=hco: e.tensor_tensor(
            out=dst[:, hco * D:(hco + 1) * D], in0=sg0[:, 0:D],
            in1=sg1[:, 0:D], op=ALU.subtract), [tm0, tm1, gate], inc="V")
        gcons[i] = tw
        for half, ct_ps in ((0, S0), (1, S1)):
            w = produce(ct_ps, [tw]) if hc == 0 else [tw]
            ctk = mm(sap(ct_ps, 0, 1, [[1, 512]]),
                     lnb_half_bf[:, hc:hc + 1],
                     dst[:, hco * D + half * 512:hco * D + (half + 1) * 512],
                     hc == 0, hc == NHC - 1, w)
        tsc = p.do("vector", lambda e, dst=dst, hco=hco, hc=hc: e.tensor_scalar(
            out=dst[:, hco * D:(hco + 1) * D],
            in0=dst[:, hco * D:(hco + 1) * D],
            scalar1=lnw_sb[:, hc:hc + 1], scalar2=None, op0=ALU.mult),
            [ctk], inc="V")
        wps_toks.append(tsc)
    set_prod(S0, ctk)
    set_prod(S1, ctk)
    tkc = None
    for half, ct_ps in ((0, S0), (1, S1)):
        tkc = p.do("scalar", lambda e, half=half, ct_ps=ct_ps: e.activation(
            out=c_bf[0:1, half * 512:(half + 1) * 512],
            in_=sap(ct_ps, 0, 1, [[1, 512]]), func=AF.Identity),
            consume(ct_ps), inc="S")
        add_cons(ct_ps, tkc)
    t_quant = [wfc2_toks[-1], wps_toks[-1], tkc]
    if debug:
        p.dma("gpsimd", lambda e: e.dma_start(out=dbg_thr[:, :], in_=thr[:, :]),
             TH, queue="dG", depth=2)
        p.dma("gpsimd", lambda e: e.dma_start(
            out=dap(dbg_wfc2, 0, [[8 * FF, 128], [1, 4 * FF]]),
            in_=wfc2_lo[:, :]), wfc2_toks, queue="dG", depth=2)
        p.dma("gpsimd", lambda e: e.dma_start(
            out=dap(dbg_wfc2, 4 * FF, [[8 * FF, 128], [1, 4 * FF]]),
            in_=wfc2_hi[:, :]), wfc2_toks, queue="dG", depth=2)
        p.dma("gpsimd", lambda e: e.dma_start(
            out=dap(dbg_wps, 0, [[32 * D, 128], [1, 16 * D]]),
            in_=wps_lo[:, :]), wps_toks, queue="dG", depth=2)
        p.dma("gpsimd", lambda e: e.dma_start(
            out=dap(dbg_wps, 16 * D, [[32 * D, 128], [1, 16 * D]]),
            in_=wps_hi[:, :]), wps_toks, queue="dG", depth=2)
        tcb = p.dma("gpsimd", lambda e: e.dma_start(
            out=dbg_cbf[0:1, :], in_=c_bf[0:1, :]), [tkc], queue="dG", depth=2)
        t_quant = t_quant + [tcb]

    x1t_cons = [None, None, None, None]
    x1T_mm_cons = [None, None]
    prev_proj_mm = [None]
    prev_h2_transp = [None, None]
    out_cons = [None, None]
    out_toks = []
    state = {}

    x1dma_toks = {}

    def p7_dma(tt):
        sl4 = tt % 4
        x1dma_toks[tt] = p.dma("sync", lambda e, tt=tt, sl4=sl4: e.dma_start(
            out=x1tile[:, sl4 * D:(sl4 + 1) * D],
            in_=x1hbm[tt * 128:(tt + 1) * 128, :]),
            [x1_toks[tt], x1t_cons[sl4]], queue="dL")

    def p7_stage_a1(tt):
        sl = tt % 2
        sl4 = tt % 4
        tok_x1 = x1dma_toks.pop(tt)
        tb = emit_transpose8(x1tile, sl4 * D, [tok_x1])
        tok_x1T = p.do("vector", lambda e, tb=tb, sl=sl: e.tensor_copy(
            out=x1T[:, sl * D:(sl + 1) * D], in_=sap(tb, 0, 128, [[1, 1024]])),
            consume(tb, [x1T_mm_cons[sl]]), inc="V")
        add_cons(tb, tok_x1T)
        return tok_x1T

    def p7_stage_a(tt, tok_x1T):
        sl = tt % 2
        st = {}
        gelu_toks = []
        pc = 8 * (tt % 2)
        for hb in range(0, NHB, 2):
            dd = [D0, D1][(hb // 2) % 2]
            w = produce(dd, [tok_x1T] + wfc2_toks)
            tkF = None
            for dc in range(NDC):
                wsrc = wfc2_lo if dc < 4 else wfc2_hi
                dco = dc % 4
                tkF = mm(dd[:, 0:512],
                         x1T[:, sl * D + dc * 128:sl * D + (dc + 1) * 128],
                         wsrc[:, dco * FF + hb * 512:dco * FF + (hb + 1) * 512],
                         dc == 0, False, w if dc == 0 else ())
                tkF = mm(dd[:, 512:1024],
                         x1T[:, sl * D + dc * 128:sl * D + (dc + 1) * 128],
                         wsrc[:, dco * FF + (hb + 1) * 512:
                              dco * FF + (hb + 2) * 512],
                         dc == 0, dc == NDC - 1, ())
            set_prod(dd, tkF)
            x1T_mm_cons[sl] = tkF
            tkG = p.do("scalar", lambda e, dd=dd, hb=hb, pc=pc: e.activation(
                out=gbuf[:, hb * 512:(hb + 2) * 512], in_=dd[:, :],
                func=AF.Gelu, scale=1.0,
                accum_out=gs[:, pc + hb // 2:pc + hb // 2 + 1]),
                consume(dd), inc="S")
            add_cons(dd, tkG)
            tkQ = p.do("scalar", lambda e, hb=hb, pc=pc: e.activation(
                out=sqscr[:, hb * 512:(hb + 2) * 512],
                in_=gbuf[:, hb * 512:(hb + 2) * 512], func=AF.Square,
                accum_out=gs[:, pc + 4 + hb // 2:pc + 5 + hb // 2]),
                [tkG], inc="S")
            gelu_toks.append(tkQ)
        if tt + 1 < n_tt:
            a1_toks[tt + 1] = p7_stage_a1(tt + 1)
        pcs = 16 + 8 * (tt % 2)
        s1, s2 = sm[:, pcs:pcs + 1], sm[:, pcs + 1:pcs + 2]
        m2 = sm[:, pcs + 3:pcs + 4]
        var, rstd = sm[:, pcs + 4:pcs + 5], sm[:, pcs + 5:pcs + 6]
        mu, nb = sm[:, pcs + 2:pcs + 3], sm[:, pcs + 6:pcs + 7]
        ts1 = p.do("vector", lambda e, pc=pc, s1=s1: e.reduce_sum(
            s1, gs[:, pc:pc + 4], axis=AX.X), gelu_toks, inc="V")
        ts2 = p.do("vector", lambda e, pc=pc, s2=s2: e.reduce_sum(
            s2, gs[:, pc + 4:pc + 8], axis=AX.X), gelu_toks, inc="V")
        p.do("vector", lambda e, s2=s2, m2=m2: e.tensor_scalar(
            out=m2, in0=s2, scalar1=float(FF), scalar2=None, op0=ALU.mult),
            [ts1, ts2], inc="V")
        tv = p.do("vector", lambda e, s1=s1, m2=m2, var=var:
                  e.scalar_tensor_tensor(
                      out=var, in0=s1, scalar=s1, in1=m2,
                      op0=ALU.mult, op1=ALU.subtract), (), inc="V")
        tl = p.do("scalar", lambda e, var=var, mu=mu: e.activation(
            out=mu, in_=var, func=AF.Ln, bias=consts[:, 1:2],
            scale=-1.0 / (float(FF) * FF)), [tv], inc="S")
        tl2 = p.do("scalar", lambda e, mu=mu, rstd=rstd: e.activation(
            out=rstd, in_=mu, func=AF.Exp, scale=-0.5), [tl], inc="S")
        tnb = p.do("vector", lambda e, s1=s1, nb=nb, rstd=rstd: e.tensor_scalar(
            out=nb, in0=s1, scalar1=rstd, scalar2=-1.0 / FF, op0=ALU.mult,
            op1=ALU.mult), [tl2], inc="V")
        h2dst = h2t if sl == 0 else h2t2
        th2a = p.do("scalar", lambda e, rstd=rstd, nb=nb, h2dst=h2dst:
                    e.activation(
                        out=h2dst[:, 0:2048], in_=gbuf[:, 0:2048],
                        func=AF.Identity, scale=rstd, bias=nb),
                    [tnb, prev_h2_transp[sl]] + t_quant, inc="S")
        th2b = p.do("scalar", lambda e, rstd=rstd, nb=nb, h2dst=h2dst:
                    e.activation(
                        out=h2dst[:, 2048:4096], in_=gbuf[:, 2048:4096],
                        func=AF.Identity, scale=rstd, bias=nb),
                    (), inc="S")
        st["th2"] = (th2a, th2b)
        st["sl"] = sl
        return st

    def p7_stage_b(tt, st):
        sl = st["sl"]
        h2src = h2t if sl == 0 else h2t2
        h2T_toks = []
        for g4 in range(4):
            tbk = pb[pb_rot[0] % 2]
            pb_rot[0] += 1
            w = produce(tbk, [st["th2"][0 if g4 < 2 else 1]])
            tkT = None
            for j in range(8):
                cj = g4 * 8 + j
                tkT = p.do("tensor",
                           lambda e, tbk=tbk, cj=cj, j=j, h2src=h2src:
                           e.transpose(
                               sap(tbk, 0, 128, [[1, 128]], j * 128),
                               h2src[:, cj * 128:(cj + 1) * 128],
                               ident_bf[:, :]),
                           w if j == 0 else (), inc="T")
            set_prod(tbk, tkT)
            prev_h2_transp[sl] = tkT
            tke = p.do("vector", lambda e, tbk=tbk, g4=g4: e.tensor_copy(
                out=h2T_t[:, g4 * 1024:(g4 + 1) * 1024],
                in_=sap(tbk, 0, 128, [[1, 1024]])),
                consume(tbk, [prev_proj_mm[0]]), inc="V")
            add_cons(tbk, tke)
            h2T_toks.append(tke)
        wA = produce(S0, h2T_toks + wps_toks)
        wB = produce(S1, h2T_toks + wps_toks)
        for hc in range(NHC):
            wsrc = wps_lo if hc < 16 else wps_hi
            hco = hc % 16
            la = h2T_t[:, hc * 128:(hc + 1) * 128]
            mm(S0[:, :], la, wsrc[:, hco * D:hco * D + 512],
               hc == 0, False, wA if hc == 0 else ())
            mm(S1[:, :], la, wsrc[:, hco * D + 512:(hco + 1) * D],
               hc == 0, False, wB if hc == 0 else ())
        tkA = mm(S0[:, :], ones_bf[0:1, 0:128], c_bf[0:1, 0:512], False, True,
                 [tkc])
        tkB = mm(S1[:, :], ones_bf[0:1, 0:128], c_bf[0:1, 512:1024],
                 False, True, ())
        set_prod(S0, tkA)
        set_prod(S1, tkB)
        prev_proj_mm[0] = tkB
        sl4 = tt % 4
        tr2a = p.do("vector", lambda e, sl4=sl4: e.tensor_add(
            out=r2buf[:, 0:512], in0=S0[:, :],
            in1=x1tile[:, sl4 * D:sl4 * D + 512]),
            consume(S0, [out_cons[0]]), inc="V")
        add_cons(S0, tr2a)
        tr2b = p.do("vector", lambda e, sl4=sl4: e.tensor_add(
            out=r2buf[:, 512:1024], in0=S1[:, :],
            in1=x1tile[:, sl4 * D + 512:(sl4 + 1) * D]),
            consume(S1, [out_cons[0]]), inc="V")
        add_cons(S1, tr2b)
        x1t_cons[sl4] = tr2b
        pcs = 32 + 8 * (tt % 2)
        s1, s2 = sm[:, pcs:pcs + 1], sm[:, pcs + 1:pcs + 2]
        m2 = sm[:, pcs + 3:pcs + 4]
        var, rstd = sm[:, pcs + 4:pcs + 5], sm[:, pcs + 5:pcs + 6]
        mu, nb = sm[:, pcs + 2:pcs + 3], sm[:, pcs + 6:pcs + 7]
        ts1 = p.do("vector", lambda e, s1=s1: e.reduce_sum(
            s1, r2buf[:, :], axis=AX.X), [tr2a, tr2b], inc="V")
        ts2 = p.do("scalar", lambda e, s2=s2: e.activation(
            out=sqscr[:, 0:D], in_=r2buf[:, :], func=AF.Square, accum_out=s2),
            [tr2a, tr2b], inc="S")
        p.do("vector", lambda e, s2=s2, m2=m2: e.tensor_scalar(
            out=m2, in0=s2, scalar1=float(D), scalar2=None, op0=ALU.mult),
            [ts1, ts2], inc="V")
        tv = p.do("vector", lambda e, s1=s1, m2=m2, var=var:
                  e.scalar_tensor_tensor(
                      out=var, in0=s1, scalar=s1, in1=m2,
                      op0=ALU.mult, op1=ALU.subtract), (), inc="V")
        tl = p.do("scalar", lambda e, var=var, mu=mu: e.activation(
            out=mu, in_=var, func=AF.Ln, bias=consts[:, 1:2],
            scale=-1.0 / (D * D)), [tv], inc="S")
        tl2 = p.do("scalar", lambda e, mu=mu, rstd=rstd: e.activation(
            out=rstd, in_=mu, func=AF.Exp, scale=-0.5), [tl], inc="S")
        tnb = p.do("vector", lambda e, s1=s1, nb=nb, rstd=rstd: e.tensor_scalar(
            out=nb, in0=s1, scalar1=rstd, scalar2=-1.0 / D, op0=ALU.mult,
            op1=ALU.mult), [tl2], inc="V")
        tout = p.do("scalar", lambda e, rstd=rstd, nb=nb: e.activation(
            out=outbuf[:, :], in_=r2buf[:, :], func=AF.Identity, scale=rstd,
            bias=nb), [tnb, out_cons[1]], inc="S")
        tst = p.dma("gpsimd", lambda e, tt=tt: e.dma_start(
            out=out_h[tt * 128:(tt + 1) * 128, :], in_=outbuf[:, :]),
            [tout], queue="dO", depth=2)
        out_cons[0] = tst
        out_cons[1] = tst
        out_toks.append(tst)

    prev = None
    p7_dma(0)
    p7_dma(1)
    a1_toks = {0: p7_stage_a1(0)}
    for tt in range(n_tt):
        if tt + 2 < n_tt:
            p7_dma(tt + 2)
        cur = p7_stage_a(tt, a1_toks.pop(tt))
        if prev is not None:
            p7_stage_b(tt - 1, prev)
        prev = cur
    p7_stage_b(n_tt - 1, prev)

    p.wait_only("sync", out_toks[-2:])

    with nc.Block() as block:
        @block.sync
        def _(e):
            p.replay("sync", e)

        @block.tensor
        def _(e):
            p.replay("tensor", e)

        @block.vector
        def _(e):
            p.replay("vector", e)

        @block.scalar
        def _(e):
            p.replay("scalar", e)

        @block.gpsimd
        def _(e):
            p.replay("gpsimd", e)

    return nc


def make_xtt(x2d, n_tt=16):
    T = n_tt * 128
    return np.ascontiguousarray(
        x2d.reshape(n_tt, 128, NDC, 128).transpose(0, 3, 2, 1).reshape(T, D))


def host_prep(wq, wk, wv, wo, wob, wfc, lnw, lnb, wproj, n_tt=16):
    T = n_tt * 128
    perm = np.concatenate([np.concatenate([64 * h + np.arange(0, 64, 2),
                                           64 * h + np.arange(1, 64, 2)])
                           for h in range(H)])
    d = dict(
        wq=np.ascontiguousarray(np.asarray(wq, np.float32)[:, perm]),
        wk=np.ascontiguousarray(np.asarray(wk, np.float32)[:, perm]),
        wv=np.ascontiguousarray(np.asarray(wv, np.float32)),
        wo=np.ascontiguousarray(np.asarray(wo, np.float32)),
        wob=np.ascontiguousarray(np.asarray(wob, np.float32)),
        wfcT=np.ascontiguousarray(np.asarray(wfc, np.float32).T),
        lnw=np.ascontiguousarray(np.asarray(lnw, np.float32)),
        lnb=np.ascontiguousarray(np.asarray(lnb, np.float32)),
        wprojT=np.ascontiguousarray(np.asarray(wproj, np.float32).T),
    )
    import jax
    import jax.numpy as jnp
    with jax.default_device(jax.devices("cpu")[0]):
        invj = 1.0 / 10000.0 ** (jnp.arange(0, HD, 2)[:HD // 2].astype(
            jnp.float32) / HD)
        angj = jnp.outer(jnp.arange(T, dtype=jnp.float32), invj)
        cosv = np.asarray(jnp.cos(angj)).reshape(n_tt, 128, 32)
        sinv = np.asarray(jnp.sin(angj)).reshape(n_tt, 128, 32)
    for nm, v in (("ctab", cosv), ("stab", sinv)):
        tab = np.zeros((128, n_tt * 512), np.float32)
        for tt in range(n_tt):
            tab[:, tt * 512:(tt + 1) * 512] = np.tile(v[tt], (1, H))
        d[nm] = tab
    eye = np.eye(128, dtype=np.float32)
    maskv = np.tril(np.full((128, 128), -30000.0, np.float32), -1)
    onesv = np.ones((128, 128), np.float32)
    d["cpkb"] = np.concatenate([eye, maskv, onesv], axis=1).astype(
        ml_dtypes.bfloat16)
    cvec = np.zeros((128, 8), np.float32)
    cvec[:, 0] = -5.0
    cvec[:, 1] = 1e-5
    d["cpkf"] = np.concatenate([eye, onesv, cvec], axis=1)
    return d


def kernel(L0_attn_x, L0_attn_wq, L0_attn_wk, L0_attn_wv, L0_attn_wo_w,
           L0_attn_wo_b, L0_ffn_W_fc, L0_ffn_ln_w, L0_ffn_ln_b, L0_ffn_W_proj):
    x = np.ascontiguousarray(np.asarray(L0_attn_x, np.float32))
    n_cores = x.shape[0]
    shared = host_prep(L0_attn_wq, L0_attn_wk, L0_attn_wv, L0_attn_wo_w,
                       L0_attn_wo_b, L0_ffn_W_fc, L0_ffn_ln_w, L0_ffn_ln_b,
                       L0_ffn_W_proj, n_tt=16)
    nc = build(n_tt=16)
    in_maps = [dict(shared, x=np.ascontiguousarray(x[b]),
                    xtt=make_xtt(x[b]))
               for b in range(n_cores)]
    res = run_bass_kernel_spmd(nc, in_maps, core_ids=list(range(n_cores)))
    return np.stack([r["out"] for r in res.results], axis=0)


# version 30
# speedup vs baseline: 1.0664x; 1.0031x over previous
import os

import ml_dtypes
import numpy as np
import concourse.bass as bass
import concourse.mybir as mybir
from concourse.bass_utils import run_bass_kernel_spmd

F32 = mybir.dt.float32
BF16 = mybir.dt.bfloat16
AF = mybir.ActivationFunctionType
ALU = mybir.AluOpType
AX = mybir.AxisListType

ENGINES = ("sync", "tensor", "vector", "scalar", "gpsimd")
ENG_SEM = {"tensor": "T", "vector": "V", "scalar": "S", "gpsimd": "G"}

D = 1024
H = 16
HD = 64
FF = 4096
NDC = D // 128
NHC = FF // 128
NHB = FF // 512


class Prog:
    def __init__(self, nc):
        self.nc = nc
        self.ops = {e: [] for e in ENGINES}
        self.sems = {}
        self.counts = {}
        self.qstate = {}
        self.hwm = {e: {} for e in ENGINES}
        self.strict = os.environ.get("BASS_SIM_FP64") == "1"

    def do(self, eng, fn, waits=(), inc=None, n=1, self_wait=False,
           hard=False):
        need = {}
        for w in waits:
            if w is None:
                continue
            s, v = w
            if v > 0 and v > need.get(s, 0):
                need[s] = v
        wl = []
        if self.strict and fn is not None and eng in ENG_SEM:
            s = ENG_SEM[eng]
            prev = self.counts.get(s, 0)
            if inc is None:
                inc = s
                if s not in self.sems:
                    self.sems[s] = self.nc.alloc_semaphore(s)
                    self.counts[s] = 0
            if prev > 0:
                wl.append((s, prev))
                if prev > self.hwm[eng].get(s, 0):
                    self.hwm[eng][s] = prev
        for s, v in need.items():
            if not hard and v <= self.hwm[eng].get(s, 0):
                continue
            if v > self.hwm[eng].get(s, 0):
                self.hwm[eng][s] = v
            wl.append((s, v))
        tok = None
        if inc is not None:
            if inc not in self.sems:
                self.sems[inc] = self.nc.alloc_semaphore(inc)
                self.counts[inc] = 0
            self.counts[inc] += n
            tok = (inc, self.counts[inc])
        self.ops[eng].append((fn, wl, inc, n))
        if self_wait and tok is not None:
            s, v = tok
            if v > self.hwm[eng].get(s, 0):
                self.hwm[eng][s] = v
                self.ops[eng].append((None, [(s, v)], None, 1))
        return tok

    def wait_only(self, eng, waits):
        self.do(eng, None, waits)

    def dma(self, eng, fn, waits=(), queue="dL", depth=4):
        st = self.qstate.setdefault((eng, queue), {"i": 0, "last": {}})
        i = st["i"]
        st["i"] += 1
        sem = "%s%d" % (queue, i % depth)
        w = list(waits)
        prev = st["last"].get(i % depth)
        if prev is not None:
            w.append(prev)
        tok = self.do(eng, fn, w, inc=sem, n=16)
        st["last"][i % depth] = tok
        return tok

    def replay(self, eng_name, e):
        for fn, wl, inc, n in self.ops[eng_name]:
            for s, v in wl:
                e.wait_ge(self.sems[s], v)
            if fn is None:
                continue
            inst = fn(e)
            if inc is not None:
                inst.then_inc(self.sems[inc], n)


def sap(t, p0, nparts, fdims, c0=0):
    w = t.shape[-1]
    return bass.AP(t, p0 * w + c0, [[w, nparts]] + [list(d) for d in fdims])


def dap(t, off, dims):
    return bass.AP(t, off, [list(d) for d in dims])


class Bank:
    __slots__ = ("prod", "cons")

    def __init__(self):
        self.prod = None
        self.cons = []


def build(n_tt=16, debug=False):
    T = n_tt * 128
    NQG = n_tt // 4
    nc = bass.Bass("TRN2", target_bir_lowering=False)

    x_h = nc.dram_tensor("x", [T, D], F32, kind="ExternalInput")
    xtt_h = nc.dram_tensor("xtt", [T, D], F32, kind="ExternalInput")
    wq_h = nc.dram_tensor("wq", [D, D], F32, kind="ExternalInput")
    wk_h = nc.dram_tensor("wk", [D, D], F32, kind="ExternalInput")
    wv_h = nc.dram_tensor("wv", [D, D], F32, kind="ExternalInput")
    wo_h = nc.dram_tensor("wo", [D, D], F32, kind="ExternalInput")
    wob_h = nc.dram_tensor("wob", [D], F32, kind="ExternalInput")
    wfcT_h = nc.dram_tensor("wfcT", [D, FF], F32, kind="ExternalInput")
    lnw_h = nc.dram_tensor("lnw", [FF], F32, kind="ExternalInput")
    lnb_h = nc.dram_tensor("lnb", [FF], F32, kind="ExternalInput")
    wprojT_h = nc.dram_tensor("wprojT", [FF, D], F32, kind="ExternalInput")
    ctab_h = nc.dram_tensor("ctab", [128, n_tt * 512], F32, kind="ExternalInput")
    stab_h = nc.dram_tensor("stab", [128, n_tt * 512], F32, kind="ExternalInput")
    cpkb_h = nc.dram_tensor("cpkb", [128, 384], BF16, kind="ExternalInput")
    cpkf_h = nc.dram_tensor("cpkf", [128, 264], F32, kind="ExternalInput")
    out_h = nc.dram_tensor("out", [T, D], F32, kind="ExternalOutput")
    iokind = "ExternalOutput" if debug else "Internal"
    vhbm = nc.dram_tensor("v_hbm", [T, D], BF16, kind=iokind)
    x1hbm = nc.dram_tensor("x1_hbm", [T, D], BF16, kind=iokind)
    if debug:
        dbg_qt = nc.dram_tensor("dbg_qt", [128, NDC * T], BF16, kind="ExternalOutput")
        dbg_kt = nc.dram_tensor("dbg_kt", [128, NDC * T], BF16, kind="ExternalOutput")
        dbg_att = nc.dram_tensor("dbg_att", [128, NDC * T], BF16, kind="ExternalOutput")
        dbg_thr = nc.dram_tensor("dbg_thr", [128, 4], F32, kind="ExternalOutput")
        dbg_wfc2 = nc.dram_tensor("dbg_wfc2", [128, 8 * FF], BF16, kind="ExternalOutput")
        dbg_wps = nc.dram_tensor("dbg_wps", [128, 32 * D], BF16, kind="ExternalOutput")
        dbg_cbf = nc.dram_tensor("dbg_cbf", [1, 1024], BF16, kind="ExternalOutput")
        dbg_rb = nc.dram_tensor("dbg_rb", [128, 512], F32, kind="ExternalOutput")
        dbg_sm = nc.dram_tensor("dbg_sm", [128, 64], F32, kind="ExternalOutput")
        dbg_r1 = nc.dram_tensor("dbg_r1", [128, 2048], F32, kind="ExternalOutput")

    AT = nc.alloc_sbuf_tensor_at
    ident_bf = AT("ident_bf", [128, 128], BF16, offset=17024)
    ident_f32 = AT("ident_f32", [128, 128], F32, offset=17280)
    mask_bf = AT("mask_bf", [128, 128], BF16, offset=17792)
    ones_bf = AT("ones_bf", [128, 128], BF16, offset=18048)
    ones_f32 = AT("ones_f32", [128, 128], F32, offset=18304)
    consts = AT("consts", [128, 8], F32, offset=18816)
    sm = AT("sm", [128, 64], F32, offset=18848)
    gp = AT("gp", [128, 72], F32, offset=19104)
    thr = AT("thr", [128, 4], F32, offset=19392)
    lnw_raw = AT("lnw_raw", [32, 128], F32, offset=19424)
    lnb_raw = AT("lnb_raw", [32, 128], F32, offset=19936)
    lnw_sb = AT("lnw_sb", [128, 32], F32, offset=20448)
    lnb_half_bf = AT("lnb_half_bf", [128, 32], BF16, offset=20576)
    gs = AT("gs", [128, 16], F32, offset=20640)
    wob_bf = AT("wob_bf", [1, 1024], BF16, offset=20704)
    wob_f = AT("wob_f", [1, 1024], F32, offset=217728)
    c_bf = AT("c_bf", [1, 1024], BF16, offset=22752)
    QT = AT("QT", [128, NDC * T], BF16, offset=25216)
    wfc2_lo = AT("wfc2_lo", [128, 4 * FF], BF16, offset=25216)
    KT = AT("KT", [128, NDC * T], BF16, offset=57984)
    wfc2_hi = AT("wfc2_hi", [128, 4 * FF], BF16, offset=57984)
    attT = AT("attT", [128, NDC * T], BF16, offset=90752)
    wps_lo = AT("wps_lo", [128, 16 * D], BF16, offset=90752)
    wslot = AT("wslot", [128, 3 * NDC * D], BF16, offset=123520)
    wps_hi = AT("wps_hi", [128, 16 * D], BF16, offset=123520)
    r1buf = AT("r1buf", [128, 2 * D], F32, offset=139904)
    tabC2 = AT("tabC2", [128, 512], F32, offset=92800)
    tabS2 = AT("tabS2", [128, 512], F32, offset=94848)
    sg0 = AT("sg0", [128, FF], BF16, offset=156288)
    sg1 = AT("sg1", [128, FF], BF16, offset=164480)
    h2t = AT("h2t", [128, FF], BF16, offset=156288)
    h2t2 = AT("h2t2", [128, FF], BF16, offset=164480)
    stage_x = AT("stage_x", [128, 2 * D], F32, offset=172672)
    r2buf = AT("r2buf", [128, D], F32, offset=172672)
    outbuf = AT("outbuf", [128, D], F32, offset=176768)
    tabC = AT("tabC", [128, 512], F32, offset=180864)
    tabS = AT("tabS", [128, 512], F32, offset=182912)
    x1tile = AT("x1tile", [128, 4 * D], BF16, offset=217728)
    x1T = AT("x1T", [128, 2 * D], BF16, offset=184960)
    tmps = AT("tmps", [128, 4 * D], BF16, offset=189056)
    h2T_t = AT("h2T", [128, FF], BF16, offset=189056)
    PT = AT("PT", [128, 4 * 512], BF16, offset=197248)
    PT2 = AT("PT2", [128, 4 * 512], BF16, offset=180864)
    gbuf = AT("gbuf", [128, FF], BF16, offset=197248)
    vbuf = AT("vbuf", [128, 2 * 2048], BF16, offset=205440)
    rbA = AT("rbA", [128, 512], F32, offset=213632)
    rbB = AT("rbB", [128, 512], F32, offset=215680)
    wstage = AT("wstage", [128, 2 * D], F32, offset=217728)
    rscr = AT("rscr", [128, 512], F32, offset=213632)
    vstage = AT("vstage", [128, D], BF16, offset=225920)
    x1stage0 = AT("x1stage0", [128, D], BF16, offset=225920)
    x1stage1 = AT("x1stage1", [128, D], BF16, offset=215680)
    sqscr = AT("sqscr", [128, FF], BF16, offset=189056)
    gsta = AT("gsta", [128, 1024], F32, offset=184960)
    gstb = AT("gstb", [128, 1024], F32, offset=201344)
    D0 = nc.alloc_psum_tensor("D0", [128, 1024], F32)
    D1 = nc.alloc_psum_tensor("D1", [128, 1024], F32)
    S0 = nc.alloc_psum_tensor("S0", [128, 512], F32)
    S1 = nc.alloc_psum_tensor("S1", [128, 512], F32)
    pb = [nc.alloc_psum_tensor("pb%d" % i, [128, 1024], BF16) for i in range(2)]

    p = Prog(nc)
    bank = {id(t): Bank() for t in [D0, D1, S0, S1] + pb}

    def B(t):
        return bank[id(t)]

    def mm(out_ap, lhsT_ap, rhs_ap, start, stop, waits=(), inc="T"):
        return p.do("tensor",
                    lambda e, o=out_ap, l=lhsT_ap, r=rhs_ap, s=start, st=stop:
                    e.matmul(o, l, r, start=s, stop=st,
                             skip_group_check=True),
                    waits, inc)

    def produce(t, waits):
        return list(waits) + B(t).cons

    def set_prod(t, tok):
        B(t).prod = tok
        B(t).cons = []

    def consume(t, waits=()):
        return list(waits) + ([B(t).prod] if B(t).prod else [])

    def add_cons(t, tok):
        B(t).cons.append(tok)

    tcb1 = p.dma("sync", lambda e: e.dma_start(
        out=ident_bf[:, :], in_=cpkb_h[:, 0:128]), queue="dL")
    tcb2 = p.dma("sync", lambda e: e.dma_start(
        out=mask_bf[:, :], in_=cpkb_h[:, 128:256]), queue="dL")
    tcb3 = p.dma("sync", lambda e: e.dma_start(
        out=ones_bf[:, :], in_=cpkb_h[:, 256:384]), queue="dL")
    tcf1 = p.dma("sync", lambda e: e.dma_start(
        out=ident_f32[:, :], in_=cpkf_h[:, 0:128]), queue="dL")
    tcf2 = p.dma("sync", lambda e: e.dma_start(
        out=ones_f32[:, :], in_=cpkf_h[:, 128:256]), queue="dL")
    tok_consts = p.dma("sync", lambda e: e.dma_start(
        out=consts[:, :], in_=cpkf_h[:, 256:264]), queue="dL")
    CONSTS = [tcb1, tcb2, tcb3, tcf1, tcf2, tok_consts]

    tok = p.dma("sync", lambda e: e.dma_start(
        out=wob_f[0:1, :], in_=dap(wob_h, 0, [[1024, 1], [1, 1024]])),
        queue="dL")
    tok_wob = p.do("vector", lambda e: e.tensor_copy(
        out=wob_bf[0:1, :], in_=wob_f[0:1, :]), [tok], inc="V")
    tok_lnw = p.dma("sync", lambda e: e.dma_start(
        out=lnw_raw[:, :], in_=dap(lnw_h, 0, [[128, 32], [1, 128]])),
        queue="dL")
    tok_lnb = p.dma("sync", lambda e: e.dma_start(
        out=lnb_raw[:, :], in_=dap(lnb_h, 0, [[128, 32], [1, 128]])),
        queue="dL")
    tk = mm(sap(S0, 0, 128, [[1, 32]]), lnw_raw[0:32, 0:128],
            ident_f32[0:32, 0:32], True, True,
            produce(S0, [tok_lnw] + CONSTS))
    set_prod(S0, tk)
    tk2 = p.do("vector", lambda e: e.tensor_scalar(
        out=lnw_sb[:, :], in0=sap(S0, 0, 128, [[1, 32]]), scalar1=1.0,
        scalar2=None, op0=ALU.mult), consume(S0), inc="V")
    add_cons(S0, tk2)
    tk = mm(sap(S1, 0, 128, [[1, 32]]), lnb_raw[0:32, 0:128],
            ident_f32[0:32, 0:32], True, True,
            produce(S1, [tok_lnb] + CONSTS))
    set_prod(S1, tk)
    tk2 = p.do("vector", lambda e: e.tensor_scalar(
        out=lnb_half_bf[:, :], in0=sap(S1, 0, 128, [[1, 32]]), scalar1=1.0,
        scalar2=None, op0=ALU.mult), consume(S1), inc="V")
    add_cons(S1, tk2)

    wst_cons = [tok_wob, None]
    w_cast_tok = {}
    for wi, wh in enumerate((wq_h, wk_h, wv_h)):
        toks = []
        for dc in range(NDC):
            sl = (wi * NDC + dc) % 2
            tok = p.dma("sync", lambda e, wh=wh, dc=dc, sl=sl: e.dma_start(
                out=wstage[:, sl * D:(sl + 1) * D],
                in_=wh[dc * 128:(dc + 1) * 128, :]),
                [wst_cons[sl]], queue="dL")
            ct = p.do("vector", lambda e, wi=wi, dc=dc, sl=sl: e.tensor_copy(
                out=wslot[:, (wi * NDC + dc) * D:(wi * NDC + dc + 1) * D],
                in_=wstage[:, sl * D:(sl + 1) * D]), [tok], inc="V")
            wst_cons[sl] = ct
            toks.append(ct)
        w_cast_tok[wi] = toks

    xstage_cons = [None, None]
    tab_cons = [None, None]
    QT_toks, KT_toks = [], []
    vhbm_toks = []
    pb_rot = [0]
    lastQmm = [None]
    prev_qkv_mm = [None]

    def emit_transpose8(tsrc, col0, waits):
        t = pb[pb_rot[0] % 2]
        pb_rot[0] += 1
        w = produce(t, list(waits) + CONSTS)
        tk = None
        for j in range(8):
            tk = p.do("tensor",
                      lambda e, t=t, tsrc=tsrc, col0=col0, j=j: e.transpose(
                          sap(t, 0, 128, [[1, 128]], j * 128),
                          tsrc[:, col0 + j * 128: col0 + (j + 1) * 128],
                          ident_bf[:, :]),
                      w if j == 0 else (), inc="T")
        set_prod(t, tk)
        return t

    cur_tab = [tabC, tabS]

    def rope6(ps_half, half, dst, dst_tok_list, waits):
        pt_, cb_ = ps_half
        ev = sap(pt_, 0, 128, [[64, 8], [1, 32]], cb_ + 0)
        od = sap(pt_, 0, 128, [[64, 8], [1, 32]], cb_ + 32)
        tC = sap(cur_tab[0], 0, 128, [[32, 8], [1, 32]], half * 256)
        tS = sap(cur_tab[1], 0, 128, [[32, 8], [1, 32]], half * 256)
        sAap = sap(rscr, 0, 128, [[32, 8], [1, 32]], 0)
        sBap = sap(rscr, 0, 128, [[32, 8], [1, 32]], 256)
        dev = sap(tmps, 0, 128, [[64, 8], [1, 32]], dst + half * 512)
        dod = sap(tmps, 0, 128, [[64, 8], [1, 32]], dst + half * 512 + 32)
        p.do("vector", lambda e: e.tensor_tensor(out=sAap, in0=ev, in1=tC,
                                                 op=ALU.mult), waits, inc="V")
        p.do("vector", lambda e: e.tensor_tensor(out=sBap, in0=od, in1=tS,
                                                 op=ALU.mult), (), inc="V")
        p.do("vector", lambda e: e.tensor_tensor(out=dev, in0=sAap, in1=sBap,
                                                 op=ALU.subtract), (), inc="V")
        p.do("vector", lambda e: e.tensor_tensor(out=sAap, in0=ev, in1=tS,
                                                 op=ALU.mult), (), inc="V")
        p.do("vector", lambda e: e.tensor_tensor(out=sBap, in0=od, in1=tC,
                                                 op=ALU.mult), (), inc="V")
        tk = p.do("vector", lambda e: e.tensor_tensor(out=dod, in0=sAap, in1=sBap,
                                                      op=ALU.add), (), inc="V")
        dst_tok_list.append(tk)
        return tk

    XBF, XT, QROT, KROT = 0, D, 2 * D, 3 * D

    NW = float(FF * D)
    gslot = [gsta, gstb]
    gcons = [None, None]
    gq = [0]
    gred_toks = []

    def emit_gamma_piece(load_ap, col):
        i = gq[0] % 2
        gq[0] += 1
        g = gslot[i]
        tok = p.dma("scalar", lambda e, g=g, load_ap=load_ap: e.dma_start(
            out=g[:, :], in_=load_ap), [gcons[i]], queue="dW", depth=2)
        tk = p.do("scalar", lambda e, g=g, col=col: e.activation(
            out=attT[:, 0:D], in_=g[:, :], func=AF.Abs,
            accum_out=gp[:, col:col + 1]), [tok], inc="S")
        gcons[i] = tk
        gred_toks.append(tk)

    gamma_pieces = []
    for dc in range(NDC):
        for q in range(4):
            gamma_pieces.append((dap(wfcT_h, dc * 128 * FF + q * 1024,
                                     [[FF, 128], [1, 1024]]), dc * 4 + q))
    for hcq in range(NHC):
        gamma_pieces.append((wprojT_h[hcq * 128:(hcq + 1) * 128, :], 32 + hcq))
    gper = -(-len(gamma_pieces) // n_tt)

    for tt in range(n_tt):
        sl = tt % 2
        tok_x = p.dma("sync", lambda e, tt=tt, sl=sl: e.dma_start(
            out=stage_x[:, sl * D:(sl + 1) * D],
            in_=xtt_h[tt * 128:(tt + 1) * 128, :]), [xstage_cons[sl]],
            queue="dL")
        tbC = tabC if tt % 2 == 0 else tabC2
        tbS = tabS if tt % 2 == 0 else tabS2
        tok_tc = p.dma("sync", lambda e, tt=tt, tbC=tbC: e.dma_start(
            out=tbC[:, :], in_=ctab_h[:, tt * 512:(tt + 1) * 512]),
            [tab_cons[tt % 2]], queue="dL")
        tok_ts = p.dma("sync", lambda e, tt=tt, tbS=tbS: e.dma_start(
            out=tbS[:, :], in_=stab_h[:, tt * 512:(tt + 1) * 512]),
            [tab_cons[tt % 2]], queue="dL")
        tok_xT = p.do("scalar", lambda e, sl=sl: e.copy(
            out=tmps[:, XT:XT + D], in_=stage_x[:, sl * D:(sl + 1) * D]),
            [tok_x, prev_qkv_mm[0]], inc="S")
        xstage_cons[sl] = tok_xT
        ps_pairs = {0: ((D0, 0), (D0, 512)), 1: ((D1, 0), (D1, 512)),
                    2: ((S0, 0), (S1, 0))}
        for wi in range(3):
            (tA, cA), (tB, cB) = ps_pairs[wi]
            wA = produce(tA, [tok_xT] + w_cast_tok[wi])
            wB = produce(tB, [tok_xT] + w_cast_tok[wi]) if tB is not tA else []
            tkA = tkB = None
            for dc in range(NDC):
                tkA = mm(tA[:, cA:cA + 512],
                         tmps[:, XT + dc * 128:XT + (dc + 1) * 128],
                         wslot[:, (wi * NDC + dc) * D:(wi * NDC + dc) * D + 512],
                         dc == 0, dc == NDC - 1, wA if dc == 0 else ())
                tkB = mm(tB[:, cB:cB + 512],
                         tmps[:, XT + dc * 128:XT + (dc + 1) * 128],
                         wslot[:, (wi * NDC + dc) * D + 512:(wi * NDC + dc + 1) * D],
                         dc == 0, dc == NDC - 1, wB if dc == 0 else ())
            set_prod(tA, tkA)
            set_prod(tB, tkB)
            prev_qkv_mm[0] = tkB
            if wi == 0:
                lastQmm[0] = tkB
        rope_toks = []
        cur_tab[0], cur_tab[1] = tbC, tbS
        for wi, dst in ((0, QROT), (1, KROT)):
            hA, hB = ps_pairs[wi]
            t1 = rope6(hA, 0, dst, rope_toks, consume(hA[0], [tok_tc, tok_ts]))
            add_cons(hA[0], t1)
            t2 = rope6(hB, 1, dst, rope_toks, consume(hB[0]))
            add_cons(hB[0], t2)
        tab_cons[tt % 2] = rope_toks[-1]
        for dst, res, toks in ((QROT, QT, QT_toks), (KROT, KT, KT_toks)):
            tb = emit_transpose8(tmps, dst, [rope_toks[-1]])
            tk = p.do("vector", lambda e, tb=tb, res=res, tt=tt: e.tensor_copy(
                out=sap(res, 0, 128, [[T, 8], [1, 128]], tt * 128),
                in_=sap(tb, 0, 128, [[1, 1024]])), consume(tb), inc="V")
            add_cons(tb, tk)
            toks.append(tk)
        tkv = p.do("vector", lambda e: e.tensor_copy(
            out=vstage[:, 0:512], in_=S0[:, :]),
            consume(S0, [vhbm_toks[-1] if vhbm_toks else None]), inc="V")
        add_cons(S0, tkv)
        tkv2 = p.do("vector", lambda e: e.tensor_copy(
            out=vstage[:, 512:1024], in_=S1[:, :]), consume(S1), inc="V")
        add_cons(S1, tkv2)
        tkst = p.dma("gpsimd", lambda e, tt=tt: e.dma_start(
            out=vhbm[tt * 128:(tt + 1) * 128, :], in_=vstage[:, :]),
            [tkv, tkv2], queue="dG", depth=2)
        vhbm_toks.append(tkst)
        for piece in gamma_pieces[tt * gper:(tt + 1) * gper]:
            emit_gamma_piece(*piece)

    tok_all_vhbm = vhbm_toks[-1]
    t_lastQKV = prev_qkv_mm[0]
    t_lastQmm = lastQmm[0]
    if debug:
        p.dma("gpsimd", lambda e: e.dma_start(out=dbg_qt[:, :], in_=QT[:, :]),
             QT_toks, queue="dG", depth=2)
        p.dma("gpsimd", lambda e: e.dma_start(out=dbg_kt[:, :], in_=KT[:, :]),
             KT_toks, queue="dG", depth=2)

    wo_toks = []
    for dc in range(NDC):
        sl = dc % 2
        tok = p.dma("sync", lambda e, dc=dc, sl=sl: e.dma_start(
            out=wstage[:, sl * D:(sl + 1) * D],
            in_=wo_h[dc * 128:(dc + 1) * 128, :]),
            [wst_cons[sl], t_lastQmm], queue="dL")
        ct = p.do("vector", lambda e, dc=dc, sl=sl: e.tensor_copy(
            out=wslot[:, dc * D:(dc + 1) * D],
            in_=wstage[:, sl * D:(sl + 1) * D]), [tok, t_lastQmm], inc="V")
        wst_cons[sl] = ct
        wo_toks.append(ct)

    t1 = p.do("vector", lambda e: e.reduce_sum(
        gp[:, 64:65], gp[:, 0:32], axis=AX.X), [gred_toks[31]], inc="V")
    t2 = p.do("vector", lambda e: e.reduce_sum(
        gp[:, 65:66], gp[:, 32:64], axis=AX.X), [gred_toks[-1]], inc="V")
    tk = mm(sap(S0, 0, 2, [[1, 128]]), gp[:, 64:66], ident_f32[:, :],
            True, True, produce(S0, [t1, t2]))
    set_prod(S0, tk)
    tg = p.do("vector", lambda e: e.tensor_reduce(
        gp[0:2, 66:67], sap(S0, 0, 2, [[1, 128]]), axis=AX.X, op=ALU.add),
        consume(S0), inc="V")
    add_cons(S0, tg)
    tk2 = mm(sap(S1, 0, 1, [[1, 2]]), gp[0:2, 66:67], ident_f32[0:2, 0:2],
             True, True, produce(S1, [tg]))
    set_prod(S1, tk2)
    tg2 = p.do("vector", lambda e: e.tensor_scalar(
        out=gp[0:1, 68:70], in0=sap(S1, 0, 1, [[1, 2]]), scalar1=0.5 / NW,
        scalar2=5e-7, op0=ALU.mult, op1=ALU.add), consume(S1), inc="V")
    add_cons(S1, tg2)
    tk = None
    for j in range(2):
        tk = mm(sap(S1, 0, 128, [[1, 1]], j), ones_f32[0:1, 0:128],
                gp[0:1, 68 + j:69 + j], j == 0, j == 1,
                produce(S1, [tg2]) if j == 0 else [tg2])
    set_prod(S1, tk)
    tb_ = p.do("vector", lambda e: e.tensor_copy(
        out=thr[:, 0:2], in_=sap(S1, 0, 128, [[1, 2]])),
        consume(S1), inc="V")
    add_cons(S1, tb_)
    tbn = p.do("vector", lambda e: e.tensor_scalar(
        out=thr[:, 2:4], in0=sap(S1, 0, 128, [[1, 2]]), scalar1=-1.0,
        scalar2=None, op0=ALU.mult), consume(S1), inc="V")
    add_cons(S1, tbn)
    TH = [tb_, tbn]

    wfc2_toks = []

    def emit_wfcq(dc, q, gate):
        i = gq[0] % 2
        gq[0] += 1
        g = gslot[i]
        tok = p.dma("scalar", lambda e, g=g, dc=dc, q=q: e.dma_start(
            out=g[:, :], in_=dap(wfcT_h, dc * 128 * FF + q * 1024,
                                 [[FF, 128], [1, 1024]])),
            [gcons[i]], queue="dW", depth=2)
        tm0 = p.do("vector", lambda e, g=g: e.tensor_scalar(
            out=sg0[:, 0:D], in0=g[:, :], scalar1=thr[:, 0:1], scalar2=None,
            op0=ALU.is_gt), [tok, t_lastQKV] + TH, inc="V")
        tm1 = p.do("vector", lambda e, g=g: e.tensor_scalar(
            out=sg1[:, 0:D], in0=g[:, :], scalar1=thr[:, 2:3], scalar2=None,
            op0=ALU.is_lt), (), inc="V")
        dst = wfc2_lo if dc < 4 else wfc2_hi
        dco = dc % 4
        tw = p.do("vector", lambda e, dst=dst, dco=dco, q=q: e.tensor_tensor(
            out=dst[:, dco * FF + q * D:dco * FF + (q + 1) * D],
            in0=sg0[:, 0:D], in1=sg1[:, 0:D], op=ALU.subtract),
            [tm0, tm1, gate], inc="V")
        gcons[i] = tw
        wfc2_toks.append(tw)

    wfcq_sched = {}
    for dc in range(NDC):
        for q in range(4):
            last_col = (dc % 4) * 4096 + q * 1024 + 1023
            gh = min(H - 1, 2 * (last_col // T) + 1)
            wfcq_sched.setdefault(gh, []).append((dc, q))

    vbuf_cons = [[], [], [], []]
    vbuf_tok = [None, None, None, None]
    PT_cons = [None] * 8
    PTS = lambda s: (PT if s < 4 else PT2, (s % 4) * 512)
    attT_toks = []
    rb_cons = [None, None]
    head_st_tok = {}
    lastST = [None]
    for hcp in range(H // 2):
        hA, hB = 2 * hcp, 2 * hcp + 1
        hc = hcp
        grp = max(1, n_tt // 4)
        slA, slB = 2 * (hcp % 2), 2 * (hcp % 2) + 1
        for sl_, hh in ((slA, hA), (slB, hB)):
            tok = p.dma("sync", lambda e, hh=hh, sl_=sl_: e.dma_start(
                out=bass.AP(vbuf, sl_ * 1024,
                            [[4096, 128], [64, n_tt], [1, 64]]),
                in_=bass.AP(vhbm, hh * 64,
                            [[D, 128], [128 * D, n_tt], [1, 64]])),
                [tok_all_vhbm] + vbuf_cons[sl_], queue="dL")
            vbuf_cons[sl_] = []
            vbuf_tok[sl_] = [tok]
        for qg in range(NQG):
            nk = 4 * (qg + 1)
            pend = []
            pv_tok = {hA: None, hB: None}
            first_pv = {(hA, 'main'): True, (hA, 'ones'): True,
                        (hB, 'main'): True, (hB, 'ones'): True}

            def pv_half(head, kt, tkE, pts, c0, kind):
                hp = (head % 2) * 64
                sl_ = slA if head == hA else slB
                ppv = S0 if head == hA else S1
                ptt, ptc = PTS(pts)
                vap = vbuf[:, sl_ * 1024 + kt * 64:sl_ * 1024 + (kt + 1) * 64]
                start = first_pv[(head, kind)]
                first_pv[(head, kind)] = False
                stop = kt == nk - 1
                w = produce(ppv, [tkE] + vbuf_tok[sl_]) if start else [tkE]
                if kind == "ones":
                    tok = p.do("tensor",
                               lambda e, ppv=ppv, hp=hp, ptt=ptt, ptc=ptc,
                               c0=c0, start=start, stop=stop:
                               e.matmul(ppv[64 - hp:128 - hp, c0:512],
                                        ones_bf[:, 0:64],
                                        ptt[:, ptc + c0:ptc + 512],
                                        start=start, stop=stop,
                                        skip_group_check=True,
                                        tile_position=(0, 64 - hp)),
                               w, inc="T")
                else:
                    tok = p.do("tensor",
                               lambda e, ppv=ppv, hp=hp, ptt=ptt, ptc=ptc,
                               c0=c0, vap=vap, start=start, stop=stop:
                               e.matmul(ppv[hp:hp + 64, c0:512], vap,
                                        ptt[:, ptc + c0:ptc + 512],
                                        start=start, stop=stop,
                                        skip_group_check=True,
                                        tile_position=(0, hp)),
                               w, inc="T")
                PT_cons[pts] = tok
                return tok

            def flush_pend():
                pend.sort(key=lambda it: it[1])
                while pend:
                    grp = [pend.pop(0)]
                    while pend and pend[0][1] == grp[0][1]:
                        grp.append(pend.pop(0))
                    for kind in ("main", "ones"):
                        for (h_, kt_, tkE_, pts_, c0_) in grp:
                            pv_tok[h_] = pv_half(h_, kt_, tkE_, pts_, c0_,
                                                 kind)

            for kp in range(0, nk, 2):
                sbase = (kp % 4 // 2) * 4
                stoks = {}
                c0ss = {}
                for kt in (kp, kp + 1):
                    for head, dd in ((hA, D0), (hB, D1)):
                        hp = (head % 2) * 64
                        c0 = max(0, kt * 128 - qg * 512)
                        cb = (kt - kp) * 512
                        ncols = 512 - c0
                        diag = kt >= 4 * qg
                        w = produce(dd, QT_toks + KT_toks + CONSTS) \
                            if kt == kp else ()
                        tkS = mm(dd[:, cb + c0:cb + 512],
                                 KT[hp:hp + 64,
                                    hc * T + kt * 128:hc * T + (kt + 1) * 128],
                                 sap(QT, hp, 64, [[1, ncols]],
                                     hc * T + qg * 512 + c0),
                                 True, not diag, w)
                        if diag:
                            tkS = mm(dd[:, cb + c0:cb + c0 + 128],
                                     ident_bf[:, :], mask_bf[:, :],
                                     False, True, ())
                        stoks[head] = tkS
                        c0ss.setdefault(head, []).append(c0)
                if pend:
                    flush_pend()
                for head, dd, soff in ((hA, D0, 0), (hB, D1, 2)):
                    hp = (head % 2) * 64
                    tkS = stoks[head]
                    c0s = c0ss[head]
                    set_prod(dd, tkS)
                    lastST[0] = tkS
                    head_st_tok[head] = tkS
                    pts = sbase + soff
                    ptt, ptc = PTS(pts)
                    if c0s[0] == 0 and c0s[1] == 0:
                        tkE = p.do("scalar",
                                   lambda e, dd=dd, ptt=ptt, ptc=ptc:
                                   e.activation(
                                       out=ptt[:, ptc:ptc + 1024],
                                       in_=dd[:, :], func=AF.Exp,
                                       bias=consts[:, 0:1], scale=0.125),
                                   consume(dd, [PT_cons[pts],
                                                PT_cons[pts + 1]] + CONSTS),
                                   inc="S")
                    else:
                        for j in range(2):
                            c0 = c0s[j]
                            tkE = p.do("scalar",
                                       lambda e, dd=dd, ptt=ptt, ptc=ptc,
                                       j=j, c0=c0:
                                       e.activation(
                                           out=ptt[:, ptc + j * 512 + c0:
                                                   ptc + (j + 1) * 512],
                                           in_=dd[:, j * 512 + c0:
                                                  (j + 1) * 512],
                                           func=AF.Exp, bias=consts[:, 0:1],
                                           scale=0.125),
                                       consume(dd, [PT_cons[pts + j]]
                                               + CONSTS), inc="S")
                    add_cons(dd, tkE)
                    pend.append((head, kp, tkE, pts, c0s[0]))
                    pend.append((head, kp + 1, tkE, pts + 1, c0s[1]))
            flush_pend()
            for head in (hA, hB):
                hp = (head % 2) * 64
                ppv = S0 if head == hA else S1
                sl_ = slA if head == hA else slB
                set_prod(ppv, pv_tok[head])
                vbuf_cons[sl_].append(pv_tok[head])
                sums_base = 64 - hp
                tkL = p.do("scalar", lambda e, ppv=ppv, sb=sums_base, hp=hp:
                           e.activation(out=rbA[hp:hp + 64, :],
                                        in_=ppv[sb:sb + 64, :], func=AF.Ln),
                           consume(ppv), inc="S")
                tkX = p.do("scalar", lambda e, hp=hp: e.activation(
                    out=rbB[hp:hp + 64, :], in_=rbA[hp:hp + 64, :],
                    func=AF.Exp, scale=-1.0),
                    [tkL, rb_cons[head % 2]], inc="S")
                tkA = p.do("vector", lambda e, ppv=ppv, hp=hp, hc=hc, qg=qg:
                           e.tensor_tensor(
                               out=attT[hp:hp + 64,
                                        hc * T + qg * 512:
                                        hc * T + (qg + 1) * 512],
                               in0=ppv[hp:hp + 64, :], in1=rbB[hp:hp + 64, :],
                               op=ALU.mult), [tkX], inc="V")
                add_cons(ppv, tkA)
                rb_cons[head % 2] = tkA
                attT_toks.append(tkA)
        for hh in (hA, hB):
            for (dcq, qq) in wfcq_sched.get(hh, []):
                emit_wfcq(dcq, qq, head_st_tok[hh])

    t_lastST = lastST[0]
    if debug:
        tda = p.dma("gpsimd", lambda e: e.dma_start(
            out=dbg_att[:, :], in_=attT[:, :]), attT_toks, queue="dG", depth=2)
        p.dma("gpsimd", lambda e: e.dma_start(out=dbg_rb[:, :], in_=rbB[:, :]),
             attT_toks, queue="dG", depth=2)
        attT_toks = attT_toks + [tda]

    xrel_cons = list(xstage_cons)
    x1st_cons = [tok_all_vhbm, tok_all_vhbm]
    r1_cons = [None, None]
    x1_toks = []
    lastWO = [None]
    for ti in range(n_tt):
        sl = ti % 2
        tok_x = p.dma("sync", lambda e, ti=ti, sl=sl: e.dma_start(
            out=stage_x[:, sl * D:(sl + 1) * D],
            in_=x_h[ti * 128:(ti + 1) * 128, :]), [xrel_cons[sl]],
            queue="dL")
        DW = D0 if ti % 2 == 0 else D1
        wA = produce(DW, attT_toks + wo_toks)
        for fc in range(NDC):
            la = attT[:, fc * T + ti * 128:fc * T + (ti + 1) * 128]
            mm(DW[:, 0:512], la, wslot[:, fc * D:fc * D + 512],
               fc == 0, False, wA if fc == 0 else ())
            mm(DW[:, 512:1024], la, wslot[:, fc * D + 512:(fc + 1) * D],
               fc == 0, False, ())
        mm(DW[:, 0:512], ones_bf[0:1, 0:128], wob_bf[0:1, 0:512], False, True,
           [tok_wob])
        tkB = mm(DW[:, 512:1024], ones_bf[0:1, 0:128], wob_bf[0:1, 512:1024],
                 False, True, ())
        set_prod(DW, tkB)
        lastWO[0] = tkB
        tr1a = p.do("vector", lambda e, sl=sl, DW=DW: e.tensor_add(
            out=r1buf[:, sl * D:sl * D + 512], in0=DW[:, 0:512],
            in1=stage_x[:, sl * D:sl * D + 512]),
            consume(DW, [tok_x, r1_cons[sl]]), inc="V")
        tr1b = p.do("vector", lambda e, sl=sl, DW=DW: e.tensor_add(
            out=r1buf[:, sl * D + 512:(sl + 1) * D], in0=DW[:, 512:1024],
            in1=stage_x[:, sl * D + 512:(sl + 1) * D]), (), inc="V")
        add_cons(DW, tr1b)
        xrel_cons[sl] = tr1b
        r1 = r1buf[:, sl * D:(sl + 1) * D]
        pc = 8 * (ti % 2)
        s1, s2 = sm[:, pc:pc + 1], sm[:, pc + 1:pc + 2]
        mu, m2 = sm[:, pc + 2:pc + 3], sm[:, pc + 3:pc + 4]
        var, rstd = sm[:, pc + 4:pc + 5], sm[:, pc + 5:pc + 6]
        nb = sm[:, pc + 6:pc + 7]
        ts1 = p.do("vector", lambda e, r1=r1, s1=s1: e.reduce_sum(
            s1, r1, axis=AX.X), [tr1a, tr1b], inc="V")
        ts2 = p.do("scalar", lambda e, r1=r1, s2=s2: e.activation(
            out=sqscr[:, 0:D], in_=r1, func=AF.Square, accum_out=s2),
            [tr1a, tr1b], inc="S")
        p.do("vector", lambda e, s2=s2, m2=m2: e.tensor_scalar(
            out=m2, in0=s2, scalar1=float(D), scalar2=None, op0=ALU.mult),
            [ts1, ts2], inc="V")
        tv = p.do("vector", lambda e, s1=s1, m2=m2, var=var:
                  e.scalar_tensor_tensor(
                      out=var, in0=s1, scalar=s1, in1=m2,
                      op0=ALU.mult, op1=ALU.subtract), (), inc="V")
        tl = p.do("scalar", lambda e, var=var, mu=mu: e.activation(
            out=mu, in_=var, func=AF.Ln, bias=consts[:, 1:2],
            scale=-1.0 / (D * D)), [tv] + CONSTS, inc="S")
        tl2 = p.do("scalar", lambda e, mu=mu, rstd=rstd: e.activation(
            out=rstd, in_=mu, func=AF.Exp, scale=-0.5), [tl], inc="S")
        tnb = p.do("vector", lambda e, s1=s1, nb=nb, rstd=rstd: e.tensor_scalar(
            out=nb, in0=s1, scalar1=rstd, scalar2=-1.0 / D, op0=ALU.mult,
            op1=ALU.mult), [tl2], inc="V")
        x1st = x1stage0 if sl == 0 else x1stage1
        tap = p.do("scalar", lambda e, r1=r1, rstd=rstd, nb=nb, x1st=x1st:
                   e.activation(
                       out=x1st[:, :], in_=r1,
                       func=AF.Identity, scale=rstd, bias=nb),
                   [tnb, x1st_cons[sl]], inc="S")
        r1_cons[sl] = tap
        tst = p.dma("gpsimd", lambda e, ti=ti, x1st=x1st: e.dma_start(
            out=x1hbm[ti * 128:(ti + 1) * 128, :],
            in_=x1st[:, :]), [tap], queue="dG", depth=2)
        x1st_cons[sl] = tst
        x1_toks.append(tst)
    t_lastWO = lastWO[0]
    t_x1_done = x1_toks[-1]
    if debug:
        p.dma("gpsimd", lambda e: e.dma_start(out=dbg_sm[:, :], in_=sm[:, :]),
             [t_x1_done], queue="dG", depth=2)
        p.dma("gpsimd", lambda e: e.dma_start(out=dbg_r1[:, :], in_=r1buf[:, :]),
             [t_x1_done], queue="dG", depth=2)

    wps_toks = []
    ctk = None
    tkc = None
    for hc in range(NHC):
        i = gq[0] % 2
        gq[0] += 1
        g = gslot[i]
        tok = p.dma("scalar", lambda e, g=g, hc=hc: e.dma_start(
            out=g[:, :], in_=wprojT_h[hc * 128:(hc + 1) * 128, :]),
            [gcons[i]], queue="dW", depth=2)
        tm0 = p.do("vector", lambda e, g=g: e.tensor_scalar(
            out=sg0[:, 0:D], in0=g[:, :], scalar1=thr[:, 1:2], scalar2=None,
            op0=ALU.is_gt), [tok] + TH, inc="V")
        tm1 = p.do("vector", lambda e, g=g: e.tensor_scalar(
            out=sg1[:, 0:D], in0=g[:, :], scalar1=thr[:, 3:4], scalar2=None,
            op0=ALU.is_lt), (), inc="V")
        dst = wps_lo if hc < 16 else wps_hi
        hco = hc % 16
        gate = t_lastWO if hc < 16 else t_x1_done
        tw = p.do("vector", lambda e, dst=dst, hco=hco: e.tensor_tensor(
            out=dst[:, hco * D:(hco + 1) * D], in0=sg0[:, 0:D],
            in1=sg1[:, 0:D], op=ALU.subtract), [tm0, tm1, gate], inc="V")
        gcons[i] = tw
        for half, ct_ps in ((0, S0), (1, S1)):
            w = produce(ct_ps, [tw]) if hc == 0 else [tw]
            ctk = mm(sap(ct_ps, 0, 1, [[1, 512]]),
                     lnb_half_bf[:, hc:hc + 1],
                     dst[:, hco * D + half * 512:hco * D + (half + 1) * 512],
                     hc == 0, hc == NHC - 1, w)
        tsc = p.do("vector", lambda e, dst=dst, hco=hco, hc=hc: e.tensor_scalar(
            out=dst[:, hco * D:(hco + 1) * D],
            in0=dst[:, hco * D:(hco + 1) * D],
            scalar1=lnw_sb[:, hc:hc + 1], scalar2=None, op0=ALU.mult),
            [ctk], inc="V")
        wps_toks.append(tsc)
    set_prod(S0, ctk)
    set_prod(S1, ctk)
    tkc = None
    for half, ct_ps in ((0, S0), (1, S1)):
        tkc = p.do("scalar", lambda e, half=half, ct_ps=ct_ps: e.activation(
            out=c_bf[0:1, half * 512:(half + 1) * 512],
            in_=sap(ct_ps, 0, 1, [[1, 512]]), func=AF.Identity),
            consume(ct_ps), inc="S")
        add_cons(ct_ps, tkc)
    t_quant = [wfc2_toks[-1], wps_toks[-1], tkc]
    if debug:
        p.dma("gpsimd", lambda e: e.dma_start(out=dbg_thr[:, :], in_=thr[:, :]),
             TH, queue="dG", depth=2)
        p.dma("gpsimd", lambda e: e.dma_start(
            out=dap(dbg_wfc2, 0, [[8 * FF, 128], [1, 4 * FF]]),
            in_=wfc2_lo[:, :]), wfc2_toks, queue="dG", depth=2)
        p.dma("gpsimd", lambda e: e.dma_start(
            out=dap(dbg_wfc2, 4 * FF, [[8 * FF, 128], [1, 4 * FF]]),
            in_=wfc2_hi[:, :]), wfc2_toks, queue="dG", depth=2)
        p.dma("gpsimd", lambda e: e.dma_start(
            out=dap(dbg_wps, 0, [[32 * D, 128], [1, 16 * D]]),
            in_=wps_lo[:, :]), wps_toks, queue="dG", depth=2)
        p.dma("gpsimd", lambda e: e.dma_start(
            out=dap(dbg_wps, 16 * D, [[32 * D, 128], [1, 16 * D]]),
            in_=wps_hi[:, :]), wps_toks, queue="dG", depth=2)
        tcb = p.dma("gpsimd", lambda e: e.dma_start(
            out=dbg_cbf[0:1, :], in_=c_bf[0:1, :]), [tkc], queue="dG", depth=2)
        t_quant = t_quant + [tcb]

    x1t_cons = [None, None, None, None]
    x1T_mm_cons = [None, None]
    prev_proj_mm = [None]
    prev_h2_transp = [None, None]
    out_cons = [None, None]
    out_toks = []
    state = {}

    def p7_stage_a1(tt):
        sl = tt % 2
        sl4 = tt % 4
        tok_x1 = p.dma("sync", lambda e, tt=tt, sl4=sl4: e.dma_start(
            out=x1tile[:, sl4 * D:(sl4 + 1) * D],
            in_=x1hbm[tt * 128:(tt + 1) * 128, :]),
            [x1_toks[tt], x1t_cons[sl4]], queue="dL")
        tb = emit_transpose8(x1tile, sl4 * D, [tok_x1])
        tok_x1T = p.do("vector", lambda e, tb=tb, sl=sl: e.tensor_copy(
            out=x1T[:, sl * D:(sl + 1) * D], in_=sap(tb, 0, 128, [[1, 1024]])),
            consume(tb, [x1T_mm_cons[sl]]), inc="V")
        add_cons(tb, tok_x1T)
        return tok_x1T

    def p7_stage_a(tt, tok_x1T):
        sl = tt % 2
        st = {}
        gelu_toks = []
        pc = 8 * (tt % 2)
        for hb in range(0, NHB, 2):
            dd = [D0, D1][(hb // 2) % 2]
            w = produce(dd, [tok_x1T] + wfc2_toks)
            tkF = None
            for dc in range(NDC):
                wsrc = wfc2_lo if dc < 4 else wfc2_hi
                dco = dc % 4
                tkF = mm(dd[:, 0:512],
                         x1T[:, sl * D + dc * 128:sl * D + (dc + 1) * 128],
                         wsrc[:, dco * FF + hb * 512:dco * FF + (hb + 1) * 512],
                         dc == 0, False, w if dc == 0 else ())
                tkF = mm(dd[:, 512:1024],
                         x1T[:, sl * D + dc * 128:sl * D + (dc + 1) * 128],
                         wsrc[:, dco * FF + (hb + 1) * 512:
                              dco * FF + (hb + 2) * 512],
                         dc == 0, dc == NDC - 1, ())
            set_prod(dd, tkF)
            x1T_mm_cons[sl] = tkF
            tkG = p.do("scalar", lambda e, dd=dd, hb=hb, pc=pc: e.activation(
                out=gbuf[:, hb * 512:(hb + 2) * 512], in_=dd[:, :],
                func=AF.Gelu, scale=1.0,
                accum_out=gs[:, pc + hb // 2:pc + hb // 2 + 1]),
                consume(dd), inc="S")
            add_cons(dd, tkG)
            tkQ = p.do("scalar", lambda e, hb=hb, pc=pc: e.activation(
                out=sqscr[:, hb * 512:(hb + 2) * 512],
                in_=gbuf[:, hb * 512:(hb + 2) * 512], func=AF.Square,
                accum_out=gs[:, pc + 4 + hb // 2:pc + 5 + hb // 2]),
                [tkG], inc="S")
            gelu_toks.append(tkQ)
        if tt + 1 < n_tt:
            a1_toks[tt + 1] = p7_stage_a1(tt + 1)
        pcs = 16 + 8 * (tt % 2)
        s1, s2 = sm[:, pcs:pcs + 1], sm[:, pcs + 1:pcs + 2]
        m2 = sm[:, pcs + 3:pcs + 4]
        var, rstd = sm[:, pcs + 4:pcs + 5], sm[:, pcs + 5:pcs + 6]
        mu, nb = sm[:, pcs + 2:pcs + 3], sm[:, pcs + 6:pcs + 7]
        ts1 = p.do("vector", lambda e, pc=pc, s1=s1: e.reduce_sum(
            s1, gs[:, pc:pc + 4], axis=AX.X), gelu_toks, inc="V")
        ts2 = p.do("vector", lambda e, pc=pc, s2=s2: e.reduce_sum(
            s2, gs[:, pc + 4:pc + 8], axis=AX.X), gelu_toks, inc="V")
        p.do("vector", lambda e, s2=s2, m2=m2: e.tensor_scalar(
            out=m2, in0=s2, scalar1=float(FF), scalar2=None, op0=ALU.mult),
            [ts1, ts2], inc="V")
        tv = p.do("vector", lambda e, s1=s1, m2=m2, var=var:
                  e.scalar_tensor_tensor(
                      out=var, in0=s1, scalar=s1, in1=m2,
                      op0=ALU.mult, op1=ALU.subtract), (), inc="V")
        tl = p.do("scalar", lambda e, var=var, mu=mu: e.activation(
            out=mu, in_=var, func=AF.Ln, bias=consts[:, 1:2],
            scale=-1.0 / (float(FF) * FF)), [tv], inc="S")
        tl2 = p.do("scalar", lambda e, mu=mu, rstd=rstd: e.activation(
            out=rstd, in_=mu, func=AF.Exp, scale=-0.5), [tl], inc="S")
        tnb = p.do("vector", lambda e, s1=s1, nb=nb, rstd=rstd: e.tensor_scalar(
            out=nb, in0=s1, scalar1=rstd, scalar2=-1.0 / FF, op0=ALU.mult,
            op1=ALU.mult), [tl2], inc="V")
        h2dst = h2t if sl == 0 else h2t2
        th2a = p.do("scalar", lambda e, rstd=rstd, nb=nb, h2dst=h2dst:
                    e.activation(
                        out=h2dst[:, 0:2048], in_=gbuf[:, 0:2048],
                        func=AF.Identity, scale=rstd, bias=nb),
                    [tnb, prev_h2_transp[sl]] + t_quant, inc="S")
        th2b = p.do("scalar", lambda e, rstd=rstd, nb=nb, h2dst=h2dst:
                    e.activation(
                        out=h2dst[:, 2048:4096], in_=gbuf[:, 2048:4096],
                        func=AF.Identity, scale=rstd, bias=nb),
                    (), inc="S")
        st["th2"] = (th2a, th2b)
        st["sl"] = sl
        return st

    def p7_stage_b(tt, st):
        sl = st["sl"]
        h2src = h2t if sl == 0 else h2t2
        h2T_toks = []
        for g4 in range(4):
            tbk = pb[pb_rot[0] % 2]
            pb_rot[0] += 1
            w = produce(tbk, [st["th2"][0 if g4 < 2 else 1]])
            tkT = None
            for j in range(8):
                cj = g4 * 8 + j
                tkT = p.do("tensor",
                           lambda e, tbk=tbk, cj=cj, j=j, h2src=h2src:
                           e.transpose(
                               sap(tbk, 0, 128, [[1, 128]], j * 128),
                               h2src[:, cj * 128:(cj + 1) * 128],
                               ident_bf[:, :]),
                           w if j == 0 else (), inc="T")
            set_prod(tbk, tkT)
            prev_h2_transp[sl] = tkT
            tke = p.do("vector", lambda e, tbk=tbk, g4=g4: e.tensor_copy(
                out=h2T_t[:, g4 * 1024:(g4 + 1) * 1024],
                in_=sap(tbk, 0, 128, [[1, 1024]])),
                consume(tbk, [prev_proj_mm[0]]), inc="V")
            add_cons(tbk, tke)
            h2T_toks.append(tke)
        wA = produce(S0, [h2T_toks[0]] + wps_toks)
        wB = produce(S1, [h2T_toks[0]] + wps_toks)
        for hc in range(NHC):
            wsrc = wps_lo if hc < 16 else wps_hi
            hco = hc % 16
            la = h2T_t[:, hc * 128:(hc + 1) * 128]
            wg = (wA if hc == 0 else
                  [h2T_toks[hc // 8]] if hc % 8 == 0 else ())
            mm(S0[:, :], la, wsrc[:, hco * D:hco * D + 512],
               hc == 0, False, wg)
            mm(S1[:, :], la, wsrc[:, hco * D + 512:(hco + 1) * D],
               hc == 0, False, wB if hc == 0 else ())
        tkA = mm(S0[:, :], ones_bf[0:1, 0:128], c_bf[0:1, 0:512], False, True,
                 [tkc])
        tkB = mm(S1[:, :], ones_bf[0:1, 0:128], c_bf[0:1, 512:1024],
                 False, True, ())
        set_prod(S0, tkA)
        set_prod(S1, tkB)
        prev_proj_mm[0] = tkB
        sl4 = tt % 4
        tr2a = p.do("vector", lambda e, sl4=sl4: e.tensor_add(
            out=r2buf[:, 0:512], in0=S0[:, :],
            in1=x1tile[:, sl4 * D:sl4 * D + 512]),
            consume(S0, [out_cons[0]]), inc="V")
        add_cons(S0, tr2a)
        tr2b = p.do("vector", lambda e, sl4=sl4: e.tensor_add(
            out=r2buf[:, 512:1024], in0=S1[:, :],
            in1=x1tile[:, sl4 * D + 512:(sl4 + 1) * D]),
            consume(S1, [out_cons[0]]), inc="V")
        add_cons(S1, tr2b)
        x1t_cons[sl4] = tr2b
        pcs = 32 + 8 * (tt % 2)
        s1, s2 = sm[:, pcs:pcs + 1], sm[:, pcs + 1:pcs + 2]
        m2 = sm[:, pcs + 3:pcs + 4]
        var, rstd = sm[:, pcs + 4:pcs + 5], sm[:, pcs + 5:pcs + 6]
        mu, nb = sm[:, pcs + 2:pcs + 3], sm[:, pcs + 6:pcs + 7]
        ts1 = p.do("vector", lambda e, s1=s1: e.reduce_sum(
            s1, r2buf[:, :], axis=AX.X), [tr2a, tr2b], inc="V")
        ts2 = p.do("scalar", lambda e, s2=s2: e.activation(
            out=sqscr[:, 0:D], in_=r2buf[:, :], func=AF.Square, accum_out=s2),
            [tr2a, tr2b], inc="S")
        p.do("vector", lambda e, s2=s2, m2=m2: e.tensor_scalar(
            out=m2, in0=s2, scalar1=float(D), scalar2=None, op0=ALU.mult),
            [ts1, ts2], inc="V")
        tv = p.do("vector", lambda e, s1=s1, m2=m2, var=var:
                  e.scalar_tensor_tensor(
                      out=var, in0=s1, scalar=s1, in1=m2,
                      op0=ALU.mult, op1=ALU.subtract), (), inc="V")
        tl = p.do("scalar", lambda e, var=var, mu=mu: e.activation(
            out=mu, in_=var, func=AF.Ln, bias=consts[:, 1:2],
            scale=-1.0 / (D * D)), [tv], inc="S")
        tl2 = p.do("scalar", lambda e, mu=mu, rstd=rstd: e.activation(
            out=rstd, in_=mu, func=AF.Exp, scale=-0.5), [tl], inc="S")
        tnb = p.do("vector", lambda e, s1=s1, nb=nb, rstd=rstd: e.tensor_scalar(
            out=nb, in0=s1, scalar1=rstd, scalar2=-1.0 / D, op0=ALU.mult,
            op1=ALU.mult), [tl2], inc="V")
        tout = p.do("scalar", lambda e, rstd=rstd, nb=nb: e.activation(
            out=outbuf[:, :], in_=r2buf[:, :], func=AF.Identity, scale=rstd,
            bias=nb), [tnb, out_cons[1]], inc="S")
        tst = p.dma("gpsimd", lambda e, tt=tt: e.dma_start(
            out=out_h[tt * 128:(tt + 1) * 128, :], in_=outbuf[:, :]),
            [tout], queue="dO", depth=2)
        out_cons[0] = tst
        out_cons[1] = tst
        out_toks.append(tst)

    prev = None
    a1_toks = {0: None}
    a1_toks[0] = p7_stage_a1(0)
    for tt in range(n_tt):
        cur = p7_stage_a(tt, a1_toks.pop(tt))
        if prev is not None:
            p7_stage_b(tt - 1, prev)
        prev = cur
    p7_stage_b(n_tt - 1, prev)

    p.wait_only("sync", out_toks[-2:])

    with nc.Block() as block:
        @block.sync
        def _(e):
            p.replay("sync", e)

        @block.tensor
        def _(e):
            p.replay("tensor", e)

        @block.vector
        def _(e):
            p.replay("vector", e)

        @block.scalar
        def _(e):
            p.replay("scalar", e)

        @block.gpsimd
        def _(e):
            p.replay("gpsimd", e)

    return nc


def make_xtt(x2d, n_tt=16):
    T = n_tt * 128
    return np.ascontiguousarray(
        x2d.reshape(n_tt, 128, NDC, 128).transpose(0, 3, 2, 1).reshape(T, D))


def host_prep(wq, wk, wv, wo, wob, wfc, lnw, lnb, wproj, n_tt=16):
    T = n_tt * 128
    perm = np.concatenate([np.concatenate([64 * h + np.arange(0, 64, 2),
                                           64 * h + np.arange(1, 64, 2)])
                           for h in range(H)])
    d = dict(
        wq=np.ascontiguousarray(np.asarray(wq, np.float32)[:, perm]),
        wk=np.ascontiguousarray(np.asarray(wk, np.float32)[:, perm]),
        wv=np.ascontiguousarray(np.asarray(wv, np.float32)),
        wo=np.ascontiguousarray(np.asarray(wo, np.float32)),
        wob=np.ascontiguousarray(np.asarray(wob, np.float32)),
        wfcT=np.ascontiguousarray(np.asarray(wfc, np.float32).T),
        lnw=np.ascontiguousarray(np.asarray(lnw, np.float32)),
        lnb=np.ascontiguousarray(np.asarray(lnb, np.float32)),
        wprojT=np.ascontiguousarray(np.asarray(wproj, np.float32).T),
    )
    import jax
    import jax.numpy as jnp
    with jax.default_device(jax.devices("cpu")[0]):
        invj = 1.0 / 10000.0 ** (jnp.arange(0, HD, 2)[:HD // 2].astype(
            jnp.float32) / HD)
        angj = jnp.outer(jnp.arange(T, dtype=jnp.float32), invj)
        cosv = np.asarray(jnp.cos(angj)).reshape(n_tt, 128, 32)
        sinv = np.asarray(jnp.sin(angj)).reshape(n_tt, 128, 32)
    for nm, v in (("ctab", cosv), ("stab", sinv)):
        tab = np.zeros((128, n_tt * 512), np.float32)
        for tt in range(n_tt):
            tab[:, tt * 512:(tt + 1) * 512] = np.tile(v[tt], (1, H))
        d[nm] = tab
    eye = np.eye(128, dtype=np.float32)
    maskv = np.tril(np.full((128, 128), -30000.0, np.float32), -1)
    onesv = np.ones((128, 128), np.float32)
    d["cpkb"] = np.concatenate([eye, maskv, onesv], axis=1).astype(
        ml_dtypes.bfloat16)
    cvec = np.zeros((128, 8), np.float32)
    cvec[:, 0] = -5.0
    cvec[:, 1] = 1e-5
    d["cpkf"] = np.concatenate([eye, onesv, cvec], axis=1)
    return d


def kernel(L0_attn_x, L0_attn_wq, L0_attn_wk, L0_attn_wv, L0_attn_wo_w,
           L0_attn_wo_b, L0_ffn_W_fc, L0_ffn_ln_w, L0_ffn_ln_b, L0_ffn_W_proj):
    x = np.ascontiguousarray(np.asarray(L0_attn_x, np.float32))
    n_cores = x.shape[0]
    shared = host_prep(L0_attn_wq, L0_attn_wk, L0_attn_wv, L0_attn_wo_w,
                       L0_attn_wo_b, L0_ffn_W_fc, L0_ffn_ln_w, L0_ffn_ln_b,
                       L0_ffn_W_proj, n_tt=16)
    nc = build(n_tt=16)
    in_maps = [dict(shared, x=np.ascontiguousarray(x[b]),
                    xtt=make_xtt(x[b]))
               for b in range(n_cores)]
    res = run_bass_kernel_spmd(nc, in_maps, core_ids=list(range(n_cores)))
    return np.stack([r["out"] for r in res.results], axis=0)


# version 31 (final)
# speedup vs baseline: 1.0711x; 1.0044x over previous
import os

import ml_dtypes
import numpy as np
import concourse.bass as bass
import concourse.mybir as mybir
from concourse.bass_utils import run_bass_kernel_spmd

F32 = mybir.dt.float32
BF16 = mybir.dt.bfloat16
AF = mybir.ActivationFunctionType
ALU = mybir.AluOpType
AX = mybir.AxisListType

ENGINES = ("sync", "tensor", "vector", "scalar", "gpsimd")
ENG_SEM = {"tensor": "T", "vector": "V", "scalar": "S", "gpsimd": "G"}

D = 1024
H = 16
HD = 64
FF = 4096
NDC = D // 128
NHC = FF // 128
NHB = FF // 512


class Prog:
    def __init__(self, nc):
        self.nc = nc
        self.ops = {e: [] for e in ENGINES}
        self.sems = {}
        self.counts = {}
        self.qstate = {}
        self.hwm = {e: {} for e in ENGINES}
        self.strict = os.environ.get("BASS_SIM_FP64") == "1"

    def do(self, eng, fn, waits=(), inc=None, n=1, self_wait=False,
           hard=False):
        need = {}
        for w in waits:
            if w is None:
                continue
            s, v = w
            if v > 0 and v > need.get(s, 0):
                need[s] = v
        wl = []
        if self.strict and fn is not None and eng in ENG_SEM:
            s = ENG_SEM[eng]
            prev = self.counts.get(s, 0)
            if inc is None:
                inc = s
                if s not in self.sems:
                    self.sems[s] = self.nc.alloc_semaphore(s)
                    self.counts[s] = 0
            if prev > 0:
                wl.append((s, prev))
                if prev > self.hwm[eng].get(s, 0):
                    self.hwm[eng][s] = prev
        for s, v in need.items():
            if not hard and v <= self.hwm[eng].get(s, 0):
                continue
            if v > self.hwm[eng].get(s, 0):
                self.hwm[eng][s] = v
            wl.append((s, v))
        tok = None
        if inc is not None:
            if inc not in self.sems:
                self.sems[inc] = self.nc.alloc_semaphore(inc)
                self.counts[inc] = 0
            self.counts[inc] += n
            tok = (inc, self.counts[inc])
        self.ops[eng].append((fn, wl, inc, n))
        if self_wait and tok is not None:
            s, v = tok
            if v > self.hwm[eng].get(s, 0):
                self.hwm[eng][s] = v
                self.ops[eng].append((None, [(s, v)], None, 1))
        return tok

    def wait_only(self, eng, waits):
        self.do(eng, None, waits)

    def dma(self, eng, fn, waits=(), queue="dL", depth=6):
        st = self.qstate.setdefault((eng, queue), {"i": 0, "last": {}})
        i = st["i"]
        st["i"] += 1
        sem = "%s%d" % (queue, i % depth)
        w = list(waits)
        prev = st["last"].get(i % depth)
        if prev is not None:
            w.append(prev)
        tok = self.do(eng, fn, w, inc=sem, n=16)
        st["last"][i % depth] = tok
        return tok

    def replay(self, eng_name, e):
        for fn, wl, inc, n in self.ops[eng_name]:
            for s, v in wl:
                e.wait_ge(self.sems[s], v)
            if fn is None:
                continue
            inst = fn(e)
            if inc is not None:
                inst.then_inc(self.sems[inc], n)


def sap(t, p0, nparts, fdims, c0=0):
    w = t.shape[-1]
    return bass.AP(t, p0 * w + c0, [[w, nparts]] + [list(d) for d in fdims])


def dap(t, off, dims):
    return bass.AP(t, off, [list(d) for d in dims])


class Bank:
    __slots__ = ("prod", "cons")

    def __init__(self):
        self.prod = None
        self.cons = []


def build(n_tt=16, debug=False):
    T = n_tt * 128
    NQG = n_tt // 4
    nc = bass.Bass("TRN2", target_bir_lowering=False)

    x_h = nc.dram_tensor("x", [T, D], F32, kind="ExternalInput")
    xtt_h = nc.dram_tensor("xtt", [T, D], F32, kind="ExternalInput")
    wq_h = nc.dram_tensor("wq", [D, D], F32, kind="ExternalInput")
    wk_h = nc.dram_tensor("wk", [D, D], F32, kind="ExternalInput")
    wv_h = nc.dram_tensor("wv", [D, D], F32, kind="ExternalInput")
    wo_h = nc.dram_tensor("wo", [D, D], F32, kind="ExternalInput")
    wob_h = nc.dram_tensor("wob", [D], F32, kind="ExternalInput")
    wfcT_h = nc.dram_tensor("wfcT", [D, FF], F32, kind="ExternalInput")
    lnw_h = nc.dram_tensor("lnw", [FF], F32, kind="ExternalInput")
    lnb_h = nc.dram_tensor("lnb", [FF], F32, kind="ExternalInput")
    wprojT_h = nc.dram_tensor("wprojT", [FF, D], F32, kind="ExternalInput")
    ctab_h = nc.dram_tensor("ctab", [128, n_tt * 512], F32, kind="ExternalInput")
    stab_h = nc.dram_tensor("stab", [128, n_tt * 512], F32, kind="ExternalInput")
    cpkb_h = nc.dram_tensor("cpkb", [128, 384], BF16, kind="ExternalInput")
    cpkf_h = nc.dram_tensor("cpkf", [128, 264], F32, kind="ExternalInput")
    out_h = nc.dram_tensor("out", [T, D], F32, kind="ExternalOutput")
    iokind = "ExternalOutput" if debug else "Internal"
    vhbm = nc.dram_tensor("v_hbm", [T, D], BF16, kind=iokind)
    x1hbm = nc.dram_tensor("x1_hbm", [T, D], BF16, kind=iokind)
    if debug:
        dbg_qt = nc.dram_tensor("dbg_qt", [128, NDC * T], BF16, kind="ExternalOutput")
        dbg_kt = nc.dram_tensor("dbg_kt", [128, NDC * T], BF16, kind="ExternalOutput")
        dbg_att = nc.dram_tensor("dbg_att", [128, NDC * T], BF16, kind="ExternalOutput")
        dbg_thr = nc.dram_tensor("dbg_thr", [128, 4], F32, kind="ExternalOutput")
        dbg_wfc2 = nc.dram_tensor("dbg_wfc2", [128, 8 * FF], BF16, kind="ExternalOutput")
        dbg_wps = nc.dram_tensor("dbg_wps", [128, 32 * D], BF16, kind="ExternalOutput")
        dbg_cbf = nc.dram_tensor("dbg_cbf", [1, 1024], BF16, kind="ExternalOutput")
        dbg_rb = nc.dram_tensor("dbg_rb", [128, 512], F32, kind="ExternalOutput")
        dbg_sm = nc.dram_tensor("dbg_sm", [128, 64], F32, kind="ExternalOutput")
        dbg_r1 = nc.dram_tensor("dbg_r1", [128, 2048], F32, kind="ExternalOutput")

    AT = nc.alloc_sbuf_tensor_at
    ident_bf = AT("ident_bf", [128, 128], BF16, offset=17024)
    ident_f32 = AT("ident_f32", [128, 128], F32, offset=17280)
    mask_bf = AT("mask_bf", [128, 128], BF16, offset=17792)
    ones_bf = AT("ones_bf", [128, 128], BF16, offset=18048)
    ones_f32 = AT("ones_f32", [128, 128], F32, offset=18304)
    consts = AT("consts", [128, 8], F32, offset=18816)
    sm = AT("sm", [128, 64], F32, offset=18848)
    gp = AT("gp", [128, 72], F32, offset=19104)
    thr = AT("thr", [128, 4], F32, offset=19392)
    lnw_raw = AT("lnw_raw", [32, 128], F32, offset=19424)
    lnb_raw = AT("lnb_raw", [32, 128], F32, offset=19936)
    lnw_sb = AT("lnw_sb", [128, 32], F32, offset=20448)
    lnb_half_bf = AT("lnb_half_bf", [128, 32], BF16, offset=20576)
    gs = AT("gs", [128, 16], F32, offset=20640)
    wob_bf = AT("wob_bf", [1, 1024], BF16, offset=20704)
    wob_f = AT("wob_f", [1, 1024], F32, offset=217728)
    c_bf = AT("c_bf", [1, 1024], BF16, offset=22752)
    QT = AT("QT", [128, NDC * T], BF16, offset=25216)
    wfc2_lo = AT("wfc2_lo", [128, 4 * FF], BF16, offset=25216)
    KT = AT("KT", [128, NDC * T], BF16, offset=57984)
    wfc2_hi = AT("wfc2_hi", [128, 4 * FF], BF16, offset=57984)
    attT = AT("attT", [128, NDC * T], BF16, offset=90752)
    wps_lo = AT("wps_lo", [128, 16 * D], BF16, offset=90752)
    wslot = AT("wslot", [128, 3 * NDC * D], BF16, offset=123520)
    wps_hi = AT("wps_hi", [128, 16 * D], BF16, offset=123520)
    r1buf = AT("r1buf", [128, 2 * D], F32, offset=139904)
    tabC2 = AT("tabC2", [128, 512], F32, offset=92800)
    tabS2 = AT("tabS2", [128, 512], F32, offset=94848)
    sg0 = AT("sg0", [128, FF], BF16, offset=156288)
    sg1 = AT("sg1", [128, FF], BF16, offset=164480)
    h2t = AT("h2t", [128, FF], BF16, offset=156288)
    h2t2 = AT("h2t2", [128, FF], BF16, offset=164480)
    stage_x = AT("stage_x", [128, 2 * D], F32, offset=172672)
    r2buf = AT("r2buf", [128, D], F32, offset=172672)
    outbuf = AT("outbuf", [128, D], F32, offset=176768)
    tabC = AT("tabC", [128, 512], F32, offset=180864)
    tabS = AT("tabS", [128, 512], F32, offset=182912)
    x1tile = AT("x1tile", [128, 4 * D], BF16, offset=217728)
    x1T = AT("x1T", [128, 2 * D], BF16, offset=184960)
    tmps = AT("tmps", [128, 4 * D], BF16, offset=189056)
    h2T_t = AT("h2T", [128, FF], BF16, offset=189056)
    PT = AT("PT", [128, 4 * 512], BF16, offset=197248)
    PT2 = AT("PT2", [128, 4 * 512], BF16, offset=180864)
    gbuf = AT("gbuf", [128, FF], BF16, offset=197248)
    vbuf = AT("vbuf", [128, 2 * 2048], BF16, offset=205440)
    rbA = AT("rbA", [128, 512], F32, offset=213632)
    rbB = AT("rbB", [128, 512], F32, offset=215680)
    wstage = AT("wstage", [128, 2 * D], F32, offset=217728)
    rscr = AT("rscr", [128, 512], F32, offset=213632)
    vstage = AT("vstage", [128, D], BF16, offset=225920)
    x1stage0 = AT("x1stage0", [128, D], BF16, offset=225920)
    x1stage1 = AT("x1stage1", [128, D], BF16, offset=215680)
    sqscr = AT("sqscr", [128, FF], BF16, offset=189056)
    gsta = AT("gsta", [128, 1024], F32, offset=184960)
    gstb = AT("gstb", [128, 1024], F32, offset=201344)
    D0 = nc.alloc_psum_tensor("D0", [128, 1024], F32)
    D1 = nc.alloc_psum_tensor("D1", [128, 1024], F32)
    S0 = nc.alloc_psum_tensor("S0", [128, 512], F32)
    S1 = nc.alloc_psum_tensor("S1", [128, 512], F32)
    pb = [nc.alloc_psum_tensor("pb%d" % i, [128, 1024], BF16) for i in range(2)]

    p = Prog(nc)
    bank = {id(t): Bank() for t in [D0, D1, S0, S1] + pb}

    def B(t):
        return bank[id(t)]

    def mm(out_ap, lhsT_ap, rhs_ap, start, stop, waits=(), inc="T"):
        return p.do("tensor",
                    lambda e, o=out_ap, l=lhsT_ap, r=rhs_ap, s=start, st=stop:
                    e.matmul(o, l, r, start=s, stop=st,
                             skip_group_check=True),
                    waits, inc)

    def produce(t, waits):
        return list(waits) + B(t).cons

    def set_prod(t, tok):
        B(t).prod = tok
        B(t).cons = []

    def consume(t, waits=()):
        return list(waits) + ([B(t).prod] if B(t).prod else [])

    def add_cons(t, tok):
        B(t).cons.append(tok)

    tcb1 = p.dma("sync", lambda e: e.dma_start(
        out=ident_bf[:, :], in_=cpkb_h[:, 0:128]), queue="dL")
    tcb2 = p.dma("sync", lambda e: e.dma_start(
        out=mask_bf[:, :], in_=cpkb_h[:, 128:256]), queue="dL")
    tcb3 = p.dma("sync", lambda e: e.dma_start(
        out=ones_bf[:, :], in_=cpkb_h[:, 256:384]), queue="dL")
    tcf1 = p.dma("sync", lambda e: e.dma_start(
        out=ident_f32[:, :], in_=cpkf_h[:, 0:128]), queue="dL")
    tcf2 = p.dma("sync", lambda e: e.dma_start(
        out=ones_f32[:, :], in_=cpkf_h[:, 128:256]), queue="dL")
    tok_consts = p.dma("sync", lambda e: e.dma_start(
        out=consts[:, :], in_=cpkf_h[:, 256:264]), queue="dL")
    CONSTS = [tcb1, tcb2, tcb3, tcf1, tcf2, tok_consts]

    tok = p.dma("sync", lambda e: e.dma_start(
        out=wob_f[0:1, :], in_=dap(wob_h, 0, [[1024, 1], [1, 1024]])),
        queue="dL")
    tok_wob = p.do("vector", lambda e: e.tensor_copy(
        out=wob_bf[0:1, :], in_=wob_f[0:1, :]), [tok], inc="V")
    tok_lnw = p.dma("sync", lambda e: e.dma_start(
        out=lnw_raw[:, :], in_=dap(lnw_h, 0, [[128, 32], [1, 128]])),
        queue="dL")
    tok_lnb = p.dma("sync", lambda e: e.dma_start(
        out=lnb_raw[:, :], in_=dap(lnb_h, 0, [[128, 32], [1, 128]])),
        queue="dL")
    tk = mm(sap(S0, 0, 128, [[1, 32]]), lnw_raw[0:32, 0:128],
            ident_f32[0:32, 0:32], True, True,
            produce(S0, [tok_lnw] + CONSTS))
    set_prod(S0, tk)
    tk2 = p.do("vector", lambda e: e.tensor_scalar(
        out=lnw_sb[:, :], in0=sap(S0, 0, 128, [[1, 32]]), scalar1=1.0,
        scalar2=None, op0=ALU.mult), consume(S0), inc="V")
    add_cons(S0, tk2)
    tk = mm(sap(S1, 0, 128, [[1, 32]]), lnb_raw[0:32, 0:128],
            ident_f32[0:32, 0:32], True, True,
            produce(S1, [tok_lnb] + CONSTS))
    set_prod(S1, tk)
    tk2 = p.do("vector", lambda e: e.tensor_scalar(
        out=lnb_half_bf[:, :], in0=sap(S1, 0, 128, [[1, 32]]), scalar1=1.0,
        scalar2=None, op0=ALU.mult), consume(S1), inc="V")
    add_cons(S1, tk2)

    wst_cons = [tok_wob, None]
    w_cast_tok = {}
    for wi, wh in enumerate((wq_h, wk_h, wv_h)):
        toks = []
        for dc in range(NDC):
            sl = (wi * NDC + dc) % 2
            tok = p.dma("sync", lambda e, wh=wh, dc=dc, sl=sl: e.dma_start(
                out=wstage[:, sl * D:(sl + 1) * D],
                in_=wh[dc * 128:(dc + 1) * 128, :]),
                [wst_cons[sl]], queue="dL")
            ct = p.do("vector", lambda e, wi=wi, dc=dc, sl=sl: e.tensor_copy(
                out=wslot[:, (wi * NDC + dc) * D:(wi * NDC + dc + 1) * D],
                in_=wstage[:, sl * D:(sl + 1) * D]), [tok], inc="V")
            wst_cons[sl] = ct
            toks.append(ct)
        w_cast_tok[wi] = toks

    xstage_cons = [None, None]
    tab_cons = [None, None]
    QT_toks, KT_toks = [], []
    vhbm_toks = []
    pb_rot = [0]
    lastQmm = [None]
    prev_qkv_mm = [None]

    def emit_transpose8(tsrc, col0, waits):
        t = pb[pb_rot[0] % 2]
        pb_rot[0] += 1
        w = produce(t, list(waits) + CONSTS)
        tk = None
        for j in range(8):
            tk = p.do("tensor",
                      lambda e, t=t, tsrc=tsrc, col0=col0, j=j: e.transpose(
                          sap(t, 0, 128, [[1, 128]], j * 128),
                          tsrc[:, col0 + j * 128: col0 + (j + 1) * 128],
                          ident_bf[:, :]),
                      w if j == 0 else (), inc="T")
        set_prod(t, tk)
        return t

    cur_tab = [tabC, tabS]

    def rope6(ps_half, half, dst, dst_tok_list, waits):
        pt_, cb_ = ps_half
        ev = sap(pt_, 0, 128, [[64, 8], [1, 32]], cb_ + 0)
        od = sap(pt_, 0, 128, [[64, 8], [1, 32]], cb_ + 32)
        tC = sap(cur_tab[0], 0, 128, [[32, 8], [1, 32]], half * 256)
        tS = sap(cur_tab[1], 0, 128, [[32, 8], [1, 32]], half * 256)
        sAap = sap(rscr, 0, 128, [[32, 8], [1, 32]], 0)
        sBap = sap(rscr, 0, 128, [[32, 8], [1, 32]], 256)
        dev = sap(tmps, 0, 128, [[64, 8], [1, 32]], dst + half * 512)
        dod = sap(tmps, 0, 128, [[64, 8], [1, 32]], dst + half * 512 + 32)
        p.do("vector", lambda e: e.tensor_tensor(out=sAap, in0=ev, in1=tC,
                                                 op=ALU.mult), waits, inc="V")
        p.do("vector", lambda e: e.tensor_tensor(out=sBap, in0=od, in1=tS,
                                                 op=ALU.mult), (), inc="V")
        p.do("vector", lambda e: e.tensor_tensor(out=dev, in0=sAap, in1=sBap,
                                                 op=ALU.subtract), (), inc="V")
        p.do("vector", lambda e: e.tensor_tensor(out=sAap, in0=ev, in1=tS,
                                                 op=ALU.mult), (), inc="V")
        p.do("vector", lambda e: e.tensor_tensor(out=sBap, in0=od, in1=tC,
                                                 op=ALU.mult), (), inc="V")
        tk = p.do("vector", lambda e: e.tensor_tensor(out=dod, in0=sAap, in1=sBap,
                                                      op=ALU.add), (), inc="V")
        dst_tok_list.append(tk)
        return tk

    XBF, XT, QROT, KROT = 0, D, 2 * D, 3 * D

    NW = float(FF * D)
    gslot = [gsta, gstb]
    gcons = [None, None]
    gq = [0]
    gred_toks = []

    def emit_gamma_piece(load_ap, col):
        i = gq[0] % 2
        gq[0] += 1
        g = gslot[i]
        tok = p.dma("scalar", lambda e, g=g, load_ap=load_ap: e.dma_start(
            out=g[:, :], in_=load_ap), [gcons[i]], queue="dW", depth=2)
        tk = p.do("scalar", lambda e, g=g, col=col: e.activation(
            out=attT[:, 0:D], in_=g[:, :], func=AF.Abs,
            accum_out=gp[:, col:col + 1]), [tok], inc="S")
        gcons[i] = tk
        gred_toks.append(tk)

    gamma_pieces = []
    for dc in range(NDC):
        for q in range(4):
            gamma_pieces.append((dap(wfcT_h, dc * 128 * FF + q * 1024,
                                     [[FF, 128], [1, 1024]]), dc * 4 + q))
    for hcq in range(NHC):
        gamma_pieces.append((wprojT_h[hcq * 128:(hcq + 1) * 128, :], 32 + hcq))
    gper = -(-len(gamma_pieces) // n_tt)

    for tt in range(n_tt):
        sl = tt % 2
        tok_x = p.dma("sync", lambda e, tt=tt, sl=sl: e.dma_start(
            out=stage_x[:, sl * D:(sl + 1) * D],
            in_=xtt_h[tt * 128:(tt + 1) * 128, :]), [xstage_cons[sl]],
            queue="dL")
        tbC = tabC if tt % 2 == 0 else tabC2
        tbS = tabS if tt % 2 == 0 else tabS2
        tok_tc = p.dma("sync", lambda e, tt=tt, tbC=tbC: e.dma_start(
            out=tbC[:, :], in_=ctab_h[:, tt * 512:(tt + 1) * 512]),
            [tab_cons[tt % 2]], queue="dL")
        tok_ts = p.dma("sync", lambda e, tt=tt, tbS=tbS: e.dma_start(
            out=tbS[:, :], in_=stab_h[:, tt * 512:(tt + 1) * 512]),
            [tab_cons[tt % 2]], queue="dL")
        tok_xT = p.do("scalar", lambda e, sl=sl: e.copy(
            out=tmps[:, XT:XT + D], in_=stage_x[:, sl * D:(sl + 1) * D]),
            [tok_x, prev_qkv_mm[0]], inc="S")
        xstage_cons[sl] = tok_xT
        ps_pairs = {0: ((D0, 0), (D0, 512)), 1: ((D1, 0), (D1, 512)),
                    2: ((S0, 0), (S1, 0))}
        for wi in range(3):
            (tA, cA), (tB, cB) = ps_pairs[wi]
            wA = produce(tA, [tok_xT] + w_cast_tok[wi])
            wB = produce(tB, [tok_xT] + w_cast_tok[wi]) if tB is not tA else []
            tkA = tkB = None
            for dc in range(NDC):
                tkA = mm(tA[:, cA:cA + 512],
                         tmps[:, XT + dc * 128:XT + (dc + 1) * 128],
                         wslot[:, (wi * NDC + dc) * D:(wi * NDC + dc) * D + 512],
                         dc == 0, dc == NDC - 1, wA if dc == 0 else ())
                tkB = mm(tB[:, cB:cB + 512],
                         tmps[:, XT + dc * 128:XT + (dc + 1) * 128],
                         wslot[:, (wi * NDC + dc) * D + 512:(wi * NDC + dc + 1) * D],
                         dc == 0, dc == NDC - 1, wB if dc == 0 else ())
            set_prod(tA, tkA)
            set_prod(tB, tkB)
            prev_qkv_mm[0] = tkB
            if wi == 0:
                lastQmm[0] = tkB
        rope_toks = []
        cur_tab[0], cur_tab[1] = tbC, tbS
        for wi, dst in ((0, QROT), (1, KROT)):
            hA, hB = ps_pairs[wi]
            t1 = rope6(hA, 0, dst, rope_toks, consume(hA[0], [tok_tc, tok_ts]))
            add_cons(hA[0], t1)
            t2 = rope6(hB, 1, dst, rope_toks, consume(hB[0]))
            add_cons(hB[0], t2)
        tab_cons[tt % 2] = rope_toks[-1]
        for dst, res, toks in ((QROT, QT, QT_toks), (KROT, KT, KT_toks)):
            tb = emit_transpose8(tmps, dst, [rope_toks[-1]])
            tk = p.do("vector", lambda e, tb=tb, res=res, tt=tt: e.tensor_copy(
                out=sap(res, 0, 128, [[T, 8], [1, 128]], tt * 128),
                in_=sap(tb, 0, 128, [[1, 1024]])), consume(tb), inc="V")
            add_cons(tb, tk)
            toks.append(tk)
        tkv = p.do("vector", lambda e: e.tensor_copy(
            out=vstage[:, 0:512], in_=S0[:, :]),
            consume(S0, [vhbm_toks[-1] if vhbm_toks else None]), inc="V")
        add_cons(S0, tkv)
        tkv2 = p.do("vector", lambda e: e.tensor_copy(
            out=vstage[:, 512:1024], in_=S1[:, :]), consume(S1), inc="V")
        add_cons(S1, tkv2)
        tkst = p.dma("gpsimd", lambda e, tt=tt: e.dma_start(
            out=vhbm[tt * 128:(tt + 1) * 128, :], in_=vstage[:, :]),
            [tkv, tkv2], queue="dG", depth=2)
        vhbm_toks.append(tkst)
        for piece in gamma_pieces[tt * gper:(tt + 1) * gper]:
            emit_gamma_piece(*piece)

    tok_all_vhbm = vhbm_toks[-1]
    t_lastQKV = prev_qkv_mm[0]
    t_lastQmm = lastQmm[0]
    if debug:
        p.dma("gpsimd", lambda e: e.dma_start(out=dbg_qt[:, :], in_=QT[:, :]),
             QT_toks, queue="dG", depth=2)
        p.dma("gpsimd", lambda e: e.dma_start(out=dbg_kt[:, :], in_=KT[:, :]),
             KT_toks, queue="dG", depth=2)

    wo_toks = []
    for dc in range(NDC):
        sl = dc % 2
        tok = p.dma("sync", lambda e, dc=dc, sl=sl: e.dma_start(
            out=wstage[:, sl * D:(sl + 1) * D],
            in_=wo_h[dc * 128:(dc + 1) * 128, :]),
            [wst_cons[sl], t_lastQmm], queue="dL")
        ct = p.do("vector", lambda e, dc=dc, sl=sl: e.tensor_copy(
            out=wslot[:, dc * D:(dc + 1) * D],
            in_=wstage[:, sl * D:(sl + 1) * D]), [tok, t_lastQmm], inc="V")
        wst_cons[sl] = ct
        wo_toks.append(ct)

    t1 = p.do("vector", lambda e: e.reduce_sum(
        gp[:, 64:65], gp[:, 0:32], axis=AX.X), [gred_toks[31]], inc="V")
    t2 = p.do("vector", lambda e: e.reduce_sum(
        gp[:, 65:66], gp[:, 32:64], axis=AX.X), [gred_toks[-1]], inc="V")
    tk = mm(sap(S0, 0, 2, [[1, 128]]), gp[:, 64:66], ident_f32[:, :],
            True, True, produce(S0, [t1, t2]))
    set_prod(S0, tk)
    tg = p.do("vector", lambda e: e.tensor_reduce(
        gp[0:2, 66:67], sap(S0, 0, 2, [[1, 128]]), axis=AX.X, op=ALU.add),
        consume(S0), inc="V")
    add_cons(S0, tg)
    tk2 = mm(sap(S1, 0, 1, [[1, 2]]), gp[0:2, 66:67], ident_f32[0:2, 0:2],
             True, True, produce(S1, [tg]))
    set_prod(S1, tk2)
    tg2 = p.do("vector", lambda e: e.tensor_scalar(
        out=gp[0:1, 68:70], in0=sap(S1, 0, 1, [[1, 2]]), scalar1=0.5 / NW,
        scalar2=5e-7, op0=ALU.mult, op1=ALU.add), consume(S1), inc="V")
    add_cons(S1, tg2)
    tk = None
    for j in range(2):
        tk = mm(sap(S1, 0, 128, [[1, 1]], j), ones_f32[0:1, 0:128],
                gp[0:1, 68 + j:69 + j], j == 0, j == 1,
                produce(S1, [tg2]) if j == 0 else [tg2])
    set_prod(S1, tk)
    tb_ = p.do("vector", lambda e: e.tensor_copy(
        out=thr[:, 0:2], in_=sap(S1, 0, 128, [[1, 2]])),
        consume(S1), inc="V")
    add_cons(S1, tb_)
    tbn = p.do("vector", lambda e: e.tensor_scalar(
        out=thr[:, 2:4], in0=sap(S1, 0, 128, [[1, 2]]), scalar1=-1.0,
        scalar2=None, op0=ALU.mult), consume(S1), inc="V")
    add_cons(S1, tbn)
    TH = [tb_, tbn]

    wfc2_toks = []

    def emit_wfcq(dc, q, gate):
        i = gq[0] % 2
        gq[0] += 1
        g = gslot[i]
        tok = p.dma("scalar", lambda e, g=g, dc=dc, q=q: e.dma_start(
            out=g[:, :], in_=dap(wfcT_h, dc * 128 * FF + q * 1024,
                                 [[FF, 128], [1, 1024]])),
            [gcons[i]], queue="dW", depth=2)
        tm0 = p.do("vector", lambda e, g=g: e.tensor_scalar(
            out=sg0[:, 0:D], in0=g[:, :], scalar1=thr[:, 0:1], scalar2=None,
            op0=ALU.is_gt), [tok, t_lastQKV] + TH, inc="V")
        tm1 = p.do("vector", lambda e, g=g: e.tensor_scalar(
            out=sg1[:, 0:D], in0=g[:, :], scalar1=thr[:, 2:3], scalar2=None,
            op0=ALU.is_lt), (), inc="V")
        dst = wfc2_lo if dc < 4 else wfc2_hi
        dco = dc % 4
        tw = p.do("vector", lambda e, dst=dst, dco=dco, q=q: e.tensor_tensor(
            out=dst[:, dco * FF + q * D:dco * FF + (q + 1) * D],
            in0=sg0[:, 0:D], in1=sg1[:, 0:D], op=ALU.subtract),
            [tm0, tm1, gate], inc="V")
        gcons[i] = tw
        wfc2_toks.append(tw)

    wfcq_sched = {}
    for dc in range(NDC):
        for q in range(4):
            last_col = (dc % 4) * 4096 + q * 1024 + 1023
            gh = min(H - 1, 2 * (last_col // T) + 1)
            wfcq_sched.setdefault(gh, []).append((dc, q))

    vbuf_cons = [[], [], [], []]
    vbuf_tok = [None, None, None, None]
    PT_cons = [None] * 8
    PTS = lambda s: (PT if s < 4 else PT2, (s % 4) * 512)
    attT_toks = []
    rb_cons = [None, None]
    head_st_tok = {}
    lastST = [None]
    for hcp in range(H // 2):
        hA, hB = 2 * hcp, 2 * hcp + 1
        hc = hcp
        grp = max(1, n_tt // 4)
        slA, slB = 2 * (hcp % 2), 2 * (hcp % 2) + 1
        for sl_, hh in ((slA, hA), (slB, hB)):
            tok = p.dma("sync", lambda e, hh=hh, sl_=sl_: e.dma_start(
                out=bass.AP(vbuf, sl_ * 1024,
                            [[4096, 128], [64, n_tt], [1, 64]]),
                in_=bass.AP(vhbm, hh * 64,
                            [[D, 128], [128 * D, n_tt], [1, 64]])),
                [tok_all_vhbm] + vbuf_cons[sl_], queue="dL")
            vbuf_cons[sl_] = []
            vbuf_tok[sl_] = [tok]
        for qg in range(NQG):
            nk = 4 * (qg + 1)
            pend = []
            pv_tok = {hA: None, hB: None}
            first_pv = {(hA, 'main'): True, (hA, 'ones'): True,
                        (hB, 'main'): True, (hB, 'ones'): True}

            def pv_half(head, kt, tkE, pts, c0, kind):
                hp = (head % 2) * 64
                sl_ = slA if head == hA else slB
                ppv = S0 if head == hA else S1
                ptt, ptc = PTS(pts)
                vap = vbuf[:, sl_ * 1024 + kt * 64:sl_ * 1024 + (kt + 1) * 64]
                start = first_pv[(head, kind)]
                first_pv[(head, kind)] = False
                stop = kt == nk - 1
                w = produce(ppv, [tkE] + vbuf_tok[sl_]) if start else [tkE]
                if kind == "ones":
                    tok = p.do("tensor",
                               lambda e, ppv=ppv, hp=hp, ptt=ptt, ptc=ptc,
                               c0=c0, start=start, stop=stop:
                               e.matmul(ppv[64 - hp:128 - hp, c0:512],
                                        ones_bf[:, 0:64],
                                        ptt[:, ptc + c0:ptc + 512],
                                        start=start, stop=stop,
                                        skip_group_check=True,
                                        tile_position=(0, 64 - hp)),
                               w, inc="T")
                else:
                    tok = p.do("tensor",
                               lambda e, ppv=ppv, hp=hp, ptt=ptt, ptc=ptc,
                               c0=c0, vap=vap, start=start, stop=stop:
                               e.matmul(ppv[hp:hp + 64, c0:512], vap,
                                        ptt[:, ptc + c0:ptc + 512],
                                        start=start, stop=stop,
                                        skip_group_check=True,
                                        tile_position=(0, hp)),
                               w, inc="T")
                PT_cons[pts] = tok
                return tok

            def flush_pend():
                pend.sort(key=lambda it: it[1])
                while pend:
                    grp = [pend.pop(0)]
                    while pend and pend[0][1] == grp[0][1]:
                        grp.append(pend.pop(0))
                    for kind in ("main", "ones"):
                        for (h_, kt_, tkE_, pts_, c0_) in grp:
                            pv_tok[h_] = pv_half(h_, kt_, tkE_, pts_, c0_,
                                                 kind)

            for kp in range(0, nk, 2):
                sbase = (kp % 4 // 2) * 4
                stoks = {}
                c0ss = {}
                for kt in (kp, kp + 1):
                    for head, dd in ((hA, D0), (hB, D1)):
                        hp = (head % 2) * 64
                        c0 = max(0, kt * 128 - qg * 512)
                        cb = (kt - kp) * 512
                        ncols = 512 - c0
                        diag = kt >= 4 * qg
                        w = produce(dd, QT_toks + KT_toks + CONSTS) \
                            if kt == kp else ()
                        tkS = mm(dd[:, cb + c0:cb + 512],
                                 KT[hp:hp + 64,
                                    hc * T + kt * 128:hc * T + (kt + 1) * 128],
                                 sap(QT, hp, 64, [[1, ncols]],
                                     hc * T + qg * 512 + c0),
                                 True, not diag, w)
                        if diag:
                            tkS = mm(dd[:, cb + c0:cb + c0 + 128],
                                     ident_bf[:, :], mask_bf[:, :],
                                     False, True, ())
                        stoks[head] = tkS
                        c0ss.setdefault(head, []).append(c0)
                if pend:
                    flush_pend()
                for head, dd, soff in ((hA, D0, 0), (hB, D1, 2)):
                    hp = (head % 2) * 64
                    tkS = stoks[head]
                    c0s = c0ss[head]
                    set_prod(dd, tkS)
                    lastST[0] = tkS
                    head_st_tok[head] = tkS
                    pts = sbase + soff
                    ptt, ptc = PTS(pts)
                    if c0s[0] == 0 and c0s[1] == 0:
                        tkE = p.do("scalar",
                                   lambda e, dd=dd, ptt=ptt, ptc=ptc:
                                   e.activation(
                                       out=ptt[:, ptc:ptc + 1024],
                                       in_=dd[:, :], func=AF.Exp,
                                       bias=consts[:, 0:1], scale=0.125),
                                   consume(dd, [PT_cons[pts],
                                                PT_cons[pts + 1]] + CONSTS),
                                   inc="S")
                    else:
                        for j in range(2):
                            c0 = c0s[j]
                            tkE = p.do("scalar",
                                       lambda e, dd=dd, ptt=ptt, ptc=ptc,
                                       j=j, c0=c0:
                                       e.activation(
                                           out=ptt[:, ptc + j * 512 + c0:
                                                   ptc + (j + 1) * 512],
                                           in_=dd[:, j * 512 + c0:
                                                  (j + 1) * 512],
                                           func=AF.Exp, bias=consts[:, 0:1],
                                           scale=0.125),
                                       consume(dd, [PT_cons[pts + j]]
                                               + CONSTS), inc="S")
                    add_cons(dd, tkE)
                    pend.append((head, kp, tkE, pts, c0s[0]))
                    pend.append((head, kp + 1, tkE, pts + 1, c0s[1]))
            flush_pend()
            for head in (hA, hB):
                hp = (head % 2) * 64
                ppv = S0 if head == hA else S1
                sl_ = slA if head == hA else slB
                set_prod(ppv, pv_tok[head])
                vbuf_cons[sl_].append(pv_tok[head])
                sums_base = 64 - hp
                tkL = p.do("scalar", lambda e, ppv=ppv, sb=sums_base, hp=hp:
                           e.activation(out=rbA[hp:hp + 64, :],
                                        in_=ppv[sb:sb + 64, :], func=AF.Ln),
                           consume(ppv), inc="S")
                tkX = p.do("scalar", lambda e, hp=hp: e.activation(
                    out=rbB[hp:hp + 64, :], in_=rbA[hp:hp + 64, :],
                    func=AF.Exp, scale=-1.0),
                    [tkL, rb_cons[head % 2]], inc="S")
                tkA = p.do("vector", lambda e, ppv=ppv, hp=hp, hc=hc, qg=qg:
                           e.tensor_tensor(
                               out=attT[hp:hp + 64,
                                        hc * T + qg * 512:
                                        hc * T + (qg + 1) * 512],
                               in0=ppv[hp:hp + 64, :], in1=rbB[hp:hp + 64, :],
                               op=ALU.mult), [tkX], inc="V")
                add_cons(ppv, tkA)
                rb_cons[head % 2] = tkA
                attT_toks.append(tkA)
        for hh in (hA, hB):
            for (dcq, qq) in wfcq_sched.get(hh, []):
                emit_wfcq(dcq, qq, head_st_tok[hh])

    t_lastST = lastST[0]
    if debug:
        tda = p.dma("gpsimd", lambda e: e.dma_start(
            out=dbg_att[:, :], in_=attT[:, :]), attT_toks, queue="dG", depth=2)
        p.dma("gpsimd", lambda e: e.dma_start(out=dbg_rb[:, :], in_=rbB[:, :]),
             attT_toks, queue="dG", depth=2)
        attT_toks = attT_toks + [tda]

    xrel_cons = list(xstage_cons)
    x1st_cons = [tok_all_vhbm, tok_all_vhbm]
    r1_cons = [None, None]
    x1_toks = []
    lastWO = [None]
    for ti in range(n_tt):
        sl = ti % 2
        tok_x = p.dma("sync", lambda e, ti=ti, sl=sl: e.dma_start(
            out=stage_x[:, sl * D:(sl + 1) * D],
            in_=x_h[ti * 128:(ti + 1) * 128, :]), [xrel_cons[sl]],
            queue="dL")
        DW = D0 if ti % 2 == 0 else D1
        wA = produce(DW, attT_toks + wo_toks)
        for fc in range(NDC):
            la = attT[:, fc * T + ti * 128:fc * T + (ti + 1) * 128]
            mm(DW[:, 0:512], la, wslot[:, fc * D:fc * D + 512],
               fc == 0, False, wA if fc == 0 else ())
            mm(DW[:, 512:1024], la, wslot[:, fc * D + 512:(fc + 1) * D],
               fc == 0, False, ())
        mm(DW[:, 0:512], ones_bf[0:1, 0:128], wob_bf[0:1, 0:512], False, True,
           [tok_wob])
        tkB = mm(DW[:, 512:1024], ones_bf[0:1, 0:128], wob_bf[0:1, 512:1024],
                 False, True, ())
        set_prod(DW, tkB)
        lastWO[0] = tkB
        tr1a = p.do("vector", lambda e, sl=sl, DW=DW: e.tensor_add(
            out=r1buf[:, sl * D:sl * D + 512], in0=DW[:, 0:512],
            in1=stage_x[:, sl * D:sl * D + 512]),
            consume(DW, [tok_x, r1_cons[sl]]), inc="V")
        tr1b = p.do("vector", lambda e, sl=sl, DW=DW: e.tensor_add(
            out=r1buf[:, sl * D + 512:(sl + 1) * D], in0=DW[:, 512:1024],
            in1=stage_x[:, sl * D + 512:(sl + 1) * D]), (), inc="V")
        add_cons(DW, tr1b)
        xrel_cons[sl] = tr1b
        r1 = r1buf[:, sl * D:(sl + 1) * D]
        pc = 8 * (ti % 2)
        s1, s2 = sm[:, pc:pc + 1], sm[:, pc + 1:pc + 2]
        mu, m2 = sm[:, pc + 2:pc + 3], sm[:, pc + 3:pc + 4]
        var, rstd = sm[:, pc + 4:pc + 5], sm[:, pc + 5:pc + 6]
        nb = sm[:, pc + 6:pc + 7]
        ts1 = p.do("vector", lambda e, r1=r1, s1=s1: e.reduce_sum(
            s1, r1, axis=AX.X), [tr1a, tr1b], inc="V")
        ts2 = p.do("scalar", lambda e, r1=r1, s2=s2: e.activation(
            out=sqscr[:, 0:D], in_=r1, func=AF.Square, accum_out=s2),
            [tr1a, tr1b], inc="S")
        p.do("vector", lambda e, s2=s2, m2=m2: e.tensor_scalar(
            out=m2, in0=s2, scalar1=float(D), scalar2=None, op0=ALU.mult),
            [ts1, ts2], inc="V")
        tv = p.do("vector", lambda e, s1=s1, m2=m2, var=var:
                  e.scalar_tensor_tensor(
                      out=var, in0=s1, scalar=s1, in1=m2,
                      op0=ALU.mult, op1=ALU.subtract), (), inc="V")
        tl = p.do("scalar", lambda e, var=var, mu=mu: e.activation(
            out=mu, in_=var, func=AF.Ln, bias=consts[:, 1:2],
            scale=-1.0 / (D * D)), [tv] + CONSTS, inc="S")
        tl2 = p.do("scalar", lambda e, mu=mu, rstd=rstd: e.activation(
            out=rstd, in_=mu, func=AF.Exp, scale=-0.5), [tl], inc="S")
        tnb = p.do("vector", lambda e, s1=s1, nb=nb, rstd=rstd: e.tensor_scalar(
            out=nb, in0=s1, scalar1=rstd, scalar2=-1.0 / D, op0=ALU.mult,
            op1=ALU.mult), [tl2], inc="V")
        x1st = x1stage0 if sl == 0 else x1stage1
        tap = p.do("scalar", lambda e, r1=r1, rstd=rstd, nb=nb, x1st=x1st:
                   e.activation(
                       out=x1st[:, :], in_=r1,
                       func=AF.Identity, scale=rstd, bias=nb),
                   [tnb, x1st_cons[sl]], inc="S")
        r1_cons[sl] = tap
        tst = p.dma("gpsimd", lambda e, ti=ti, x1st=x1st: e.dma_start(
            out=x1hbm[ti * 128:(ti + 1) * 128, :],
            in_=x1st[:, :]), [tap], queue="dG", depth=2)
        x1st_cons[sl] = tst
        x1_toks.append(tst)
    t_lastWO = lastWO[0]
    t_x1_done = x1_toks[-1]
    if debug:
        p.dma("gpsimd", lambda e: e.dma_start(out=dbg_sm[:, :], in_=sm[:, :]),
             [t_x1_done], queue="dG", depth=2)
        p.dma("gpsimd", lambda e: e.dma_start(out=dbg_r1[:, :], in_=r1buf[:, :]),
             [t_x1_done], queue="dG", depth=2)

    wps_toks = []
    ctk = None
    tkc = None
    for hc in range(NHC):
        i = gq[0] % 2
        gq[0] += 1
        g = gslot[i]
        tok = p.dma("scalar", lambda e, g=g, hc=hc: e.dma_start(
            out=g[:, :], in_=wprojT_h[hc * 128:(hc + 1) * 128, :]),
            [gcons[i]], queue="dW", depth=2)
        tm0 = p.do("vector", lambda e, g=g: e.tensor_scalar(
            out=sg0[:, 0:D], in0=g[:, :], scalar1=thr[:, 1:2], scalar2=None,
            op0=ALU.is_gt), [tok] + TH, inc="V")
        tm1 = p.do("vector", lambda e, g=g: e.tensor_scalar(
            out=sg1[:, 0:D], in0=g[:, :], scalar1=thr[:, 3:4], scalar2=None,
            op0=ALU.is_lt), (), inc="V")
        dst = wps_lo if hc < 16 else wps_hi
        hco = hc % 16
        gate = t_lastWO if hc < 16 else t_x1_done
        tw = p.do("vector", lambda e, dst=dst, hco=hco: e.tensor_tensor(
            out=dst[:, hco * D:(hco + 1) * D], in0=sg0[:, 0:D],
            in1=sg1[:, 0:D], op=ALU.subtract), [tm0, tm1, gate], inc="V")
        gcons[i] = tw
        for half, ct_ps in ((0, S0), (1, S1)):
            w = produce(ct_ps, [tw]) if hc == 0 else [tw]
            ctk = mm(sap(ct_ps, 0, 1, [[1, 512]]),
                     lnb_half_bf[:, hc:hc + 1],
                     dst[:, hco * D + half * 512:hco * D + (half + 1) * 512],
                     hc == 0, hc == NHC - 1, w)
        tsc = p.do("vector", lambda e, dst=dst, hco=hco, hc=hc: e.tensor_scalar(
            out=dst[:, hco * D:(hco + 1) * D],
            in0=dst[:, hco * D:(hco + 1) * D],
            scalar1=lnw_sb[:, hc:hc + 1], scalar2=None, op0=ALU.mult),
            [ctk], inc="V")
        wps_toks.append(tsc)
    set_prod(S0, ctk)
    set_prod(S1, ctk)
    tkc = None
    for half, ct_ps in ((0, S0), (1, S1)):
        tkc = p.do("scalar", lambda e, half=half, ct_ps=ct_ps: e.activation(
            out=c_bf[0:1, half * 512:(half + 1) * 512],
            in_=sap(ct_ps, 0, 1, [[1, 512]]), func=AF.Identity),
            consume(ct_ps), inc="S")
        add_cons(ct_ps, tkc)
    t_quant = [wfc2_toks[-1], wps_toks[-1], tkc]
    if debug:
        p.dma("gpsimd", lambda e: e.dma_start(out=dbg_thr[:, :], in_=thr[:, :]),
             TH, queue="dG", depth=2)
        p.dma("gpsimd", lambda e: e.dma_start(
            out=dap(dbg_wfc2, 0, [[8 * FF, 128], [1, 4 * FF]]),
            in_=wfc2_lo[:, :]), wfc2_toks, queue="dG", depth=2)
        p.dma("gpsimd", lambda e: e.dma_start(
            out=dap(dbg_wfc2, 4 * FF, [[8 * FF, 128], [1, 4 * FF]]),
            in_=wfc2_hi[:, :]), wfc2_toks, queue="dG", depth=2)
        p.dma("gpsimd", lambda e: e.dma_start(
            out=dap(dbg_wps, 0, [[32 * D, 128], [1, 16 * D]]),
            in_=wps_lo[:, :]), wps_toks, queue="dG", depth=2)
        p.dma("gpsimd", lambda e: e.dma_start(
            out=dap(dbg_wps, 16 * D, [[32 * D, 128], [1, 16 * D]]),
            in_=wps_hi[:, :]), wps_toks, queue="dG", depth=2)
        tcb = p.dma("gpsimd", lambda e: e.dma_start(
            out=dbg_cbf[0:1, :], in_=c_bf[0:1, :]), [tkc], queue="dG", depth=2)
        t_quant = t_quant + [tcb]

    x1t_cons = [None, None, None, None]
    x1T_mm_cons = [None, None]
    prev_proj_mm = [None]
    prev_h2_transp = [None, None]
    out_cons = [None, None]
    out_toks = []
    state = {}

    def p7_stage_a1(tt):
        sl = tt % 2
        sl4 = tt % 4
        tok_x1 = p.dma("sync", lambda e, tt=tt, sl4=sl4: e.dma_start(
            out=x1tile[:, sl4 * D:(sl4 + 1) * D],
            in_=x1hbm[tt * 128:(tt + 1) * 128, :]),
            [x1_toks[tt], x1t_cons[sl4]], queue="dL")
        tb = emit_transpose8(x1tile, sl4 * D, [tok_x1])
        tok_x1T = p.do("vector", lambda e, tb=tb, sl=sl: e.tensor_copy(
            out=x1T[:, sl * D:(sl + 1) * D], in_=sap(tb, 0, 128, [[1, 1024]])),
            consume(tb, [x1T_mm_cons[sl]]), inc="V")
        add_cons(tb, tok_x1T)
        return tok_x1T

    def p7_stage_a(tt, tok_x1T):
        sl = tt % 2
        st = {}
        gelu_toks = []
        pc = 8 * (tt % 2)
        for hb in range(0, NHB, 2):
            dd = [D0, D1][(hb // 2) % 2]
            w = produce(dd, [tok_x1T] + wfc2_toks)
            tkF = None
            for dc in range(NDC):
                wsrc = wfc2_lo if dc < 4 else wfc2_hi
                dco = dc % 4
                tkF = mm(dd[:, 0:512],
                         x1T[:, sl * D + dc * 128:sl * D + (dc + 1) * 128],
                         wsrc[:, dco * FF + hb * 512:dco * FF + (hb + 1) * 512],
                         dc == 0, False, w if dc == 0 else ())
                tkF = mm(dd[:, 512:1024],
                         x1T[:, sl * D + dc * 128:sl * D + (dc + 1) * 128],
                         wsrc[:, dco * FF + (hb + 1) * 512:
                              dco * FF + (hb + 2) * 512],
                         dc == 0, dc == NDC - 1, ())
            set_prod(dd, tkF)
            x1T_mm_cons[sl] = tkF
            tkG = p.do("scalar", lambda e, dd=dd, hb=hb, pc=pc: e.activation(
                out=gbuf[:, hb * 512:(hb + 2) * 512], in_=dd[:, :],
                func=AF.Gelu, scale=1.0,
                accum_out=gs[:, pc + hb // 2:pc + hb // 2 + 1]),
                consume(dd), inc="S")
            add_cons(dd, tkG)
            tkQ = p.do("scalar", lambda e, hb=hb, pc=pc: e.activation(
                out=sqscr[:, hb * 512:(hb + 2) * 512],
                in_=gbuf[:, hb * 512:(hb + 2) * 512], func=AF.Square,
                accum_out=gs[:, pc + 4 + hb // 2:pc + 5 + hb // 2]),
                [tkG], inc="S")
            gelu_toks.append(tkQ)
        if tt + 1 < n_tt:
            a1_toks[tt + 1] = p7_stage_a1(tt + 1)
        pcs = 16 + 8 * (tt % 2)
        s1, s2 = sm[:, pcs:pcs + 1], sm[:, pcs + 1:pcs + 2]
        m2 = sm[:, pcs + 3:pcs + 4]
        var, rstd = sm[:, pcs + 4:pcs + 5], sm[:, pcs + 5:pcs + 6]
        mu, nb = sm[:, pcs + 2:pcs + 3], sm[:, pcs + 6:pcs + 7]
        ts1 = p.do("vector", lambda e, pc=pc, s1=s1: e.reduce_sum(
            s1, gs[:, pc:pc + 4], axis=AX.X), gelu_toks, inc="V")
        ts2 = p.do("vector", lambda e, pc=pc, s2=s2: e.reduce_sum(
            s2, gs[:, pc + 4:pc + 8], axis=AX.X), gelu_toks, inc="V")
        p.do("vector", lambda e, s2=s2, m2=m2: e.tensor_scalar(
            out=m2, in0=s2, scalar1=float(FF), scalar2=None, op0=ALU.mult),
            [ts1, ts2], inc="V")
        tv = p.do("vector", lambda e, s1=s1, m2=m2, var=var:
                  e.scalar_tensor_tensor(
                      out=var, in0=s1, scalar=s1, in1=m2,
                      op0=ALU.mult, op1=ALU.subtract), (), inc="V")
        tl = p.do("scalar", lambda e, var=var, mu=mu: e.activation(
            out=mu, in_=var, func=AF.Ln, bias=consts[:, 1:2],
            scale=-1.0 / (float(FF) * FF)), [tv], inc="S")
        tl2 = p.do("scalar", lambda e, mu=mu, rstd=rstd: e.activation(
            out=rstd, in_=mu, func=AF.Exp, scale=-0.5), [tl], inc="S")
        tnb = p.do("vector", lambda e, s1=s1, nb=nb, rstd=rstd: e.tensor_scalar(
            out=nb, in0=s1, scalar1=rstd, scalar2=-1.0 / FF, op0=ALU.mult,
            op1=ALU.mult), [tl2], inc="V")
        h2dst = h2t if sl == 0 else h2t2
        th2a = p.do("scalar", lambda e, rstd=rstd, nb=nb, h2dst=h2dst:
                    e.activation(
                        out=h2dst[:, 0:2048], in_=gbuf[:, 0:2048],
                        func=AF.Identity, scale=rstd, bias=nb),
                    [tnb, prev_h2_transp[sl]] + t_quant, inc="S")
        th2b = p.do("scalar", lambda e, rstd=rstd, nb=nb, h2dst=h2dst:
                    e.activation(
                        out=h2dst[:, 2048:4096], in_=gbuf[:, 2048:4096],
                        func=AF.Identity, scale=rstd, bias=nb),
                    (), inc="S")
        st["th2"] = (th2a, th2b)
        st["sl"] = sl
        return st

    def p7_stage_b(tt, st):
        sl = st["sl"]
        h2src = h2t if sl == 0 else h2t2
        h2T_toks = []
        for g4 in range(4):
            tbk = pb[pb_rot[0] % 2]
            pb_rot[0] += 1
            w = produce(tbk, [st["th2"][0 if g4 < 2 else 1]])
            tkT = None
            for j in range(8):
                cj = g4 * 8 + j
                tkT = p.do("tensor",
                           lambda e, tbk=tbk, cj=cj, j=j, h2src=h2src:
                           e.transpose(
                               sap(tbk, 0, 128, [[1, 128]], j * 128),
                               h2src[:, cj * 128:(cj + 1) * 128],
                               ident_bf[:, :]),
                           w if j == 0 else (), inc="T")
            set_prod(tbk, tkT)
            prev_h2_transp[sl] = tkT
            tke = p.do("vector", lambda e, tbk=tbk, g4=g4: e.tensor_copy(
                out=h2T_t[:, g4 * 1024:(g4 + 1) * 1024],
                in_=sap(tbk, 0, 128, [[1, 1024]])),
                consume(tbk, [prev_proj_mm[0]]), inc="V")
            add_cons(tbk, tke)
            h2T_toks.append(tke)
        wA = produce(S0, [h2T_toks[0]] + wps_toks)
        wB = produce(S1, [h2T_toks[0]] + wps_toks)
        for hc in range(NHC):
            wsrc = wps_lo if hc < 16 else wps_hi
            hco = hc % 16
            la = h2T_t[:, hc * 128:(hc + 1) * 128]
            wg = (wA if hc == 0 else
                  [h2T_toks[hc // 8]] if hc % 8 == 0 else ())
            mm(S0[:, :], la, wsrc[:, hco * D:hco * D + 512],
               hc == 0, False, wg)
            mm(S1[:, :], la, wsrc[:, hco * D + 512:(hco + 1) * D],
               hc == 0, False, wB if hc == 0 else ())
        tkA = mm(S0[:, :], ones_bf[0:1, 0:128], c_bf[0:1, 0:512], False, True,
                 [tkc])
        tkB = mm(S1[:, :], ones_bf[0:1, 0:128], c_bf[0:1, 512:1024],
                 False, True, ())
        set_prod(S0, tkA)
        set_prod(S1, tkB)
        prev_proj_mm[0] = tkB
        sl4 = tt % 4
        tr2a = p.do("vector", lambda e, sl4=sl4: e.tensor_add(
            out=r2buf[:, 0:512], in0=S0[:, :],
            in1=x1tile[:, sl4 * D:sl4 * D + 512]),
            consume(S0, [out_cons[0]]), inc="V")
        add_cons(S0, tr2a)
        tr2b = p.do("vector", lambda e, sl4=sl4: e.tensor_add(
            out=r2buf[:, 512:1024], in0=S1[:, :],
            in1=x1tile[:, sl4 * D + 512:(sl4 + 1) * D]),
            consume(S1, [out_cons[0]]), inc="V")
        add_cons(S1, tr2b)
        x1t_cons[sl4] = tr2b
        pcs = 32 + 8 * (tt % 2)
        s1, s2 = sm[:, pcs:pcs + 1], sm[:, pcs + 1:pcs + 2]
        m2 = sm[:, pcs + 3:pcs + 4]
        var, rstd = sm[:, pcs + 4:pcs + 5], sm[:, pcs + 5:pcs + 6]
        mu, nb = sm[:, pcs + 2:pcs + 3], sm[:, pcs + 6:pcs + 7]
        ts1 = p.do("vector", lambda e, s1=s1: e.reduce_sum(
            s1, r2buf[:, :], axis=AX.X), [tr2a, tr2b], inc="V")
        ts2 = p.do("scalar", lambda e, s2=s2: e.activation(
            out=sqscr[:, 0:D], in_=r2buf[:, :], func=AF.Square, accum_out=s2),
            [tr2a, tr2b], inc="S")
        p.do("vector", lambda e, s2=s2, m2=m2: e.tensor_scalar(
            out=m2, in0=s2, scalar1=float(D), scalar2=None, op0=ALU.mult),
            [ts1, ts2], inc="V")
        tv = p.do("vector", lambda e, s1=s1, m2=m2, var=var:
                  e.scalar_tensor_tensor(
                      out=var, in0=s1, scalar=s1, in1=m2,
                      op0=ALU.mult, op1=ALU.subtract), (), inc="V")
        tl = p.do("scalar", lambda e, var=var, mu=mu: e.activation(
            out=mu, in_=var, func=AF.Ln, bias=consts[:, 1:2],
            scale=-1.0 / (D * D)), [tv], inc="S")
        tl2 = p.do("scalar", lambda e, mu=mu, rstd=rstd: e.activation(
            out=rstd, in_=mu, func=AF.Exp, scale=-0.5), [tl], inc="S")
        tnb = p.do("vector", lambda e, s1=s1, nb=nb, rstd=rstd: e.tensor_scalar(
            out=nb, in0=s1, scalar1=rstd, scalar2=-1.0 / D, op0=ALU.mult,
            op1=ALU.mult), [tl2], inc="V")
        tout = p.do("scalar", lambda e, rstd=rstd, nb=nb: e.activation(
            out=outbuf[:, :], in_=r2buf[:, :], func=AF.Identity, scale=rstd,
            bias=nb), [tnb, out_cons[1]], inc="S")
        tst = p.dma("gpsimd", lambda e, tt=tt: e.dma_start(
            out=out_h[tt * 128:(tt + 1) * 128, :], in_=outbuf[:, :]),
            [tout], queue="dO", depth=2)
        out_cons[0] = tst
        out_cons[1] = tst
        out_toks.append(tst)

    prev = None
    a1_toks = {0: None}
    a1_toks[0] = p7_stage_a1(0)
    for tt in range(n_tt):
        cur = p7_stage_a(tt, a1_toks.pop(tt))
        if prev is not None:
            p7_stage_b(tt - 1, prev)
        prev = cur
    p7_stage_b(n_tt - 1, prev)

    p.wait_only("sync", out_toks[-2:])

    with nc.Block() as block:
        @block.sync
        def _(e):
            p.replay("sync", e)

        @block.tensor
        def _(e):
            p.replay("tensor", e)

        @block.vector
        def _(e):
            p.replay("vector", e)

        @block.scalar
        def _(e):
            p.replay("scalar", e)

        @block.gpsimd
        def _(e):
            p.replay("gpsimd", e)

    return nc


def make_xtt(x2d, n_tt=16):
    T = n_tt * 128
    return np.ascontiguousarray(
        x2d.reshape(n_tt, 128, NDC, 128).transpose(0, 3, 2, 1).reshape(T, D))


def host_prep(wq, wk, wv, wo, wob, wfc, lnw, lnb, wproj, n_tt=16):
    T = n_tt * 128
    perm = np.concatenate([np.concatenate([64 * h + np.arange(0, 64, 2),
                                           64 * h + np.arange(1, 64, 2)])
                           for h in range(H)])
    d = dict(
        wq=np.ascontiguousarray(np.asarray(wq, np.float32)[:, perm]),
        wk=np.ascontiguousarray(np.asarray(wk, np.float32)[:, perm]),
        wv=np.ascontiguousarray(np.asarray(wv, np.float32)),
        wo=np.ascontiguousarray(np.asarray(wo, np.float32)),
        wob=np.ascontiguousarray(np.asarray(wob, np.float32)),
        wfcT=np.ascontiguousarray(np.asarray(wfc, np.float32).T),
        lnw=np.ascontiguousarray(np.asarray(lnw, np.float32)),
        lnb=np.ascontiguousarray(np.asarray(lnb, np.float32)),
        wprojT=np.ascontiguousarray(np.asarray(wproj, np.float32).T),
    )
    import jax
    import jax.numpy as jnp
    with jax.default_device(jax.devices("cpu")[0]):
        invj = 1.0 / 10000.0 ** (jnp.arange(0, HD, 2)[:HD // 2].astype(
            jnp.float32) / HD)
        angj = jnp.outer(jnp.arange(T, dtype=jnp.float32), invj)
        cosv = np.asarray(jnp.cos(angj)).reshape(n_tt, 128, 32)
        sinv = np.asarray(jnp.sin(angj)).reshape(n_tt, 128, 32)
    for nm, v in (("ctab", cosv), ("stab", sinv)):
        tab = np.zeros((128, n_tt * 512), np.float32)
        for tt in range(n_tt):
            tab[:, tt * 512:(tt + 1) * 512] = np.tile(v[tt], (1, H))
        d[nm] = tab
    eye = np.eye(128, dtype=np.float32)
    maskv = np.tril(np.full((128, 128), -30000.0, np.float32), -1)
    onesv = np.ones((128, 128), np.float32)
    d["cpkb"] = np.concatenate([eye, maskv, onesv], axis=1).astype(
        ml_dtypes.bfloat16)
    cvec = np.zeros((128, 8), np.float32)
    cvec[:, 0] = -5.0
    cvec[:, 1] = 1e-5
    d["cpkf"] = np.concatenate([eye, onesv, cvec], axis=1)
    return d


def kernel(L0_attn_x, L0_attn_wq, L0_attn_wk, L0_attn_wv, L0_attn_wo_w,
           L0_attn_wo_b, L0_ffn_W_fc, L0_ffn_ln_w, L0_ffn_ln_b, L0_ffn_W_proj):
    x = np.ascontiguousarray(np.asarray(L0_attn_x, np.float32))
    n_cores = x.shape[0]
    shared = host_prep(L0_attn_wq, L0_attn_wk, L0_attn_wv, L0_attn_wo_w,
                       L0_attn_wo_b, L0_ffn_W_fc, L0_ffn_ln_w, L0_ffn_ln_b,
                       L0_ffn_W_proj, n_tt=16)
    nc = build(n_tt=16)
    in_maps = [dict(shared, x=np.ascontiguousarray(x[b]),
                    xtt=make_xtt(x[b]))
               for b in range(n_cores)]
    res = run_bass_kernel_spmd(nc, in_maps, core_ids=list(range(n_cores)))
    return np.stack([r["out"] for r in res.results], axis=0)
